# version 14
# baseline (speedup 1.0000x reference)
import numpy as np
import ml_dtypes
import concourse.bass as bass
import concourse.mybir as mybir
from concourse.bass_utils import run_bass_kernel_spmd
from contextlib import ExitStack

F32 = mybir.dt.float32
BF16 = mybir.dt.bfloat16
AF = mybir.ActivationFunctionType
ALU = mybir.AluOpType
AX = mybir.AxisListType
bfloat16 = ml_dtypes.bfloat16

NCORE = 8
D = 2048
S_LEN = 8192
A = 16
NT = 8
KC = 512
NKC = 16
SCL = 1.0 / float(np.sqrt(2048.0))
EPS = 1e-5
NEG = -1e30
BISECT_ITERS = 36


def T_of(c, j):
    return 8 * j + (c + j) % 8


def slots_for_chunk(i):
    return [j for j in range(NT) if 2 * j + 2 > i]


class Prom:
    __slots__ = ("sem", "val")

    def __init__(self, sem):
        self.sem = sem
        self.val = None


class Oper:
    __slots__ = ("fn", "waits", "inc_sem", "inc_amt", "prom", "tag")

    def __init__(self, fn, waits, inc_sem, inc_amt, prom, tag):
        self.fn = fn; self.waits = waits; self.inc_sem = inc_sem
        self.inc_amt = inc_amt; self.prom = prom; self.tag = tag


class Sched:
    QUEUES = ("T", "V", "S", "Y", "G")
    SEMS = ("eT", "eV", "eS", "eG", "cc", "dY0", "dYm", "dK0", "dK1", "dV0", "dV1",
            "dX0", "dX1", "dX2", "dX3", "dSP", "dstg0", "dstg1", "dstg2", "dstg3",
            "dss0", "dss1", "dwr", "dw0", "dw1", "dw2", "dw3", "dbe", "dws0", "dws1")

    def __init__(self):
        self.q = {k: [] for k in self.QUEUES}

    ENGINE_SEM = {"T": "eT", "V": "eV", "S": "eS", "G": "eG"}

    def add(self, queue, fn, waits=(), inc=None, amt=1, tag=""):
        if inc is None and queue in ("V", "S", "G"):
            inc = self.ENGINE_SEM[queue]
            amt = 1
        prom = Prom(inc) if inc is not None else None
        waits = [w for w in waits if w is not None]
        self.q[queue].append(Oper(fn, waits, inc, amt, prom, tag))
        return prom

    def resolve(self):
        for qn in ("V", "S", "G"):
            es = self.ENGINE_SEM[qn]
            prev = None
            for o in self.q[qn]:
                if o.inc_sem == es:
                    if prev is not None:
                        o.waits.append(prev)
                    prev = o.prom
        ctr = dict.fromkeys(self.SEMS, 0)
        for qn in self.QUEUES:
            for o in self.q[qn]:
                if o.inc_sem is not None:
                    ctr[o.inc_sem] += o.inc_amt
                    o.prom.val = ctr[o.inc_sem]

    def check(self):
        heads = {k: 0 for k in self.QUEUES}
        cur = dict.fromkeys(self.SEMS, 0)
        progress = True
        while progress:
            progress = False
            for qn in self.QUEUES:
                while heads[qn] < len(self.q[qn]):
                    o = self.q[qn][heads[qn]]
                    if all(w.val is not None and cur[w.sem] >= w.val for w in o.waits):
                        if o.inc_sem is not None:
                            cur[o.inc_sem] += o.inc_amt
                        heads[qn] += 1
                        progress = True
                    else:
                        break
        stuck = {k: (heads[k], len(self.q[k])) for k in self.QUEUES
                 if heads[k] < len(self.q[k])}
        if stuck:
            lines = []
            for k, (h, n) in stuck.items():
                o = self.q[k][h]
                missing = [(w.sem, w.val, cur[w.sem]) for w in o.waits
                           if w.val is None or cur[w.sem] < w.val]
                lines.append(f"queue {k} stuck at {h}/{n} tag={o.tag} missing={missing}")
            raise RuntimeError("SCHEDULE DEADLOCK:\n" + "\n".join(lines))

    def emit_queue(self, qn, eng, semh):
        water = dict.fromkeys(self.SEMS, 0)
        for o in self.q[qn]:
            for w in o.waits:
                assert w.val is not None, f"unresolved promise in {qn} tag={o.tag}"
                if w.val > water[w.sem]:
                    eng.wait_ge(semh[w.sem], w.val)
                    water[w.sem] = w.val
            instr = o.fn(eng)
            if o.inc_sem is not None:
                assert instr is not None, f"op {o.tag} returned no instruction"
                if o.inc_sem == "cc":
                    instr.then_inc(semh["cc"])
                else:
                    instr.then_inc(semh[o.inc_sem], o.inc_amt)


def build_program(check_only=False):
    nc = bass.Bass("TRN2", target_bir_lowering=False)
    sch = Sched()
    EPI_TAIL = []

    d_x = nc.dram_tensor("x", [NT * 128, D], F32, kind="ExternalInput").ap()
    d_wshin = nc.dram_tensor("wshin", [5 * 256, D], F32, kind="ExternalInput").ap()
    d_wk32 = nc.dram_tensor("wk32", [D, D], F32, kind="ExternalInput").ap()
    d_wr = nc.dram_tensor("wr", [D, 16], F32, kind="ExternalInput").ap()
    d_gbc = nc.dram_tensor("gbc", [128, D], F32, kind="ExternalInput").ap()
    d_bbc = nc.dram_tensor("bbc", [128, D], F32, kind="ExternalInput").ap()
    d_bebc = nc.dram_tensor("bebc", [128, D], F32, kind="ExternalInput").ap()
    d_rbbc = nc.dram_tensor("rbbc", [128, 16], F32, kind="ExternalInput").ap()
    d_masks = nc.dram_tensor("masks", [NT, 2, 128, KC], BF16, kind="ExternalInput").ap()
    d_ident = nc.dram_tensor("identb", [128, 128], BF16, kind="ExternalInput").ap()
    d_i128f = nc.dram_tensor("i128f", [128, 128], F32, kind="ExternalInput").ap()
    d_e2f = nc.dram_tensor("e2f", [128, 128], F32, kind="ExternalInput").ap()
    d_onesf = nc.dram_tensor("onesf", [1, 128], F32, kind="ExternalInput").ap()
    d_out = nc.dram_tensor("out", [NT * 128, D], F32, kind="ExternalOutput").ap()

    d_xh = nc.dram_tensor("xh_d", [NT * 128, D], F32, kind="Internal").ap()
    d_ktsh = nc.dram_tensor("kt_sh", [2 * NT * 128, D // 2], BF16, kind="Internal").ap()
    d_vsh = nc.dram_tensor("v_sh", [2 * NT * 128, D // 2], BF16, kind="Internal").ap()
    d_ktag = nc.dram_tensor("kt_ag", [2 * NCORE * NT * 128, D // 2], BF16, kind="Internal", addr_space="Shared").ap()
    d_vag = nc.dram_tensor("v_ag", [2 * NCORE * NT * 128, D // 2], BF16, kind="Internal", addr_space="Shared").ap()
    d_ssT = nc.dram_tensor("s_sT", [128, 128], F32, kind="Internal").ap()
    d_wsh = nc.dram_tensor("w_sh", [5 * 256, D], BF16, kind="Internal").ap()
    d_wag = nc.dram_tensor("w_ag", [NCORE * 5 * 256, D], BF16, kind="Internal",
                           addr_space="Shared").ap()
    d_sagT = nc.dram_tensor("s_agT", [NCORE * 128, 128], F32, kind="Internal", addr_space="Shared").ap()

    stk = ExitStack()
    with stk:
        def sb(name, shape, dt):
            return stk.enter_context(nc.sbuf_tensor(name, shape, dt))
        xbuf = sb("xbuf", [128, NT * D], F32)
        identb = sb("identb_s", [128, 128], BF16)
        i128f = sb("i128f_s", [128, 128], F32)
        e2f = sb("e2f_s", [128, 128], F32)
        onesf = sb("onesf_s", [1, 128], F32)
        sm = sb("sm", [128, 256], F32)
        stg = sb("stg", [128, 4 * 512], BF16)
        wr_sb = sb("wr_sb", [128, A * 16], BF16)
        rbbc = sb("rbbc_s", [128, 16], F32)
        S_sb = sb("S_sb", [128, NT * 16], F32)
        smi = sb("smi", [128, 4], mybir.dt.int32)

        class SMAlloc:
            def __init__(self):
                self.n = 0

            def col(self, k=1):
                c = self.n
                self.n += k
                assert self.n <= 256, "sm overflow"
                return c
        smc = SMAlloc()

        pSC = stk.enter_context(nc.psum_tensor("pSC", [128, 1024], F32))
        pT = stk.enter_context(nc.psum_tensor("pT", [128, 2048], BF16))
        pAV = stk.enter_context(nc.psum_tensor("pAV", [128, 2048], F32))

        semh = {s: stk.enter_context(nc.semaphore(s)) for s in Sched.SEMS}

        def emit_ln(src_ap, waits, tag, dump_ap, sq_waits=()):
            c_s1 = smc.col(); c_s2 = smc.col(); c_m = smc.col()
            c_bm = smc.col(); c_std = smc.col(); c_r = smc.col()
            sch.add("V", lambda e: e.reduce_sum(sm[:, c_s1:c_s1 + 1], src_ap, axis=AX.X),
                    waits=waits, tag=f"{tag}_rs")
            pr_sq = sch.add("S", lambda e: e.activation(
                dump_ap, src_ap, AF.Square, accum_out=sm[:, c_s2:c_s2 + 1]),
                waits=list(waits) + list(sq_waits), inc="eS", tag=f"{tag}_sq")
            sch.add("V", lambda e: e.tensor_single_scalar(
                sm[:, c_m:c_m + 1], sm[:, c_s1:c_s1 + 1], 1.0 / D, ALU.mult),
                tag=f"{tag}_m")
            sch.add("V", lambda e: e.tensor_scalar(
                sm[:, c_bm:c_bm + 1], sm[:, c_m:c_m + 1], sm[:, c_m:c_m + 1], EPS,
                ALU.mult, op1=ALU.subtract), tag=f"{tag}_mm")
            pr_bm = sch.add("V", lambda e: e.tensor_single_scalar(
                sm[:, c_bm:c_bm + 1], sm[:, c_bm:c_bm + 1], -1.0, ALU.mult),
                inc="eV", tag=f"{tag}_bm")
            pr_std = sch.add("S", lambda e: e.activation(
                sm[:, c_std:c_std + 1], sm[:, c_s2:c_s2 + 1], AF.Sqrt,
                bias=sm[:, c_bm:c_bm + 1], scale=1.0 / D),
                waits=[pr_bm], inc="eS", tag=f"{tag}_sqrt")
            pr_r = sch.add("V", lambda e: e.reciprocal(
                sm[:, c_r:c_r + 1], sm[:, c_std:c_std + 1]),
                waits=[pr_std], inc="eV", tag=f"{tag}_recip")
            return c_m, c_r, pr_r, pr_sq

        pr_xt = []
        for n in range(NT):
            pr_xt.append(sch.add("Y", lambda e, n=n: e.dma_start(
                out=xbuf[:, n * D:(n + 1) * D],
                in_=d_x[n * 128:(n + 1) * 128, :]),
                inc=f"dX{n % 4}", amt=16, tag=f"ld_x{n}"))
        pr_ident = sch.add("Y", lambda e: e.dma_start(out=identb[:, :], in_=d_ident),
                           inc="dY0", amt=16, tag="ld_ident")
        pr_i128 = sch.add("Y", lambda e: e.dma_start(out=i128f[:, :], in_=d_i128f),
                          inc="dY0", amt=16, tag="ld_i128")
        pr_e2 = sch.add("Y", lambda e: e.dma_start(out=e2f[:, :], in_=d_e2f),
                        inc="dY0", amt=16, tag="ld_e2")
        pr_ones = sch.add("Y", lambda e: e.dma_start(out=onesf[:, :], in_=d_onesf),
                          inc="dY0", amt=16, tag="ld_ones")
        pr_rb = sch.add("Y", lambda e: e.dma_start(out=rbbc[:, :], in_=d_rbbc),
                        inc="dY0", amt=16, tag="ld_rb")
        pr_wr = sch.add("G", lambda e: e.dma_start(
            out=wr_sb[:, :], in_=d_wr.rearrange("(a p) e -> p a e", p=128)),
            inc="dwr", amt=16, tag="ld_wr")
        c_eps = smc.col()
        pr_epsc = sch.add("V", lambda e: e.memset(sm[:, c_eps:c_eps + 1], EPS),
                          inc="eV", tag="eps_const")

        WPH = {}
        WKPRE = {}

        def emit_wk_direct(qq, slot, extra):
            last = None
            for part in range(4):
                def ldwk(e, qq=qq, part=part, slot=slot):
                    srcw = d_wk32[part * 512:(part + 1) * 512,
                                  qq * 512:(qq + 1) * 512]
                    return e.dma_start(
                        out=WPH["wpool"][:, slot * 8192 + part * 2048:
                                         slot * 8192 + (part + 1) * 2048],
                        in_=srcw.rearrange("(a p) nn -> p a nn", p=128))
                last = sch.add("G", ldwk, waits=extra if part == 0 else [],
                               inc=f"dw{slot}", amt=16, tag=f"ldwk{qq}_{part}")
            return last

        pr_wcast = sch.add("G", lambda e: e.dma_start(
            out=d_wsh[:, :], in_=d_wshin[:, :]),
            inc="dws0", amt=16, tag="wcast")
        pr_agw = sch.add("G", lambda e: e.collective_compute(
            "AllGather", ALU.bypass, replica_groups=[list(range(NCORE))],
            ins=[d_wsh.opt()], outs=[d_wag.opt()]),
            waits=[pr_wcast], inc="cc", amt=1, tag="AG_w")
        WKPRE[0] = emit_wk_direct(0, 0, [])
        WKPRE[1] = emit_wk_direct(1, 1, [])

        def wag_quarter_load(w_idx, qq, dst_tile, dst_off, sem, extra_waits, tag):
            last = None
            for a1 in range(2):
                def ld(e, w_idx=w_idx, qq=qq, dst_tile=dst_tile, dst_off=dst_off,
                       a1=a1):
                    vv = d_wag.rearrange("(c w a1 p) d -> w a1 p c d", c=NCORE, w=5,
                                         a1=2)[w_idx, a1][:, :, qq * 512:(qq + 1) * 512]
                    ov = dst_tile[:, dst_off:dst_off + 8192].rearrange(
                        "p (c half nn) -> p c half nn", c=8, half=2)[:, :, a1, :]
                    return e.dma_start(out=ov, in_=vv)
                last = sch.add("Y", ld, waits=([pr_agw] + list(extra_waits))
                               if a1 == 0 else [], inc=sem, amt=16,
                               tag=f"{tag}_{a1}")
            return last

        with nc.sbuf_tensor("qT", [128, A * 1024], BF16) as qT:
            with (nc.sbuf_tensor("xnormT", [128, NT * D], BF16) as xnormT,
                  nc.sbuf_tensor("wpool", [128, 2 * A * 512], BF16) as wpool):
                WPH["wpool"] = wpool
                pr_spill = [None] * NT
                pr_xnT_evac = [None] * NT
                pr_xh = [None] * NT
                with (nc.sbuf_tensor("trash", [128, 2 * D], F32) as trash,
                      nc.sbuf_tensor("xnsc", [128, 2 * D], BF16) as xnsc,
                      nc.sbuf_tensor("gbc_s", [128, D], F32) as gbc,
                      nc.sbuf_tensor("bbc_s", [128, D], F32) as bbc):
                    pr_gbc = sch.add("Y", lambda e: e.dma_start(out=gbc[:, :], in_=d_gbc),
                                     inc="dY0", amt=16, tag="ld_gbc")
                    pr_bbc = sch.add("Y", lambda e: e.dma_start(out=bbc[:, :], in_=d_bbc),
                                     inc="dY0", amt=16, tag="ld_bbc")
                    pr_init_all = pr_bbc
                    pr_tb_last = [None, None]
                    pr_beta = [None] * NT
                    pr_tT = [None] * NT
                    for n in range(NT):
                        xt = xbuf[:, n * D:(n + 1) * D]
                        par = n % 2
                        trs = trash[:, par * D:(par + 1) * D]
                        xns = xnsc[:, par * D:(par + 1) * D]
                        cs1 = smc.col(); cs2 = smc.col(4); cm1 = smc.col()
                        cv1 = smc.col(); cst1 = smc.col(); cr1 = smc.col()
                        cv2 = smc.col(); cst2 = smc.col(); cr2 = smc.col()
                        cmm = smc.col(); ca = smc.col(); cb = smc.col()
                        csf = smc.col()
                        sch.add("V", lambda e, xt=xt, cs1=cs1: e.reduce_sum(
                            sm[:, cs1:cs1 + 1], xt, axis=AX.X),
                            waits=[pr_xt[n]], tag=f"ln1_{n}_rs")

                        def sq(e, xt=xt, cs2=cs2):
                            last = None
                            for q in range(4):
                                last = e.activation(
                                    pAV[:, q * 512:(q + 1) * 512],
                                    xt[:, q * 512:(q + 1) * 512], AF.Square,
                                    accum_out=sm[:, cs2 + q:cs2 + q + 1])
                            return last
                        pr_sq = sch.add("S", sq, waits=[pr_xt[n]], inc="eS",
                                        tag=f"ln1_{n}_sq")
                        sch.add("V", lambda e, cm1=cm1, cs1=cs1: e.tensor_single_scalar(
                            sm[:, cm1:cm1 + 1], sm[:, cs1:cs1 + 1], 1.0 / D, ALU.mult),
                            tag=f"ln1_{n}_m")
                        sch.add("V", lambda e, cv1=cv1, cs2=cs2: e.reduce_sum(
                            sm[:, cv1:cv1 + 1], sm[:, cs2:cs2 + 4], axis=AX.X),
                            waits=[pr_sq], tag=f"ln1_{n}_v00")
                        sch.add("V", lambda e, cv1=cv1: e.tensor_single_scalar(
                            sm[:, cv1:cv1 + 1], sm[:, cv1:cv1 + 1], 1.0 / D, ALU.mult),
                            tag=f"ln1_{n}_v0")
                        sch.add("V", lambda e, cmm=cmm, cm1=cm1: e.tensor_tensor(
                            sm[:, cmm:cmm + 1], sm[:, cm1:cm1 + 1], sm[:, cm1:cm1 + 1],
                            ALU.mult), tag=f"ln1_{n}_mm")
                        pr_v1 = sch.add("V", lambda e, cv1=cv1, cmm=cmm: e.tensor_tensor(
                            sm[:, cv1:cv1 + 1], sm[:, cv1:cv1 + 1], sm[:, cmm:cmm + 1],
                            ALU.subtract), inc="eV", tag=f"ln1_{n}_v1")
                        pr_st1 = sch.add("S", lambda e, cst1=cst1, cv1=cv1: e.activation(
                            sm[:, cst1:cst1 + 1], sm[:, cv1:cv1 + 1], AF.Sqrt,
                            bias=sm[:, c_eps:c_eps + 1]),
                            waits=[pr_v1, pr_epsc], inc="eS", tag=f"ln1_{n}_std")
                        sch.add("V", lambda e, cr1=cr1, cst1=cst1: e.reciprocal(
                            sm[:, cr1:cr1 + 1], sm[:, cst1:cst1 + 1]),
                            waits=[pr_st1], tag=f"ln1_{n}_r")
                        sch.add("V", lambda e, cv2=cv2, cr1=cr1: e.tensor_tensor(
                            sm[:, cv2:cv2 + 1], sm[:, cr1:cr1 + 1], sm[:, cr1:cr1 + 1],
                            ALU.mult), tag=f"ln2_{n}_q")
                        pr_v2 = sch.add("V", lambda e, cv2=cv2, cv1=cv1: e.tensor_tensor(
                            sm[:, cv2:cv2 + 1], sm[:, cv2:cv2 + 1], sm[:, cv1:cv1 + 1],
                            ALU.mult), inc="eV", tag=f"ln2_{n}_v")
                        pr_st2 = sch.add("S", lambda e, cst2=cst2, cv2=cv2: e.activation(
                            sm[:, cst2:cst2 + 1], sm[:, cv2:cv2 + 1], AF.Sqrt,
                            bias=sm[:, c_eps:c_eps + 1]),
                            waits=[pr_v2], inc="eS", tag=f"ln2_{n}_std")
                        sch.add("V", lambda e, cr2=cr2, cst2=cst2: e.reciprocal(
                            sm[:, cr2:cr2 + 1], sm[:, cst2:cst2 + 1]),
                            waits=[pr_st2], tag=f"ln2_{n}_r")
                        sch.add("V", lambda e, csf=csf, cr1=cr1, cr2=cr2: e.tensor_tensor(
                            sm[:, csf:csf + 1], sm[:, cr1:cr1 + 1], sm[:, cr2:cr2 + 1],
                            ALU.mult), tag=f"ln2_{n}_sf")
                        sch.add("V", lambda e, ca=ca, cr1=cr1: e.tensor_single_scalar(
                            sm[:, ca:ca + 1], sm[:, cr1:cr1 + 1], 1.0, ALU.add),
                            tag=f"ln1_{n}_a")
                        sch.add("V", lambda e, cb=cb, cm1=cm1, cr1=cr1: e.tensor_tensor(
                            sm[:, cb:cb + 1], sm[:, cm1:cm1 + 1], sm[:, cr1:cr1 + 1],
                            ALU.mult), tag=f"ln1_{n}_b")
                        t1w = [pr_init_all]
                        if n >= 2:
                            t1w.append(pr_beta[n - 2])
                        pr_t1 = sch.add("V", lambda e, xt=xt, trs=trs, cm1=cm1, csf=csf:
                            e.tensor_scalar(trs, xt, sm[:, cm1:cm1 + 1],
                                            sm[:, csf:csf + 1], ALU.subtract,
                                            op1=ALU.mult),
                            waits=t1w, inc="eV", tag=f"t1_{n}")
                        pr_xh[n] = sch.add("V", lambda e, xt=xt, ca=ca, cb=cb:
                            e.tensor_scalar(xt, xt, sm[:, ca:ca + 1],
                                            sm[:, cb:cb + 1], ALU.mult,
                                            op1=ALU.subtract),
                            waits=[pr_sq], inc="eV", tag=f"xh_{n}")
                        pr_spill[n] = sch.add("S", lambda e, n=n, xt=xt: e.dma_start(
                            out=d_xh[n * 128:(n + 1) * 128, :], in_=xt),
                            waits=[pr_xh[n]], inc="dSP", amt=16, tag=f"spill_{n}")
                        sch.add("G", lambda e, trs=trs: e.tensor_tensor(
                            trs, trs, gbc[:, :], ALU.mult),
                            waits=[pr_t1, pr_init_all], tag=f"gmul_{n}")
                        bw = [pr_init_all]
                        if n >= 2:
                            bw.append(pr_tT[n - 2])
                        pr_beta[n] = sch.add("G", lambda e, trs=trs, xns=xns:
                            e.tensor_tensor(
                            xns, trs, bbc[:, :], ALU.add),
                            waits=bw, inc="eG", tag=f"xnorm_{n}")
                        for half in range(2):
                            def tb(e, half=half, xns=xns):
                                last = None
                                for a8 in range(8):
                                    a = half * 8 + a8
                                    last = e.transpose(
                                        pT[:, half * 1024 + a8 * 128:
                                           half * 1024 + (a8 + 1) * 128],
                                        xns[:, a * 128:(a + 1) * 128], identb[:, :])
                                return last
                            wt = [pr_beta[n], pr_init_all]
                            if pr_tb_last[half] is not None:
                                wt.append(pr_tb_last[half])
                            pr_t = sch.add("T", tb, waits=wt, inc="eT",
                                           tag=f"xnT_{n}_{half}")
                            pr_tb_last[half] = sch.add(
                                "V", lambda e, n=n, half=half: e.tensor_copy(
                                    xnormT[:, n * D + half * 1024:
                                           n * D + (half + 1) * 1024],
                                    pT[:, half * 1024:(half + 1) * 1024]),
                                waits=[pr_t], inc="eV", tag=f"xnTev_{n}_{half}")
                        pr_tT[n] = pr_t
                        pr_xnT_evac[n] = pr_tb_last[1]
                    p1_done_V = pr_tb_last[1]
                    p1_done_G = pr_beta[NT - 1]

                with nc.sbuf_tensor("wpoolB", [128, 2 * A * 512], BF16) as wpoolB:
                    def wsl(slot):
                        return wpoolB if slot >= 2 else wpool

                    def wsl_off(slot):
                        return (slot - 2) * 8192 if slot >= 2 else slot * 8192
                    pr_kwrite = []
                    pr_vwrite = []
                    pr_sc_freeS = [None, None]
                    pr_stg_free = [None] * 4
                    stg_i = [0]
                    par_i = [0]
                    first_stg = [True]

                    def produce_piece(tag, lhsT_fn, rhs_fn, evac_dst_fn, dma_fn, waits_mm):
                        par = par_i[0] % 2
                        par_i[0] += 1
                        psl = pSC[:, par * 512:(par + 1) * 512]
                        wt = list(waits_mm)
                        if pr_sc_freeS[par] is not None:
                            wt.append(pr_sc_freeS[par])

                        def mms(e, lhsT_fn=lhsT_fn, rhs_fn=rhs_fn, psl=psl):
                            last = None
                            for a in range(A):
                                last = e.matmul(psl, lhsT_fn(a), rhs_fn(a),
                                                start=(a == 0), stop=(a == A - 1))
                            return last
                        pr_mm = sch.add("T", mms, waits=wt, inc="eT", tag=f"{tag}_mm")
                        if dma_fn is not None:
                            sl = stg_i[0] % 4
                            stg_i[0] += 1
                            dst = stg[:, sl * 512:(sl + 1) * 512]
                            ew = [pr_mm]
                            if pr_stg_free[sl] is not None:
                                ew.append(pr_stg_free[sl])
                            pr_ev = sch.add("S", lambda e, dst=dst, psl=psl: e.activation(
                                dst, psl, AF.Copy), waits=ew, inc="eS", tag=f"{tag}_ev")
                            pr_sc_freeS[par] = pr_ev
                            pr_d = sch.add("S", lambda e, dst=dst, dma_fn=dma_fn:
                                           dma_fn(e, dst),
                                           waits=[pr_ev], inc=f"dstg{sl}", amt=16,
                                           tag=f"{tag}_dma")
                            pr_stg_free[sl] = pr_d
                            return pr_mm, pr_d
                        pr_ev = sch.add("S", lambda e, psl=psl, f=evac_dst_fn: e.activation(
                            f(), psl, AF.Copy), waits=[pr_mm], inc="eS", tag=f"{tag}_ev")
                        pr_sc_freeS[par] = pr_ev
                        return pr_mm, pr_ev

                    pr_slot_freeT = [None, None, None, None]
                    gseq = [(w, qq) for w in ("k", "v", "q") for qq in range(4)]
                    wmap = {"q": 0, "k": 1, "v": 2}
                    pr_agkt = None
                    first_wagB = [True, True]
                    wag_i = [0]
                    for wname, qq in gseq:
                        if wname == "k":
                            slot = qq % 2
                            if qq < 2:
                                pr_w = WKPRE[qq]
                            else:
                                wtld = []
                                if pr_slot_freeT[slot] is not None:
                                    wtld.append(pr_slot_freeT[slot])
                                pr_w = emit_wk_direct(qq, slot, wtld)
                        else:
                            slot = 2 + wag_i[0] % 2
                            wag_i[0] += 1
                            wtld = []
                            if pr_slot_freeT[slot] is not None:
                                wtld.append(pr_slot_freeT[slot])
                            if first_wagB[slot - 2]:
                                wtld += [p1_done_V, p1_done_G]
                                first_wagB[slot - 2] = False
                            pr_w = wag_quarter_load(wmap[wname], qq, wsl(slot),
                                                    wsl_off(slot), f"dw{slot}", wtld,
                                                    f"ldw_{wname}{qq}")
                        last_mm = None
                        if wname == "k":
                            for g in range(2):
                                for ao4 in range(4):
                                    a_out = qq * 4 + ao4
                                    def lhsT_fn(a, slot=slot, ao4=ao4):
                                        base = wsl_off(slot)
                                        return wsl(slot)[:, base + a * 512 + ao4 * 128:
                                                         base + a * 512 + (ao4 + 1) * 128]
                                    def rhs_fn(a, g=g):
                                        v = xnormT[:, :].rearrange("p (n c) -> p n c", n=NT)
                                        return v[:, 4 * g:4 * g + 4, a * 128:(a + 1) * 128]
                                    def dma_fn(e, src, a_out=a_out, g=g):
                                        hh = a_out // 8
                                        dstv = d_ktsh[hh * 1024:(hh + 1) * 1024, :]
                                        dstv = dstv.rearrange("(j p) c -> j p c", p=128)[
                                            4 * g:4 * g + 4, :,
                                            (a_out % 8) * 128:((a_out % 8) + 1) * 128]
                                        return e.dma_start(
                                            out=dstv.rearrange("j p t -> p j t"),
                                            in_=src.rearrange("p (j t) -> p j t", j=4))
                                    pr_mm, pr_d = produce_piece(
                                        f"K{a_out}_{g}", lhsT_fn, rhs_fn, None, dma_fn,
                                        [pr_w, pr_xnT_evac[4 * g + 3]])
                                    pr_kwrite.append(pr_d)
                                    last_mm = pr_mm
                        elif wname == "v":
                            for j in range(NT):
                                def lhsT_fn(a, j=j):
                                    return xnormT[:, j * D + a * 128:
                                                  j * D + (a + 1) * 128]
                                def rhs_fn(a, slot=slot):
                                    base = wsl_off(slot)
                                    return wsl(slot)[:, base + a * 512:
                                                     base + (a + 1) * 512]
                                def dma_fn(e, src, j=j, qq=qq):
                                    hh = qq // 2
                                    return e.dma_start(
                                        out=d_vsh[hh * 1024 + j * 128:
                                                  hh * 1024 + (j + 1) * 128,
                                                  (qq % 2) * 512:((qq % 2) + 1) * 512],
                                        in_=src)
                                pr_mm, pr_d = produce_piece(
                                    f"V{qq}_{j}", lhsT_fn, rhs_fn, None, dma_fn,
                                    [pr_w, pr_xnT_evac[j]])
                                pr_vwrite.append(pr_d)
                                last_mm = pr_mm
                        else:
                            for g in range(2):
                                for ao4 in range(4):
                                    a_out = qq * 4 + ao4
                                    def lhsT_fn(a, slot=slot, ao4=ao4):
                                        base = wsl_off(slot)
                                        return wsl(slot)[:, base + a * 512 + ao4 * 128:
                                                         base + a * 512 + (ao4 + 1) * 128]
                                    def rhs_fn(a, g=g):
                                        v = xnormT[:, :].rearrange("p (n c) -> p n c", n=NT)
                                        return v[:, 4 * g:4 * g + 4, a * 128:(a + 1) * 128]
                                    def evac_dst_fn(a_out=a_out, g=g):
                                        return qT[:, a_out * 1024 + g * 512:
                                                  a_out * 1024 + (g + 1) * 512]
                                    pr_mm, pr_ev = produce_piece(
                                        f"Q{a_out}_{g}", lhsT_fn, rhs_fn, evac_dst_fn,
                                        None, [pr_w, pr_xnT_evac[4 * g + 3]])
                                    last_mm = pr_mm
                        pr_slot_freeT[slot] = last_mm
                        if wname == "k" and qq in (1, 3):
                            hh = 0 if qq == 1 else 1
                            pr_agkt = sch.add("G", lambda e, hh=hh: e.collective_compute(
                                "AllGather", ALU.bypass,
                                replica_groups=[list(range(NCORE))],
                                ins=[d_ktsh[hh * 1024:(hh + 1) * 1024, :]],
                                outs=[d_ktag[hh * 8192:(hh + 1) * 8192, :]]),
                                waits=list(pr_kwrite[hh * 16:(hh + 1) * 16]),
                                inc="cc", amt=1, tag=f"AG_kt{hh}")
                        if wname == "v" and qq in (1, 3):
                            hh = 0 if qq == 1 else 1
                            pr_agv = sch.add("G", lambda e, hh=hh: e.collective_compute(
                                "AllGather", ALU.bypass,
                                replica_groups=[list(range(NCORE))],
                                ins=[d_vsh[hh * 1024:(hh + 1) * 1024, :]],
                                outs=[d_vag[hh * 8192:(hh + 1) * 8192, :]]),
                                waits=list(pr_vwrite[hh * 16:(hh + 1) * 16]),
                                inc="cc", amt=1, tag=f"AG_v{hh}")
                    pr_p3_T = [pr_slot_freeT[0], pr_slot_freeT[1]]
                    pr_p3_S = [pr_sc_freeS[0], pr_sc_freeS[1]]
                    pr_p3_stg = list(pr_stg_free)

            with (nc.sbuf_tensor("kbuf", [128, 2 * 4 * D], BF16) as kbuf,
                  nc.sbuf_tensor("vbuf", [128, 2 * 4 * D], BF16) as vbuf,
                  nc.sbuf_tensor("maskb", [128, NT * 2 * KC], BF16) as maskb,
                  nc.sbuf_tensor("pbuf", [128, 2 * KC], BF16) as pbuf,
                  nc.sbuf_tensor("ptile", [128, 2 * KC], BF16) as ptile):
                jobs = []
                for i in range(NKC):
                    for j in slots_for_chunk(i):
                        jobs.append((j, i))
                njobs = len(jobs)
                slot_jobidx = {j: [] for j in range(NT)}
                for idx, (j, i) in enumerate(jobs):
                    slot_jobidx[j].append(idx)

                c_mA = smc.col(NT); c_mB = smc.col(NT)
                c_srun = smc.col(NT)
                c_cm = smc.col(2); c_d = smc.col(2); c_nb = smc.col(2)
                c_cs = smc.col(2); c_fac = smc.col(2)
                c_rec = smc.col(NT); c_gate = smc.col(NT)

                region_guard = pr_p3_T + pr_p3_S + pr_p3_stg
                pr_mask = sch.add("Y", lambda e: e.dma_start(
                    out=maskb[:, :].rearrange("p (j i t) -> p j i t", j=NT, i=2),
                    in_=d_masks.rearrange("j i p t -> p j i t")),
                    waits=region_guard, inc="dYm", amt=16, tag="ld_masks")
                pr_minit = sch.add("V", lambda e: e.memset(
                    sm[:, c_mA:c_mA + 2 * NT], NEG), inc="eV", tag="m_init")
                pr_sinit = sch.add("V", lambda e: e.memset(
                    sm[:, c_srun:c_srun + NT], 0.0), inc="eV", tag="s_init")

                pr_kv = [None] * NKC
                pr_ktload = [None] * NKC
                pr_scores = [None] * njobs
                pr_transp = [None] * njobs
                pr_attnv = [None] * njobs
                pr_nb = [None] * njobs
                pr_evac = [None] * njobs
                pr_aostt = [None] * njobs
                pr_av_dv = [None] * njobs
                pr_ao_dv = [None] * njobs
                pr_srunstt = [None] * njobs
                pr_factor = [None] * njobs
                pr_probs = [None] * njobs
                PR_CHUNK_DONE = {}

                def emit_kv_load(i):
                    par = i % 2
                    wt = [pr_agkt]
                    if i == 0:
                        wt.append(pr_mask)
                    if i >= 2:
                        wt.append(PR_CHUNK_DONE[i - 2])
                    lastk = None
                    lastv = None
                    rows = []
                    for s in range(4):
                        tau = 4 * i + s
                        jt = tau // 8
                        r = (tau % 8 - jt) % 8
                        rows.append(r * (NT * 128) + jt * 128)
                    for s in range(4):
                        for hh in range(2):
                            def ldk(e, row0=rows[s], par=par, s=s, hh=hh):
                                return e.dma_start(
                                    out=kbuf[:, par * 4 * D + s * D + hh * 1024:
                                             par * 4 * D + s * D + (hh + 1) * 1024],
                                    in_=d_ktag[hh * 8192 + row0:hh * 8192 + row0 + 128, :])
                            lastk = sch.add("Y", ldk,
                                            waits=wt if (s == 0 and hh == 0) else [],
                                            inc=f"dK{i % 2}", amt=16,
                                            tag=f"ldkt_{i}_{s}_{hh}")
                    for s in range(4):
                        for hh in range(2):
                            def ldv(e, row0=rows[s], par=par, s=s, hh=hh):
                                return e.dma_start(
                                    out=vbuf[:, par * 4 * D + s * D + hh * 1024:
                                             par * 4 * D + s * D + (hh + 1) * 1024],
                                    in_=d_vag[hh * 8192 + row0:hh * 8192 + row0 + 128, :])
                            lastv = sch.add("Y", ldv,
                                            waits=[pr_agv] if (s == 0 and hh == 0
                                                               and i < 2) else [],
                                            inc=f"dV{i % 2}", amt=16,
                                            tag=f"ldv_{i}_{s}_{hh}")
                    pr_ktload[i] = lastk
                    return lastv

                def emit_scores(n):
                    j, i = jobs[n]
                    par = n % 2
                    psl = pSC[:, par * 512:(par + 1) * 512]
                    masked = (i >= 2 * j)
                    wt = [pr_ktload[i]]
                    if n >= 2:
                        wt += [pr_nb[n - 2], pr_probs[n - 2]]
                    else:
                        wt += [pr_minit, pr_sinit]
                    if n == 0:
                        wt.append(pr_spill[NT - 1])

                    def mms(e, j=j, i=i, psl=psl, masked=masked):
                        kv = kbuf[:, (i % 2) * 4 * D:((i % 2) + 1) * 4 * D].rearrange(
                            "p (s c) -> p s c", s=4)
                        last = None
                        for a in range(A):
                            last = e.matmul(
                                psl, qT[:, a * 1024 + j * 128:a * 1024 + (j + 1) * 128],
                                kv[:, :, a * 128:(a + 1) * 128],
                                start=(a == 0), stop=(a == A - 1 and not masked))
                        if masked:
                            ii = i - 2 * j
                            last = e.matmul(
                                psl, identb[:, :],
                                maskb[:, (j * 2 + ii) * 512:(j * 2 + ii + 1) * 512],
                                start=False, stop=True)
                        return last
                    pr_scores[n] = sch.add("T", mms, waits=wt, inc="eT", tag=f"sc_{n}")

                def emit_transp(n):
                    par = n % 2
                    wt = [pr_probs[n]]
                    if n >= 2:
                        wt.append(pr_evac[n - 2])

                    def tr(e, par=par):
                        last = None
                        for s in range(4):
                            last = e.transpose(
                                pT[:, par * 1024 + s * 128:par * 1024 + (s + 1) * 128],
                                pbuf[:, par * 512 + s * 128:par * 512 + (s + 1) * 128],
                                identb[:, :])
                        return last
                    pr_transp[n] = sch.add("T", tr, waits=wt, inc="eT", tag=f"tr_{n}")

                def emit_attnv(n):
                    j, i = jobs[n]
                    par = n % 2
                    proms = []
                    for dv in range(4):
                        wt = [pr_evac[n], pr_kv[i]] if dv == 0 else []
                        if n >= 1:
                            wt.append(pr_ao_dv[n - 1][dv])

                        def av(e, n=n, i=i, dv=dv, par=par):
                            last = None
                            for s in range(4):
                                last = e.matmul(
                                    pAV[:, dv * 512:(dv + 1) * 512],
                                    ptile[:, par * 512 + s * 128:
                                          par * 512 + (s + 1) * 128],
                                    vbuf[:, (i % 2) * 4 * D + s * D + dv * 512:
                                         (i % 2) * 4 * D + s * D + (dv + 1) * 512],
                                    start=(s == 0), stop=(s == 3))
                            return last
                        proms.append(sch.add("T", av, waits=wt, inc="eT",
                                             tag=f"av_{n}_{dv}"))
                    pr_av_dv[n] = proms
                    pr_attnv[n] = proms[3]

                def emit_vopsA(n):
                    j, i = jobs[n]
                    par = n % 2
                    k_in_slot = slot_jobidx[j].index(n)
                    mo = c_mA + j if (k_in_slot % 2 == 0) else c_mB + j
                    mn = c_mB + j if (k_in_slot % 2 == 0) else c_mA + j
                    psl = pSC[:, par * 512:(par + 1) * 512]
                    sch.add("V", lambda e, psl=psl, par=par: e.reduce_max(
                        sm[:, c_cm + par:c_cm + par + 1], psl, axis=AX.X),
                        waits=[pr_scores[n]], tag=f"cmax_{n}")
                    sch.add("V", lambda e, mo=mo, mn=mn, par=par: e.tensor_tensor(
                        sm[:, mn:mn + 1], sm[:, mo:mo + 1],
                        sm[:, c_cm + par:c_cm + par + 1], ALU.max), tag=f"mnew_{n}")
                    sch.add("V", lambda e, mo=mo, mn=mn, par=par: e.tensor_tensor(
                        sm[:, c_d + par:c_d + par + 1], sm[:, mo:mo + 1],
                        sm[:, mn:mn + 1], ALU.subtract), tag=f"d_{n}")
                    pr_nb[n] = sch.add("V", lambda e, mn=mn, par=par: e.tensor_single_scalar(
                        sm[:, c_nb + par:c_nb + par + 1], sm[:, mn:mn + 1], -SCL,
                        ALU.mult), inc="eV", tag=f"nb_{n}")

                def emit_vopsB(n):
                    j, i = jobs[n]
                    par = n % 2
                    pr_srunstt[n] = sch.add("V", lambda e, j=j, par=par:
                        e.scalar_tensor_tensor(
                            sm[:, c_srun + j:c_srun + j + 1],
                            sm[:, c_srun + j:c_srun + j + 1],
                            sm[:, c_fac + par:c_fac + par + 1],
                            sm[:, c_cs + par:c_cs + par + 1], ALU.mult, ALU.add),
                        waits=[pr_probs[n], pr_factor[n]], inc="eV", tag=f"srun_{n}")
                    pr_evac[n] = sch.add("V", lambda e, par=par: e.tensor_copy(
                        ptile[:, par * 512:(par + 1) * 512],
                        pT[:, par * 1024:par * 1024 + 512]),
                        waits=[pr_transp[n]], inc="eV", tag=f"pev_{n}")

                def emit_vopsC(n):
                    j, i = jobs[n]
                    par = n % 2
                    proms = []
                    for dv in range(4):
                        proms.append(sch.add("V", lambda e, j=j, par=par, dv=dv:
                            e.scalar_tensor_tensor(
                                xbuf[:, j * D + dv * 512:j * D + (dv + 1) * 512],
                                xbuf[:, j * D + dv * 512:j * D + (dv + 1) * 512],
                                sm[:, c_fac + par:c_fac + par + 1],
                                pAV[:, dv * 512:(dv + 1) * 512],
                                ALU.mult, ALU.add),
                            waits=[pr_av_dv[n][dv]], inc="eV", tag=f"aostt_{n}_{dv}"))
                    pr_ao_dv[n] = proms
                    pr_aostt[n] = proms[3]

                def emit_sops(n):
                    par = n % 2
                    wt = [pr_nb[n]]
                    if n >= 2:
                        wt += [pr_aostt[n - 2], pr_srunstt[n - 2]]
                    pr_factor[n] = sch.add("S", lambda e, par=par: e.activation(
                        sm[:, c_fac + par:c_fac + par + 1],
                        sm[:, c_d + par:c_d + par + 1], AF.Exp, scale=SCL),
                        waits=wt, inc="eS", tag=f"fac_{n}")
                    wtp = [pr_scores[n]]
                    if n >= 2:
                        wtp.append(pr_transp[n - 2])
                    psl = pSC[:, par * 512:(par + 1) * 512]
                    pr_probs[n] = sch.add("S", lambda e, psl=psl, par=par: e.activation(
                        pbuf[:, par * 512:(par + 1) * 512], psl, AF.Exp,
                        bias=sm[:, c_nb + par:c_nb + par + 1], scale=SCL,
                        accum_out=sm[:, c_cs + par:c_cs + par + 1]),
                        waits=wtp, inc="eS", tag=f"probs_{n}")

                cur_chunk = -1
                for n in range(njobs):
                    j, i = jobs[n]
                    if i != cur_chunk:
                        cur_chunk = i
                        pr_kv[i] = emit_kv_load(i)
                    if n == 0:
                        emit_scores(0)
                    emit_vopsA(n)
                    emit_sops(n)
                    if n + 1 < njobs:
                        jj, ii = jobs[n + 1]
                        if ii != cur_chunk:
                            pr_kv[ii] = emit_kv_load(ii)
                            cur_chunk = ii
                        emit_scores(n + 1)
                    emit_transp(n)
                    emit_vopsB(n)
                    emit_attnv(n)
                    emit_vopsC(n)
                    PR_CHUNK_DONE[i] = pr_attnv[n]
                pr_attn_done = pr_aostt[njobs - 1]

            with (nc.sbuf_tensor("aon", [128, 2 * D], BF16) as aon,
                  nc.sbuf_tensor("aot", [128, NT * D], BF16) as aot,
                  nc.sbuf_tensor("wpool2", [128, 2 * A * 512], BF16) as wpool2,
                  nc.sbuf_tensor("xhb", [128, 4 * 512], F32) as xhb):
                pr_aotev = [None] * NT
                pr_tb2 = [None, None]
                pr_transp_aot = [None] * NT
                for j in range(NT):
                    last_of_slot = pr_aostt[slot_jobidx[j][-1]]
                    sch.add("V", lambda e, j=j: e.reciprocal(
                        sm[:, c_rec + j:c_rec + j + 1],
                        sm[:, c_srun + j:c_srun + j + 1]),
                        waits=[last_of_slot], tag=f"rec_{j}")
                    wt = [last_of_slot]
                    if j >= 2:
                        wt.append(pr_transp_aot[j - 2])
                    pr_aonj = sch.add("V", lambda e, j=j: e.tensor_single_scalar(
                        aon[:, (j % 2) * D:((j % 2) + 1) * D],
                        xbuf[:, j * D:(j + 1) * D],
                        sm[:, c_rec + j:c_rec + j + 1], ALU.mult),
                        waits=wt, inc="eV", tag=f"aon_{j}")
                    prt = None
                    for half in range(2):
                        wt2 = [pr_aonj]
                        if pr_tb2[half] is not None:
                            wt2.append(pr_tb2[half])

                        def tb2(e, j=j, half=half):
                            last = None
                            for a8 in range(8):
                                a = half * 8 + a8
                                last = e.transpose(
                                    pT[:, half * 1024 + a8 * 128:
                                       half * 1024 + (a8 + 1) * 128],
                                    aon[:, (j % 2) * D + a * 128:
                                        (j % 2) * D + (a + 1) * 128],
                                    identb[:, :])
                            return last
                        prt = sch.add("T", tb2, waits=wt2, inc="eT",
                                      tag=f"aotT_{j}_{half}")
                        pr_tb2[half] = sch.add("V", lambda e, j=j, half=half: e.tensor_copy(
                            aot[:, j * D + half * 1024:j * D + (half + 1) * 1024],
                            pT[:, half * 1024:(half + 1) * 1024]),
                            waits=[prt], inc="eV", tag=f"aotEv_{j}_{half}")
                    pr_transp_aot[j] = prt
                    pr_aotev[j] = pr_tb2[1]

                pr_wo = [None] * 4
                last_mm_of_dq = [None] * 4

                def emit_wo_load(qq, extra_wait):
                    slot = qq % 2
                    pr_wo[qq] = wag_quarter_load(3, qq, wpool2, slot * 8192,
                                                 f"dw{2 + slot}", extra_wait,
                                                 f"ldwo{qq}")

                emit_wo_load(0, [pr_attn_done])
                emit_wo_load(1, [])
                pr_epi5 = {}
                pc5 = 0
                for dq in range(4):
                    for j in range(NT):
                        k = pc5
                        pc5 += 1
                        bank = k % 4
                        xsl = k % 4
                        wtx = []
                        if k >= 4:
                            wtx.append(pr_epi5[k - 4])
                        if k == 0:
                            wtx.append(pr_attn_done)
                        pr_xh_ld = sch.add("Y", lambda e, j=j, dq=dq, xsl=xsl: e.dma_start(
                            out=xhb[:, xsl * 512:(xsl + 1) * 512],
                            in_=d_xh[j * 128:(j + 1) * 128, dq * 512:(dq + 1) * 512]),
                            waits=wtx, inc=f"dX{k % 4}", amt=16, tag=f"xhld_{k}")
                        wtm = [pr_wo[dq], pr_aotev[j]]
                        if k >= 4:
                            wtm.append(pr_epi5[k - 4])

                        def mmo(e, j=j, dq=dq, bank=bank):
                            slot = dq % 2
                            last = None
                            for a in range(A):
                                last = e.matmul(
                                    pAV[:, bank * 512:(bank + 1) * 512],
                                    aot[:, j * D + a * 128:j * D + (a + 1) * 128],
                                    wpool2[:, slot * 8192 + a * 512:
                                           slot * 8192 + (a + 1) * 512],
                                    start=(a == 0), stop=(a == A - 1))
                            return last
                        pr_mm = sch.add("T", mmo, waits=wtm, inc="eT", tag=f"opro_{k}")
                        last_mm_of_dq[dq] = pr_mm
                        pr_epi5[k] = sch.add("V", lambda e, j=j, dq=dq, bank=bank,
                                             xsl=xsl: e.tensor_tensor(
                            xbuf[:, j * D + dq * 512:j * D + (dq + 1) * 512],
                            xhb[:, xsl * 512:(xsl + 1) * 512],
                            pAV[:, bank * 512:(bank + 1) * 512], ALU.add),
                            waits=[pr_mm, pr_xh_ld], inc="eV", tag=f"epi5_{k}")
                    if dq + 2 <= 3:
                        emit_wo_load(dq + 2, [last_mm_of_dq[dq]])
                pr_x1_all = pr_epi5[pc5 - 1]

        with (nc.sbuf_tensor("g2T", [128, A * 1024], BF16) as g2T,
              nc.sbuf_tensor("g2sc", [128, 2 * D], BF16) as g2sc,
              nc.sbuf_tensor("Sbis", [128, 1024], F32) as Sbis,
              nc.sbuf_tensor("bmask", [128, 1024], BF16) as bmask,
              nc.sbuf_tensor("wpool3", [128, 2 * A * 512], BF16) as wpool3,
              nc.sbuf_tensor("bebc_s", [128, D], F32) as bebc,
              nc.sbuf_tensor("ebuf", [128, 8 * 512], F32) as ebuf,
              nc.sbuf_tensor("outb", [128, 4 * 512], F32) as outb,
              nc.sbuf_tensor("sstg", [16, 256], F32) as sstg,
              nc.sbuf_tensor("thbc", [128, 16], F32) as thbc,
              nc.sbuf_tensor("throw", [1, 16], F32) as throw,
              nc.sbuf_tensor("tmp16", [128, 16], F32) as tmp16):
            c_rmax = smc.col(2); c_nrm = smc.col(2); c_ssum = smc.col(2); c_sr = smc.col(2)
            c_lo = smc.col(); c_hi = smc.col(); c_mid = smc.col()
            c_cnt = smc.col(); c_cond = smc.col(); c_d1 = smc.col(); c_d2 = smc.col()
            c_red = smc.col()

            pr_bebc = sch.add("Y", lambda e: e.dma_start(out=bebc[:, :], in_=d_bebc),
                              waits=[pr_x1_all], inc="dbe", amt=16, tag="ld_bebc")

            pr_g2ev = [None] * NT
            pr_Sfin = [None] * NT
            pr_sT_dma = [None] * NT
            pr_tb3 = [None, None]
            pr_scread = [None, None]
            pr_sstg_free = [None, None]
            pr_we = [None] * 4

            def emit_we_load(qq, extra_wait):
                slot = qq % 2
                pr_we[qq] = wag_quarter_load(4, qq, wpool3, slot * 8192,
                                             f"dw{2 + slot}", extra_wait, f"ldwe{qq}")

            emit_we_load(0, [pr_x1_all])
            emit_we_load(1, [])
            pr_out = []
            pr_mmk = {}
            pr_e1k = {}
            pr_ock = {}
            pr_dmak = {}
            last_mm_of_dq7 = [None] * 4
            piece_list = [(dq, j) for dq in range(4) for j in range(NT)]

            def emit_piece_mm(k):
                dq, j = piece_list[k]
                bank = k % 4
                wtm = [pr_we[dq], pr_g2ev[j]]
                if k >= 4:
                    wtm.append(pr_e1k[k - 4])

                def mme(e, j=j, dq=dq, bank=bank):
                    slot = dq % 2
                    last = None
                    for a in range(A):
                        last = e.matmul(
                            pAV[:, bank * 512:(bank + 1) * 512],
                            g2T[:, a * 1024 + j * 128:a * 1024 + (j + 1) * 128],
                            wpool3[:, slot * 8192 + a * 512:
                                   slot * 8192 + (a + 1) * 512],
                            start=(a == 0), stop=(a == A - 1))
                    return last
                pr_mmk[k] = sch.add("T", mme, waits=wtm, inc="eT", tag=f"exp_{k}")
                last_mm_of_dq7[dq] = pr_mmk[k]

            def emit_piece_e1(k):
                dq, j = piece_list[k]
                bank = k % 4
                esl = k % 8
                wte = [pr_mmk[k], pr_bebc]
                if k >= 8:
                    wte.append(pr_ock[k - 8])

                def e1(e, dq=dq, bank=bank, esl=esl):
                    return e.tensor_tensor(
                        ebuf[:, esl * 512:(esl + 1) * 512],
                        pAV[:, bank * 512:(bank + 1) * 512],
                        bebc[:, dq * 512:(dq + 1) * 512], ALU.add)
                pr_e1k[k] = sch.add("V", e1, waits=wte, inc="eV", tag=f"e1_{k}")

            def emit_piece_oc(k):
                dq, j = piece_list[k]
                esl = k % 8
                osl = k % 4
                wt = [pr_e1k[k], gate_done[0]]
                if k >= 4:
                    wt.append(pr_dmak[k - 4])
                pr_ock[k] = sch.add("V", lambda e, j=j, dq=dq, esl=esl, osl=osl:
                    e.scalar_tensor_tensor(
                        outb[:, osl * 512:(osl + 1) * 512],
                        ebuf[:, esl * 512:(esl + 1) * 512],
                        sm[:, c_gate + j:c_gate + j + 1],
                        xbuf[:, j * D + dq * 512:j * D + (dq + 1) * 512],
                        ALU.mult, ALU.add), waits=wt, inc="eV", tag=f"oc_{k}")
                pr_dmak[k] = sch.add("Y", lambda e, j=j, dq=dq, osl=osl: e.dma_start(
                    out=d_out[j * 128:(j + 1) * 128, dq * 512:(dq + 1) * 512],
                    in_=outb[:, osl * 512:(osl + 1) * 512]),
                    waits=[pr_ock[k]], inc=f"dX{osl}", amt=16, tag=f"odma_{k}")
                pr_out.append(pr_dmak[k])

            pr_g2done = [None] * NT
            for j in range(NT):
                x1t = xbuf[:, j * D:(j + 1) * D]
                m3, r3, pr_r3, pr_sq3 = emit_ln(x1t, [pr_x1_all], f"ln3_{j}", outb[:, :])
                wt = [pr_r3]
                if pr_tb3[1] is not None:
                    wt.append(pr_tb3[1])
                pr_g2 = sch.add("V", lambda e, j=j, m3=m3, r3=r3: e.tensor_scalar(
                    g2sc[:, (j % 2) * D:((j % 2) + 1) * D], xbuf[:, j * D:(j + 1) * D],
                    sm[:, m3:m3 + 1], sm[:, r3:r3 + 1], ALU.subtract, op1=ALU.mult),
                    waits=wt, inc="eV", tag=f"g2_{j}")
                for half in range(2):
                    wt2 = [pr_g2]
                    if pr_tb3[half] is not None:
                        wt2.append(pr_tb3[half])

                    def tb3(e, j=j, half=half):
                        last = None
                        for a8 in range(8):
                            a = half * 8 + a8
                            last = e.transpose(
                                pT[:, half * 1024 + a8 * 128:
                                   half * 1024 + (a8 + 1) * 128],
                                g2sc[:, (j % 2) * D + a * 128:
                                     (j % 2) * D + (a + 1) * 128],
                                identb[:, :])
                        return last
                    prt = sch.add("T", tb3, waits=wt2, inc="eT", tag=f"g2T_{j}_{half}")

                    def ev3(e, j=j, half=half):
                        v = g2T[:, :].rearrange("p (a jj t) -> p a jj t", a=A, jj=NT)
                        return e.tensor_copy(
                            v[:, half * 8:(half + 1) * 8, j, :],
                            pT[:, half * 1024:(half + 1) * 1024].rearrange(
                                "p (a8 t) -> p a8 t", a8=8))
                    pr_tb3[half] = sch.add("V", ev3, waits=[prt], inc="eV",
                                           tag=f"g2Tev_{j}_{half}")
                pr_g2ev[j] = pr_tb3[1]
                emit_piece_mm(j)
                emit_piece_e1(j)
                if j >= 4:
                    emit_piece_mm(8 + j - 4)
                if j == 7:
                    emit_we_load(2, [pr_mmk[7]])


            for j in range(NT):
                par = j % 2
                wtr = [pr_g2ev[j], pr_wr]
                if pr_scread[par] is not None:
                    wtr.append(pr_scread[par])

                def mml(e, j=j, par=par):
                    last = None
                    for a in range(A):
                        last = e.matmul(
                            pSC[:, par * 512:par * 512 + 16],
                            g2T[:, a * 1024 + j * 128:a * 1024 + (j + 1) * 128],
                            wr_sb[:, a * 16:(a + 1) * 16],
                            start=(a == 0), stop=(a == A - 1))
                    return last
                pr_lg = sch.add("T", mml, waits=wtr, inc="eT", tag=f"logits_{j}")
                sch.add("V", lambda e, j=j, par=par: e.tensor_tensor(
                    S_sb[:, j * 16:(j + 1) * 16], pSC[:, par * 512:par * 512 + 16],
                    rbbc[:, :], ALU.add), waits=[pr_lg, pr_init_all], tag=f"spre_{j}")
                sch.add("V", lambda e, j=j, par=par: e.reduce_max(
                    sm[:, c_rmax + par:c_rmax + par + 1],
                    S_sb[:, j * 16:(j + 1) * 16], axis=AX.X), tag=f"rmax_{j}")
                pr_nrm = sch.add("V", lambda e, par=par: e.tensor_single_scalar(
                    sm[:, c_nrm + par:c_nrm + par + 1],
                    sm[:, c_rmax + par:c_rmax + par + 1], -1.0, ALU.mult),
                    inc="eV", tag=f"nrm_{j}")
                pr_sexp = sch.add("S", lambda e, j=j, par=par: e.activation(
                    S_sb[:, j * 16:(j + 1) * 16], S_sb[:, j * 16:(j + 1) * 16], AF.Exp,
                    bias=sm[:, c_nrm + par:c_nrm + par + 1],
                    accum_out=sm[:, c_ssum + par:c_ssum + par + 1]),
                    waits=[pr_nrm], inc="eS", tag=f"sexp_{j}")
                sch.add("V", lambda e, par=par: e.reciprocal(
                    sm[:, c_sr + par:c_sr + par + 1],
                    sm[:, c_ssum + par:c_ssum + par + 1]),
                    waits=[pr_sexp], tag=f"srec_{j}")
                pr_Sfin[j] = sch.add("V", lambda e, j=j, par=par: e.tensor_single_scalar(
                    S_sb[:, j * 16:(j + 1) * 16], S_sb[:, j * 16:(j + 1) * 16],
                    sm[:, c_sr + par:c_sr + par + 1], ALU.mult),
                    inc="eV", tag=f"sfin_{j}")
                pr_st = sch.add("T", lambda e, j=j, par=par: e.matmul(
                    pSC[0:16, par * 512 + 128:par * 512 + 256],
                    S_sb[:, j * 16:(j + 1) * 16], i128f[:, :], start=True, stop=True),
                    waits=[pr_Sfin[j], pr_init_all], inc="eT", tag=f"sT_{j}")
                wts = [pr_st]
                if pr_sstg_free[par] is not None:
                    wts.append(pr_sstg_free[par])
                pr_stev = sch.add("V", lambda e, par=par: e.tensor_copy(
                    sstg[0:16, par * 128:(par + 1) * 128],
                    pSC[0:16, par * 512 + 128:par * 512 + 256]),
                    waits=wts, inc="eV", tag=f"sTev_{j}")
                pr_scread[par] = pr_stev
                pr_sT_dma[j] = sch.add("S", lambda e, j=j, par=par: e.dma_start(
                    out=d_ssT[j * 16:(j + 1) * 16, :],
                    in_=sstg[0:16, par * 128:(par + 1) * 128]),
                    waits=[pr_stev], inc=f"dss{par}", amt=16, tag=f"sTdma_{j}")
                pr_sstg_free[par] = pr_sT_dma[j]

            pr_ags = sch.add("G", lambda e: e.collective_compute(
                "AllGather", ALU.bypass, replica_groups=[list(range(NCORE))],
                ins=[d_ssT.opt()], outs=[d_sagT.opt()]),
                waits=[pr_sT_dma[j] for j in range(NT)], inc="cc", amt=1, tag="AG_s")
            pr_sbis = sch.add("Y", lambda e: e.dma_start(
                out=Sbis[:, :].rearrange("p (ri t) -> p ri t", ri=NCORE),
                in_=d_sagT.rearrange("(ri p) t -> p ri t", p=128)),
                waits=[pr_ags], inc="dYm", amt=16, tag="ld_sbis")

            pr_init_lo = sch.add("V", lambda e: e.memset(sm[:, c_lo:c_lo + 1], 0.0),
                                 inc="eV", tag="bis_lo")

            pr_red = [None] * BISECT_ITERS
            pr_cnt = [None] * BISECT_ITERS
            pr_ccV = [None] * BISECT_ITERS
            bis_k = [0]
            gate_done = [None]

            def emit_one_bisect():
                k = bis_k[0]
                if k >= BISECT_ITERS:
                    return
                bis_k[0] += 1
                delta = 0.5 ** (k + 1)
                wt = ([pr_sbis, pr_init_lo] if k == 0 else [pr_ccV[k - 1]])
                sch.add("V", lambda e, delta=delta: e.tensor_single_scalar(
                    sm[:, c_mid:c_mid + 1], sm[:, c_lo:c_lo + 1], delta, ALU.add),
                    waits=wt, tag=f"bth_{k}")
                pr_red[k] = sch.add("V", lambda e: e.tensor_scalar(
                    bmask[:, :], Sbis[:, :], sm[:, c_mid:c_mid + 1], 0.0,
                    ALU.is_ge, op1=ALU.add, accum_out=sm[:, c_red:c_red + 1]),
                    inc="eV", tag=f"bred_{k}")
                wtc = [pr_red[k], pr_init_all]
                if k >= 2:
                    wtc.append(pr_ccV[k - 2])

                def cnt(e, k=k):
                    par = k % 2
                    return e.matmul(pSC[:, par * 512 + 256:par * 512 + 257],
                                    e2f[:, :], sm[:, c_red:c_red + 1],
                                    start=True, stop=True)
                pr_cnt[k] = sch.add("T", cnt, waits=wtc, inc="eT", tag=f"bcnt_{k}")
                par = k % 2
                sch.add("V", lambda e, par=par: e.tensor_single_scalar(
                    sm[:, c_cond:c_cond + 1],
                    pSC[:, par * 512 + 256:par * 512 + 257], 512.0, ALU.is_ge),
                    waits=[pr_cnt[k]], tag=f"bc0_{k}")
                pr_ccV[k] = sch.add("V", lambda e, delta=delta: e.scalar_tensor_tensor(
                    sm[:, c_lo:c_lo + 1], sm[:, c_cond:c_cond + 1], delta,
                    sm[:, c_lo:c_lo + 1], ALU.mult, ALU.add),
                    inc="eV", tag=f"bc1_{k}")

            def emit_gate():
                pr_th1 = sch.add("T", lambda e: e.matmul(
                    pSC[0:1, 0:16], sm[0:16, c_lo:c_lo + 1], i128f[0:16, 0:16],
                    start=True, stop=True),
                    waits=[pr_ccV[BISECT_ITERS - 1], pr_init_all], inc="eT", tag="th_row")
                pr_thev = sch.add("V", lambda e: e.tensor_copy(
                    throw[0:1, :], pSC[0:1, 0:16]), waits=[pr_th1], inc="eV",
                    tag="th_ev")
                pr_th2 = sch.add("T", lambda e: e.matmul(
                    pSC[:, 16:32], onesf[0:1, :], throw[0:1, 0:16],
                    start=True, stop=True), waits=[pr_thev, pr_init_all], inc="eT",
                    tag="th_bc")
                pr_thbc = sch.add("V", lambda e: e.tensor_copy(
                    thbc[:, :], pSC[:, 16:32]), waits=[pr_th2], inc="eV",
                    tag="thbc_ev")
                last = None
                for j in range(NT):
                    sch.add("V", lambda e, j=j: e.tensor_tensor(
                        tmp16[:, :], S_sb[:, j * 16:(j + 1) * 16], thbc[:, :],
                        ALU.is_ge), waits=[pr_thbc], tag=f"gm_{j}")
                    sch.add("V", lambda e, j=j: e.tensor_tensor(
                        tmp16[:, :], S_sb[:, j * 16:(j + 1) * 16], tmp16[:, :],
                        ALU.mult), tag=f"gs_{j}")
                    last = sch.add("V", lambda e, j=j: e.reduce_sum(
                        sm[:, c_gate + j:c_gate + j + 1], tmp16[:, :], axis=AX.X),
                        inc="eV", tag=f"gate_{j}")
                gate_done[0] = last


            while bis_k[0] < BISECT_ITERS:
                emit_one_bisect()
            emit_gate()
            for k in range(8):
                emit_piece_oc(k)
            for k in range(8, 12):
                emit_piece_e1(k)
                emit_piece_oc(k)
            for k in range(12, 32):
                emit_piece_mm(k)
                emit_piece_e1(k)
                emit_piece_oc(k)
                if k == 16:
                    emit_we_load(3, [pr_mmk[15]])

            sch.add("Y", lambda e: e.nop(), waits=[pr_dmak[28], pr_dmak[29],
                                                    pr_dmak[30], pr_dmak[31]],
                    tag="final_wait")

        sch.resolve()
        sch.check()
        if check_only:
            counts = {k: len(sch.q[k]) for k in sch.QUEUES}
            print("schedule ok:", counts)
            return None
        with nc.Block() as block:
            @block.tensor
            def _(t):
                sch.emit_queue("T", t, semh)

            @block.vector
            def _(v):
                sch.emit_queue("V", v, semh)

            @block.scalar
            def _(s):
                sch.emit_queue("S", s, semh)

            @block.sync
            def _(y):
                sch.emit_queue("Y", y, semh)

            @block.gpsimd
            def _(g):
                sch.emit_queue("G", g, semh)
    return nc


def kernel(L0_attn_x, L0_attn_qkv, L0_attn_o_proj, L0_attn_gamma, L0_attn_beta,
           L0_ffn_router_w, L0_ffn_router_b, L0_ffn_expert_w, L0_ffn_expert_b):
    x = np.ascontiguousarray(np.asarray(L0_attn_x, dtype=np.float32).reshape(S_LEN, D))
    qkv = np.asarray(L0_attn_qkv, dtype=np.float32)
    wq = np.ascontiguousarray(qkv[:, 0:D])
    wk = np.ascontiguousarray(qkv[:, D:2 * D])
    wv = np.ascontiguousarray(qkv[:, 2 * D:3 * D])
    wo = np.ascontiguousarray(np.asarray(L0_attn_o_proj, dtype=np.float32))
    we = np.ascontiguousarray(np.asarray(L0_ffn_expert_w, dtype=np.float32))
    wr = np.ascontiguousarray(np.asarray(L0_ffn_router_w, dtype=np.float32))
    gamma = np.asarray(L0_attn_gamma, dtype=np.float32)
    beta = np.asarray(L0_attn_beta, dtype=np.float32)
    rb = np.asarray(L0_ffn_router_b, dtype=np.float32)
    be = np.asarray(L0_ffn_expert_b, dtype=np.float32)

    gbc = np.ascontiguousarray(np.broadcast_to(gamma[None, :], (128, D)))
    bbc = np.ascontiguousarray(np.broadcast_to(beta[None, :], (128, D)))
    bebc = np.ascontiguousarray(np.broadcast_to(be[None, :], (128, D)))
    rbbc = np.ascontiguousarray(np.broadcast_to(rb[None, :], (128, 16)))
    identb = np.eye(128, dtype=np.float32).astype(bfloat16)
    i128f = np.eye(128, dtype=np.float32)
    e2f = (np.arange(128)[:, None] % 16 == np.arange(128)[None, :] % 16).astype(np.float32)
    onesf = np.ones((1, 128), dtype=np.float32)

    row = np.arange(128)
    col = np.arange(KC)
    in_maps = []
    for c in range(NCORE):
        tiles = [T_of(c, j) for j in range(NT)]
        idx = np.concatenate([np.arange(t * 128, (t + 1) * 128) for t in tiles])
        x_own = np.ascontiguousarray(x[idx])
        masks = np.zeros((NT, 2, 128, KC), dtype=np.float32)
        for j in range(NT):
            t = tiles[j]
            q_global = t * 128 + row
            for ii in range(2):
                chunk = 2 * j + ii
                k_global = chunk * KC + col
                masks[j, ii] = np.where(k_global[None, :] <= q_global[:, None], 0.0, NEG)
        in_maps.append({
            "x": x_own, "wk32": wk, "wshin": np.ascontiguousarray(np.concatenate(
                [wq[256 * c:256 * (c + 1)], wk[256 * c:256 * (c + 1)],
                 wv[256 * c:256 * (c + 1)], wo[256 * c:256 * (c + 1)],
                 we[256 * c:256 * (c + 1)]], axis=0)), "wr": wr,
            "gbc": gbc, "bbc": bbc, "bebc": bebc, "rbbc": rbbc,
            "masks": masks.astype(bfloat16),
            "identb": identb, "i128f": i128f, "e2f": e2f, "onesf": onesf,
        })

    nc = build_program()
    res = run_bass_kernel_spmd(nc, in_maps, core_ids=list(range(NCORE)))

    full = np.zeros((S_LEN, D), dtype=np.float32)
    for c in range(NCORE):
        o = res.results[c]["out"]
        for j in range(NT):
            t = T_of(c, j)
            full[t * 128:(t + 1) * 128, :] = o[j * 128:(j + 1) * 128, :]
    return full.reshape(1, S_LEN, D)


if __name__ == "__main__":
    import sys
    if "--check" in sys.argv:
        build_program(check_only=True)
    else:
        nc = build_program()
        n_inst = sum(1 for f in nc.m.functions for b in f.blocks
                     for i in b.instructions)
        print("built ok, instructions:", n_inst)


# version 15
# speedup vs baseline: 1.0413x; 1.0413x over previous
import numpy as np
import ml_dtypes
import concourse.bass as bass
import concourse.mybir as mybir
from concourse.bass_utils import run_bass_kernel_spmd
from contextlib import ExitStack

F32 = mybir.dt.float32
BF16 = mybir.dt.bfloat16
AF = mybir.ActivationFunctionType
ALU = mybir.AluOpType
AX = mybir.AxisListType
bfloat16 = ml_dtypes.bfloat16

NCORE = 8
D = 2048
S_LEN = 8192
A = 16
NT = 8
KC = 512
NKC = 16
SCL = 1.0 / float(np.sqrt(2048.0))
EPS = 1e-5
NEG = -1e30
BISECT_ITERS = 36


def T_of(c, j):
    return 8 * j + (c + j) % 8


def slots_for_chunk(i):
    return [j for j in range(NT) if 2 * j + 2 > i]


class Prom:
    __slots__ = ("sem", "val")

    def __init__(self, sem):
        self.sem = sem
        self.val = None


class Oper:
    __slots__ = ("fn", "waits", "inc_sem", "inc_amt", "prom", "tag")

    def __init__(self, fn, waits, inc_sem, inc_amt, prom, tag):
        self.fn = fn; self.waits = waits; self.inc_sem = inc_sem
        self.inc_amt = inc_amt; self.prom = prom; self.tag = tag


class Sched:
    QUEUES = ("T", "V", "S", "Y", "G")
    SEMS = ("eT", "eV", "eS", "eG", "cc", "dY0", "dYm", "dK0", "dK1", "dV0", "dV1",
            "dX0", "dX1", "dX2", "dX3", "dSP", "dstg0", "dstg1", "dstg2", "dstg3",
            "dss0", "dss1", "dwr", "dw0", "dw1", "dw2", "dw3", "dbe", "dws0", "dws1",
            "dxv")

    def __init__(self):
        self.q = {k: [] for k in self.QUEUES}

    ENGINE_SEM = {"T": "eT", "V": "eV", "S": "eS", "G": "eG"}

    def add(self, queue, fn, waits=(), inc=None, amt=1, tag=""):
        if inc is None and queue in ("V", "S", "G"):
            inc = self.ENGINE_SEM[queue]
            amt = 1
        prom = Prom(inc) if inc is not None else None
        waits = [w for w in waits if w is not None]
        self.q[queue].append(Oper(fn, waits, inc, amt, prom, tag))
        return prom

    def resolve(self):
        for qn in ("V", "S", "G"):
            es = self.ENGINE_SEM[qn]
            prev = None
            for o in self.q[qn]:
                if o.inc_sem == es:
                    if prev is not None:
                        o.waits.append(prev)
                    prev = o.prom
        ctr = dict.fromkeys(self.SEMS, 0)
        for qn in self.QUEUES:
            for o in self.q[qn]:
                if o.inc_sem is not None:
                    ctr[o.inc_sem] += o.inc_amt
                    o.prom.val = ctr[o.inc_sem]

    def check(self):
        heads = {k: 0 for k in self.QUEUES}
        cur = dict.fromkeys(self.SEMS, 0)
        progress = True
        while progress:
            progress = False
            for qn in self.QUEUES:
                while heads[qn] < len(self.q[qn]):
                    o = self.q[qn][heads[qn]]
                    if all(w.val is not None and cur[w.sem] >= w.val for w in o.waits):
                        if o.inc_sem is not None:
                            cur[o.inc_sem] += o.inc_amt
                        heads[qn] += 1
                        progress = True
                    else:
                        break
        stuck = {k: (heads[k], len(self.q[k])) for k in self.QUEUES
                 if heads[k] < len(self.q[k])}
        if stuck:
            lines = []
            for k, (h, n) in stuck.items():
                o = self.q[k][h]
                missing = [(w.sem, w.val, cur[w.sem]) for w in o.waits
                           if w.val is None or cur[w.sem] < w.val]
                lines.append(f"queue {k} stuck at {h}/{n} tag={o.tag} missing={missing}")
            raise RuntimeError("SCHEDULE DEADLOCK:\n" + "\n".join(lines))

    def emit_queue(self, qn, eng, semh):
        water = dict.fromkeys(self.SEMS, 0)
        for o in self.q[qn]:
            for w in o.waits:
                assert w.val is not None, f"unresolved promise in {qn} tag={o.tag}"
                if w.val > water[w.sem]:
                    eng.wait_ge(semh[w.sem], w.val)
                    water[w.sem] = w.val
            instr = o.fn(eng)
            if o.inc_sem is not None:
                assert instr is not None, f"op {o.tag} returned no instruction"
                if o.inc_sem == "cc":
                    instr.then_inc(semh["cc"])
                else:
                    instr.then_inc(semh[o.inc_sem], o.inc_amt)


def build_program(check_only=False):
    nc = bass.Bass("TRN2", target_bir_lowering=False)
    sch = Sched()
    EPI_TAIL = []

    d_x = nc.dram_tensor("x", [NT * 128, D], F32, kind="ExternalInput").ap()
    d_wshin = nc.dram_tensor("wshin", [5 * 256, D], F32, kind="ExternalInput").ap()
    d_wk32 = nc.dram_tensor("wk32", [D, D], F32, kind="ExternalInput").ap()
    d_wr = nc.dram_tensor("wr", [D, 16], F32, kind="ExternalInput").ap()
    d_gbc = nc.dram_tensor("gbc", [128, D], F32, kind="ExternalInput").ap()
    d_bbc = nc.dram_tensor("bbc", [128, D], F32, kind="ExternalInput").ap()
    d_bebc = nc.dram_tensor("bebc", [128, D], F32, kind="ExternalInput").ap()
    d_rbbc = nc.dram_tensor("rbbc", [128, 16], F32, kind="ExternalInput").ap()
    d_masks = nc.dram_tensor("masks", [NT, 2, 128, KC], BF16, kind="ExternalInput").ap()
    d_ident = nc.dram_tensor("identb", [128, 128], BF16, kind="ExternalInput").ap()
    d_i128f = nc.dram_tensor("i128f", [128, 128], F32, kind="ExternalInput").ap()
    d_e2f = nc.dram_tensor("e2f", [128, 128], F32, kind="ExternalInput").ap()
    d_onesf = nc.dram_tensor("onesf", [1, 128], F32, kind="ExternalInput").ap()
    d_out = nc.dram_tensor("out", [NT * 128, D], F32, kind="ExternalOutput").ap()

    d_xh = nc.dram_tensor("xh_d", [NT * 128, D], F32, kind="Internal").ap()
    d_ktsh = nc.dram_tensor("kt_sh", [2 * NT * 128, D // 2], BF16, kind="Internal").ap()
    d_vsh = nc.dram_tensor("v_sh", [2 * NT * 128, D // 2], BF16, kind="Internal").ap()
    d_ktag = nc.dram_tensor("kt_ag", [2 * NCORE * NT * 128, D // 2], BF16, kind="Internal", addr_space="Shared").ap()
    d_vag = nc.dram_tensor("v_ag", [2 * NCORE * NT * 128, D // 2], BF16, kind="Internal", addr_space="Shared").ap()
    d_ssT = nc.dram_tensor("s_sT", [128, 128], F32, kind="Internal").ap()
    d_wsh = nc.dram_tensor("w_sh", [5 * 256, D], BF16, kind="Internal").ap()
    d_wag = nc.dram_tensor("w_ag", [NCORE * 5 * 256, D], BF16, kind="Internal",
                           addr_space="Shared").ap()
    d_sagT = nc.dram_tensor("s_agT", [NCORE * 128, 128], F32, kind="Internal", addr_space="Shared").ap()

    stk = ExitStack()
    with stk:
        def sb(name, shape, dt):
            return stk.enter_context(nc.sbuf_tensor(name, shape, dt))
        xbuf = sb("xbuf", [128, NT * D], F32)
        identb = sb("identb_s", [128, 128], BF16)
        i128f = sb("i128f_s", [128, 128], F32)
        e2f = sb("e2f_s", [128, 128], F32)
        onesf = sb("onesf_s", [1, 128], F32)
        sm = sb("sm", [128, 256], F32)
        stg = sb("stg", [128, 4 * 512], BF16)
        wr_sb = sb("wr_sb", [128, A * 16], BF16)
        rbbc = sb("rbbc_s", [128, 16], F32)
        S_sb = sb("S_sb", [128, NT * 16], F32)
        smi = sb("smi", [128, 4], mybir.dt.int32)

        class SMAlloc:
            def __init__(self):
                self.n = 0

            def col(self, k=1):
                c = self.n
                self.n += k
                assert self.n <= 256, "sm overflow"
                return c
        smc = SMAlloc()

        pSC = stk.enter_context(nc.psum_tensor("pSC", [128, 1024], F32))
        pT = stk.enter_context(nc.psum_tensor("pT", [128, 2048], BF16))
        pAV = stk.enter_context(nc.psum_tensor("pAV", [128, 2048], F32))

        semh = {s: stk.enter_context(nc.semaphore(s)) for s in Sched.SEMS}

        def emit_ln(src_ap, waits, tag, dump_ap, sq_waits=()):
            c_s1 = smc.col(); c_s2 = smc.col(); c_m = smc.col()
            c_bm = smc.col(); c_std = smc.col(); c_r = smc.col()
            sch.add("V", lambda e: e.reduce_sum(sm[:, c_s1:c_s1 + 1], src_ap, axis=AX.X),
                    waits=waits, tag=f"{tag}_rs")
            pr_sq = sch.add("S", lambda e: e.activation(
                dump_ap, src_ap, AF.Square, accum_out=sm[:, c_s2:c_s2 + 1]),
                waits=list(waits) + list(sq_waits), inc="eS", tag=f"{tag}_sq")
            sch.add("V", lambda e: e.tensor_single_scalar(
                sm[:, c_m:c_m + 1], sm[:, c_s1:c_s1 + 1], 1.0 / D, ALU.mult),
                tag=f"{tag}_m")
            sch.add("V", lambda e: e.tensor_scalar(
                sm[:, c_bm:c_bm + 1], sm[:, c_m:c_m + 1], sm[:, c_m:c_m + 1], EPS,
                ALU.mult, op1=ALU.subtract), tag=f"{tag}_mm")
            pr_bm = sch.add("V", lambda e: e.tensor_single_scalar(
                sm[:, c_bm:c_bm + 1], sm[:, c_bm:c_bm + 1], -1.0, ALU.mult),
                inc="eV", tag=f"{tag}_bm")
            pr_std = sch.add("S", lambda e: e.activation(
                sm[:, c_std:c_std + 1], sm[:, c_s2:c_s2 + 1], AF.Sqrt,
                bias=sm[:, c_bm:c_bm + 1], scale=1.0 / D),
                waits=[pr_bm], inc="eS", tag=f"{tag}_sqrt")
            pr_r = sch.add("V", lambda e: e.reciprocal(
                sm[:, c_r:c_r + 1], sm[:, c_std:c_std + 1]),
                waits=[pr_std], inc="eV", tag=f"{tag}_recip")
            return c_m, c_r, pr_r, pr_sq

        XENG = {0: ("Y", "dX0"), 1: ("S", "dSP"), 2: ("G", "dws1")}
        pr_xt = []
        for n in range(NT):
            qn, sem = XENG[n % 3]
            pr_xt.append(sch.add(qn, lambda e, n=n: e.dma_start(
                out=xbuf[:, n * D:(n + 1) * D],
                in_=d_x[n * 128:(n + 1) * 128, :]),
                inc=sem, amt=16, tag=f"ld_x{n}"))
        pr_ident = sch.add("Y", lambda e: e.dma_start(out=identb[:, :], in_=d_ident),
                           inc="dY0", amt=16, tag="ld_ident")
        pr_i128 = sch.add("Y", lambda e: e.dma_start(out=i128f[:, :], in_=d_i128f),
                          inc="dY0", amt=16, tag="ld_i128")
        pr_e2 = sch.add("Y", lambda e: e.dma_start(out=e2f[:, :], in_=d_e2f),
                        inc="dY0", amt=16, tag="ld_e2")
        pr_ones = sch.add("Y", lambda e: e.dma_start(out=onesf[:, :], in_=d_onesf),
                          inc="dY0", amt=16, tag="ld_ones")
        pr_rb = sch.add("Y", lambda e: e.dma_start(out=rbbc[:, :], in_=d_rbbc),
                        inc="dY0", amt=16, tag="ld_rb")
        pr_wr = sch.add("G", lambda e: e.dma_start(
            out=wr_sb[:, :], in_=d_wr.rearrange("(a p) e -> p a e", p=128)),
            inc="dwr", amt=16, tag="ld_wr")
        c_eps = smc.col()
        pr_epsc = sch.add("V", lambda e: e.memset(sm[:, c_eps:c_eps + 1], EPS),
                          inc="eV", tag="eps_const")

        WPH = {}
        WKPRE = {}

        def emit_wk_direct(qq, slot, extra):
            last = None
            for part in range(4):
                def ldwk(e, qq=qq, part=part, slot=slot):
                    srcw = d_wk32[part * 512:(part + 1) * 512,
                                  qq * 512:(qq + 1) * 512]
                    return e.dma_start(
                        out=WPH["wpool"][:, slot * 8192 + part * 2048:
                                         slot * 8192 + (part + 1) * 2048],
                        in_=srcw.rearrange("(a p) nn -> p a nn", p=128))
                last = sch.add("G", ldwk, waits=extra if part == 0 else [],
                               inc=f"dw{slot}", amt=16, tag=f"ldwk{qq}_{part}")
            return last

        pr_wcast = sch.add("G", lambda e: e.dma_start(
            out=d_wsh[:, :], in_=d_wshin[:, :]),
            inc="dws0", amt=16, tag="wcast")
        pr_agw = sch.add("G", lambda e: e.collective_compute(
            "AllGather", ALU.bypass, replica_groups=[list(range(NCORE))],
            ins=[d_wsh.opt()], outs=[d_wag.opt()]),
            waits=[pr_wcast], inc="cc", amt=1, tag="AG_w")
        WKPRE[0] = emit_wk_direct(0, 0, [])
        WKPRE[1] = emit_wk_direct(1, 1, [])

        def wag_quarter_load(w_idx, qq, dst_tile, dst_off, sem, extra_waits, tag):
            last = None
            for a1 in range(2):
                def ld(e, w_idx=w_idx, qq=qq, dst_tile=dst_tile, dst_off=dst_off,
                       a1=a1):
                    vv = d_wag.rearrange("(c w a1 p) d -> w a1 p c d", c=NCORE, w=5,
                                         a1=2)[w_idx, a1][:, :, qq * 512:(qq + 1) * 512]
                    ov = dst_tile[:, dst_off:dst_off + 8192].rearrange(
                        "p (c half nn) -> p c half nn", c=8, half=2)[:, :, a1, :]
                    return e.dma_start(out=ov, in_=vv)
                last = sch.add("Y", ld, waits=([pr_agw] + list(extra_waits))
                               if a1 == 0 else [], inc=sem, amt=16,
                               tag=f"{tag}_{a1}")
            return last

        with nc.sbuf_tensor("qT", [128, A * 1024], BF16) as qT:
            with (nc.sbuf_tensor("xnormT", [128, NT * D], BF16) as xnormT,
                  nc.sbuf_tensor("wpool", [128, 2 * A * 512], BF16) as wpool):
                WPH["wpool"] = wpool
                pr_spill = [None] * NT
                pr_xnT_evac = [None] * NT
                pr_xh = [None] * NT
                with (nc.sbuf_tensor("trash", [128, 2 * D], F32) as trash,
                      nc.sbuf_tensor("xnsc", [128, 2 * D], BF16) as xnsc,
                      nc.sbuf_tensor("gbc_s", [128, D], F32) as gbc,
                      nc.sbuf_tensor("bbc_s", [128, D], F32) as bbc):
                    pr_gbc = sch.add("Y", lambda e: e.dma_start(out=gbc[:, :], in_=d_gbc),
                                     inc="dY0", amt=16, tag="ld_gbc")
                    pr_bbc = sch.add("Y", lambda e: e.dma_start(out=bbc[:, :], in_=d_bbc),
                                     inc="dY0", amt=16, tag="ld_bbc")
                    pr_init_all = pr_bbc
                    pr_tb_last = [None, None]
                    pr_beta = [None] * NT
                    pr_tT = [None] * NT
                    for n in range(NT):
                        xt = xbuf[:, n * D:(n + 1) * D]
                        par = n % 2
                        trs = trash[:, par * D:(par + 1) * D]
                        xns = xnsc[:, par * D:(par + 1) * D]
                        cs1 = smc.col(); cs2 = smc.col(4); cm1 = smc.col()
                        cv1 = smc.col(); cst1 = smc.col(); cr1 = smc.col()
                        cv2 = smc.col(); cst2 = smc.col(); cr2 = smc.col()
                        cmm = smc.col(); ca = smc.col(); cb = smc.col()
                        csf = smc.col()
                        sch.add("V", lambda e, xt=xt, cs1=cs1: e.reduce_sum(
                            sm[:, cs1:cs1 + 1], xt, axis=AX.X),
                            waits=[pr_xt[n]], tag=f"ln1_{n}_rs")

                        def sq(e, xt=xt, cs2=cs2):
                            last = None
                            for q in range(4):
                                last = e.activation(
                                    pAV[:, q * 512:(q + 1) * 512],
                                    xt[:, q * 512:(q + 1) * 512], AF.Square,
                                    accum_out=sm[:, cs2 + q:cs2 + q + 1])
                            return last
                        pr_sq = sch.add("S", sq, waits=[pr_xt[n]], inc="eS",
                                        tag=f"ln1_{n}_sq")
                        sch.add("V", lambda e, cm1=cm1, cs1=cs1: e.tensor_single_scalar(
                            sm[:, cm1:cm1 + 1], sm[:, cs1:cs1 + 1], 1.0 / D, ALU.mult),
                            tag=f"ln1_{n}_m")
                        sch.add("V", lambda e, cv1=cv1, cs2=cs2: e.reduce_sum(
                            sm[:, cv1:cv1 + 1], sm[:, cs2:cs2 + 4], axis=AX.X),
                            waits=[pr_sq], tag=f"ln1_{n}_v00")
                        sch.add("V", lambda e, cv1=cv1: e.tensor_single_scalar(
                            sm[:, cv1:cv1 + 1], sm[:, cv1:cv1 + 1], 1.0 / D, ALU.mult),
                            tag=f"ln1_{n}_v0")
                        sch.add("V", lambda e, cmm=cmm, cm1=cm1: e.tensor_tensor(
                            sm[:, cmm:cmm + 1], sm[:, cm1:cm1 + 1], sm[:, cm1:cm1 + 1],
                            ALU.mult), tag=f"ln1_{n}_mm")
                        pr_v1 = sch.add("V", lambda e, cv1=cv1, cmm=cmm: e.tensor_tensor(
                            sm[:, cv1:cv1 + 1], sm[:, cv1:cv1 + 1], sm[:, cmm:cmm + 1],
                            ALU.subtract), inc="eV", tag=f"ln1_{n}_v1")
                        pr_st1 = sch.add("S", lambda e, cst1=cst1, cv1=cv1: e.activation(
                            sm[:, cst1:cst1 + 1], sm[:, cv1:cv1 + 1], AF.Sqrt,
                            bias=sm[:, c_eps:c_eps + 1]),
                            waits=[pr_v1, pr_epsc], inc="eS", tag=f"ln1_{n}_std")
                        sch.add("V", lambda e, cr1=cr1, cst1=cst1: e.reciprocal(
                            sm[:, cr1:cr1 + 1], sm[:, cst1:cst1 + 1]),
                            waits=[pr_st1], tag=f"ln1_{n}_r")
                        sch.add("V", lambda e, cv2=cv2, cr1=cr1: e.tensor_tensor(
                            sm[:, cv2:cv2 + 1], sm[:, cr1:cr1 + 1], sm[:, cr1:cr1 + 1],
                            ALU.mult), tag=f"ln2_{n}_q")
                        pr_v2 = sch.add("V", lambda e, cv2=cv2, cv1=cv1: e.tensor_tensor(
                            sm[:, cv2:cv2 + 1], sm[:, cv2:cv2 + 1], sm[:, cv1:cv1 + 1],
                            ALU.mult), inc="eV", tag=f"ln2_{n}_v")
                        pr_st2 = sch.add("S", lambda e, cst2=cst2, cv2=cv2: e.activation(
                            sm[:, cst2:cst2 + 1], sm[:, cv2:cv2 + 1], AF.Sqrt,
                            bias=sm[:, c_eps:c_eps + 1]),
                            waits=[pr_v2], inc="eS", tag=f"ln2_{n}_std")
                        sch.add("V", lambda e, cr2=cr2, cst2=cst2: e.reciprocal(
                            sm[:, cr2:cr2 + 1], sm[:, cst2:cst2 + 1]),
                            waits=[pr_st2], tag=f"ln2_{n}_r")
                        sch.add("V", lambda e, csf=csf, cr1=cr1, cr2=cr2: e.tensor_tensor(
                            sm[:, csf:csf + 1], sm[:, cr1:cr1 + 1], sm[:, cr2:cr2 + 1],
                            ALU.mult), tag=f"ln2_{n}_sf")
                        sch.add("V", lambda e, ca=ca, cr1=cr1: e.tensor_single_scalar(
                            sm[:, ca:ca + 1], sm[:, cr1:cr1 + 1], 1.0, ALU.add),
                            tag=f"ln1_{n}_a")
                        sch.add("V", lambda e, cb=cb, cm1=cm1, cr1=cr1: e.tensor_tensor(
                            sm[:, cb:cb + 1], sm[:, cm1:cm1 + 1], sm[:, cr1:cr1 + 1],
                            ALU.mult), tag=f"ln1_{n}_b")
                        t1w = [pr_init_all]
                        if n >= 2:
                            t1w.append(pr_beta[n - 2])
                        pr_t1 = sch.add("V", lambda e, xt=xt, trs=trs, cm1=cm1, csf=csf:
                            e.tensor_scalar(trs, xt, sm[:, cm1:cm1 + 1],
                                            sm[:, csf:csf + 1], ALU.subtract,
                                            op1=ALU.mult),
                            waits=t1w, inc="eV", tag=f"t1_{n}")
                        pr_xh[n] = sch.add("V", lambda e, xt=xt, ca=ca, cb=cb:
                            e.tensor_scalar(xt, xt, sm[:, ca:ca + 1],
                                            sm[:, cb:cb + 1], ALU.mult,
                                            op1=ALU.subtract),
                            waits=[pr_sq], inc="eV", tag=f"xh_{n}")
                        pr_spill[n] = sch.add("S", lambda e, n=n, xt=xt: e.dma_start(
                            out=d_xh[n * 128:(n + 1) * 128, :], in_=xt),
                            waits=[pr_xh[n]], inc="dSP", amt=16, tag=f"spill_{n}")
                        sch.add("G", lambda e, trs=trs: e.tensor_tensor(
                            trs, trs, gbc[:, :], ALU.mult),
                            waits=[pr_t1, pr_init_all], tag=f"gmul_{n}")
                        bw = [pr_init_all]
                        if n >= 2:
                            bw.append(pr_tT[n - 2])
                        pr_beta[n] = sch.add("G", lambda e, trs=trs, xns=xns:
                            e.tensor_tensor(
                            xns, trs, bbc[:, :], ALU.add),
                            waits=bw, inc="eG", tag=f"xnorm_{n}")
                        for half in range(2):
                            def tb(e, half=half, xns=xns):
                                last = None
                                for a8 in range(8):
                                    a = half * 8 + a8
                                    last = e.transpose(
                                        pT[:, half * 1024 + a8 * 128:
                                           half * 1024 + (a8 + 1) * 128],
                                        xns[:, a * 128:(a + 1) * 128], identb[:, :])
                                return last
                            wt = [pr_beta[n], pr_init_all]
                            if pr_tb_last[half] is not None:
                                wt.append(pr_tb_last[half])
                            pr_t = sch.add("T", tb, waits=wt, inc="eT",
                                           tag=f"xnT_{n}_{half}")
                            pr_tb_last[half] = sch.add(
                                "V", lambda e, n=n, half=half: e.tensor_copy(
                                    xnormT[:, n * D + half * 1024:
                                           n * D + (half + 1) * 1024],
                                    pT[:, half * 1024:(half + 1) * 1024]),
                                waits=[pr_t], inc="eV", tag=f"xnTev_{n}_{half}")
                        pr_tT[n] = pr_t
                        pr_xnT_evac[n] = pr_tb_last[1]
                    p1_done_V = pr_tb_last[1]
                    p1_done_G = pr_beta[NT - 1]

                with nc.sbuf_tensor("wpoolB", [128, 2 * A * 512], BF16) as wpoolB:
                    def wsl(slot):
                        return wpoolB if slot >= 2 else wpool

                    def wsl_off(slot):
                        return (slot - 2) * 8192 if slot >= 2 else slot * 8192
                    pr_kwrite = []
                    pr_vwrite = []
                    pr_sc_freeS = [None, None]
                    pr_stg_free = [None] * 4
                    stg_i = [0]
                    par_i = [0]
                    first_stg = [True]

                    def produce_piece(tag, lhsT_fn, rhs_fn, evac_dst_fn, dma_fn, waits_mm):
                        par = par_i[0] % 2
                        par_i[0] += 1
                        psl = pSC[:, par * 512:(par + 1) * 512]
                        wt = list(waits_mm)
                        if pr_sc_freeS[par] is not None:
                            wt.append(pr_sc_freeS[par])

                        def mms(e, lhsT_fn=lhsT_fn, rhs_fn=rhs_fn, psl=psl):
                            last = None
                            for a in range(A):
                                last = e.matmul(psl, lhsT_fn(a), rhs_fn(a),
                                                start=(a == 0), stop=(a == A - 1))
                            return last
                        pr_mm = sch.add("T", mms, waits=wt, inc="eT", tag=f"{tag}_mm")
                        if dma_fn is not None:
                            sl = stg_i[0] % 4
                            stg_i[0] += 1
                            dst = stg[:, sl * 512:(sl + 1) * 512]
                            ew = [pr_mm]
                            if pr_stg_free[sl] is not None:
                                ew.append(pr_stg_free[sl])
                            pr_ev = sch.add("S", lambda e, dst=dst, psl=psl: e.activation(
                                dst, psl, AF.Copy), waits=ew, inc="eS", tag=f"{tag}_ev")
                            pr_sc_freeS[par] = pr_ev
                            pr_d = sch.add("S", lambda e, dst=dst, dma_fn=dma_fn:
                                           dma_fn(e, dst),
                                           waits=[pr_ev], inc=f"dstg{sl}", amt=16,
                                           tag=f"{tag}_dma")
                            pr_stg_free[sl] = pr_d
                            return pr_mm, pr_d
                        pr_ev = sch.add("S", lambda e, psl=psl, f=evac_dst_fn: e.activation(
                            f(), psl, AF.Copy), waits=[pr_mm], inc="eS", tag=f"{tag}_ev")
                        pr_sc_freeS[par] = pr_ev
                        return pr_mm, pr_ev

                    pr_slot_freeT = [None, None, None, None]
                    gseq = [(w, qq) for w in ("k", "v", "q") for qq in range(4)]
                    wmap = {"q": 0, "k": 1, "v": 2}
                    pr_agkt = None
                    first_wagB = [True, True]
                    wag_i = [0]
                    for wname, qq in gseq:
                        if wname == "k":
                            slot = qq % 2
                            if qq < 2:
                                pr_w = WKPRE[qq]
                            else:
                                wtld = []
                                if pr_slot_freeT[slot] is not None:
                                    wtld.append(pr_slot_freeT[slot])
                                pr_w = emit_wk_direct(qq, slot, wtld)
                        else:
                            slot = 2 + wag_i[0] % 2
                            wag_i[0] += 1
                            wtld = []
                            if pr_slot_freeT[slot] is not None:
                                wtld.append(pr_slot_freeT[slot])
                            if first_wagB[slot - 2]:
                                wtld += [p1_done_V, p1_done_G]
                                first_wagB[slot - 2] = False
                            pr_w = wag_quarter_load(wmap[wname], qq, wsl(slot),
                                                    wsl_off(slot), f"dw{slot}", wtld,
                                                    f"ldw_{wname}{qq}")
                        last_mm = None
                        if wname == "k":
                            for g in range(2):
                                for ao4 in range(4):
                                    a_out = qq * 4 + ao4
                                    def lhsT_fn(a, slot=slot, ao4=ao4):
                                        base = wsl_off(slot)
                                        return wsl(slot)[:, base + a * 512 + ao4 * 128:
                                                         base + a * 512 + (ao4 + 1) * 128]
                                    def rhs_fn(a, g=g):
                                        v = xnormT[:, :].rearrange("p (n c) -> p n c", n=NT)
                                        return v[:, 4 * g:4 * g + 4, a * 128:(a + 1) * 128]
                                    def dma_fn(e, src, a_out=a_out, g=g):
                                        hh = a_out // 8
                                        dstv = d_ktsh[hh * 1024:(hh + 1) * 1024, :]
                                        dstv = dstv.rearrange("(j p) c -> j p c", p=128)[
                                            4 * g:4 * g + 4, :,
                                            (a_out % 8) * 128:((a_out % 8) + 1) * 128]
                                        return e.dma_start(
                                            out=dstv.rearrange("j p t -> p j t"),
                                            in_=src.rearrange("p (j t) -> p j t", j=4))
                                    pr_mm, pr_d = produce_piece(
                                        f"K{a_out}_{g}", lhsT_fn, rhs_fn, None, dma_fn,
                                        [pr_w, pr_xnT_evac[4 * g + 3]])
                                    pr_kwrite.append(pr_d)
                                    last_mm = pr_mm
                        elif wname == "v":
                            for j in range(NT):
                                def lhsT_fn(a, j=j):
                                    return xnormT[:, j * D + a * 128:
                                                  j * D + (a + 1) * 128]
                                def rhs_fn(a, slot=slot):
                                    base = wsl_off(slot)
                                    return wsl(slot)[:, base + a * 512:
                                                     base + (a + 1) * 512]
                                def dma_fn(e, src, j=j, qq=qq):
                                    hh = qq // 2
                                    return e.dma_start(
                                        out=d_vsh[hh * 1024 + j * 128:
                                                  hh * 1024 + (j + 1) * 128,
                                                  (qq % 2) * 512:((qq % 2) + 1) * 512],
                                        in_=src)
                                pr_mm, pr_d = produce_piece(
                                    f"V{qq}_{j}", lhsT_fn, rhs_fn, None, dma_fn,
                                    [pr_w, pr_xnT_evac[j]])
                                pr_vwrite.append(pr_d)
                                last_mm = pr_mm
                        else:
                            for g in range(2):
                                for ao4 in range(4):
                                    a_out = qq * 4 + ao4
                                    def lhsT_fn(a, slot=slot, ao4=ao4):
                                        base = wsl_off(slot)
                                        return wsl(slot)[:, base + a * 512 + ao4 * 128:
                                                         base + a * 512 + (ao4 + 1) * 128]
                                    def rhs_fn(a, g=g):
                                        v = xnormT[:, :].rearrange("p (n c) -> p n c", n=NT)
                                        return v[:, 4 * g:4 * g + 4, a * 128:(a + 1) * 128]
                                    def evac_dst_fn(a_out=a_out, g=g):
                                        return qT[:, a_out * 1024 + g * 512:
                                                  a_out * 1024 + (g + 1) * 512]
                                    pr_mm, pr_ev = produce_piece(
                                        f"Q{a_out}_{g}", lhsT_fn, rhs_fn, evac_dst_fn,
                                        None, [pr_w, pr_xnT_evac[4 * g + 3]])
                                    last_mm = pr_mm
                        pr_slot_freeT[slot] = last_mm
                        if wname == "k" and qq in (1, 3):
                            hh = 0 if qq == 1 else 1
                            pr_agkt = sch.add("G", lambda e, hh=hh: e.collective_compute(
                                "AllGather", ALU.bypass,
                                replica_groups=[list(range(NCORE))],
                                ins=[d_ktsh[hh * 1024:(hh + 1) * 1024, :]],
                                outs=[d_ktag[hh * 8192:(hh + 1) * 8192, :]]),
                                waits=list(pr_kwrite[hh * 16:(hh + 1) * 16]),
                                inc="cc", amt=1, tag=f"AG_kt{hh}")
                        if wname == "v" and qq in (1, 3):
                            hh = 0 if qq == 1 else 1
                            pr_agv = sch.add("G", lambda e, hh=hh: e.collective_compute(
                                "AllGather", ALU.bypass,
                                replica_groups=[list(range(NCORE))],
                                ins=[d_vsh[hh * 1024:(hh + 1) * 1024, :]],
                                outs=[d_vag[hh * 8192:(hh + 1) * 8192, :]]),
                                waits=list(pr_vwrite[hh * 16:(hh + 1) * 16]),
                                inc="cc", amt=1, tag=f"AG_v{hh}")
                    pr_p3_T = [pr_slot_freeT[0], pr_slot_freeT[1]]
                    pr_p3_S = [pr_sc_freeS[0], pr_sc_freeS[1]]
                    pr_p3_stg = list(pr_stg_free)

            with (nc.sbuf_tensor("kbuf", [128, 2 * 4 * D], BF16) as kbuf,
                  nc.sbuf_tensor("vbuf", [128, 2 * 4 * D], BF16) as vbuf,
                  nc.sbuf_tensor("maskb", [128, NT * 2 * KC], BF16) as maskb,
                  nc.sbuf_tensor("pbuf", [128, 2 * KC], BF16) as pbuf,
                  nc.sbuf_tensor("ptile", [128, 2 * KC], BF16) as ptile):
                jobs = []
                for i in range(NKC):
                    for j in slots_for_chunk(i):
                        jobs.append((j, i))
                njobs = len(jobs)
                slot_jobidx = {j: [] for j in range(NT)}
                for idx, (j, i) in enumerate(jobs):
                    slot_jobidx[j].append(idx)

                c_mA = smc.col(NT); c_mB = smc.col(NT)
                c_srun = smc.col(NT)
                c_cm = smc.col(2); c_d = smc.col(2); c_nb = smc.col(2)
                c_cs = smc.col(2); c_fac = smc.col(2)
                c_rec = smc.col(NT); c_gate = smc.col(NT)

                region_guard = pr_p3_T + pr_p3_S + pr_p3_stg
                pr_mask = sch.add("Y", lambda e: e.dma_start(
                    out=maskb[:, :].rearrange("p (j i t) -> p j i t", j=NT, i=2),
                    in_=d_masks.rearrange("j i p t -> p j i t")),
                    waits=region_guard, inc="dYm", amt=16, tag="ld_masks")
                pr_minit = sch.add("V", lambda e: e.memset(
                    sm[:, c_mA:c_mA + 2 * NT], NEG), inc="eV", tag="m_init")
                pr_sinit = sch.add("V", lambda e: e.memset(
                    sm[:, c_srun:c_srun + NT], 0.0), inc="eV", tag="s_init")

                pr_kv = [None] * NKC
                pr_ktload = [None] * NKC
                pr_scores = [None] * njobs
                pr_transp = [None] * njobs
                pr_attnv = [None] * njobs
                pr_nb = [None] * njobs
                pr_evac = [None] * njobs
                pr_aostt = [None] * njobs
                pr_av_dv = [None] * njobs
                pr_ao_dv = [None] * njobs
                pr_srunstt = [None] * njobs
                pr_factor = [None] * njobs
                pr_probs = [None] * njobs
                PR_CHUNK_DONE = {}

                def emit_kv_load(i):
                    par = i % 2
                    wt = [pr_agkt]
                    if i == 0:
                        wt.append(pr_mask)
                    if i >= 2:
                        wt.append(PR_CHUNK_DONE[i - 2])
                    lastk = None
                    lastv = None
                    rows = []
                    for s in range(4):
                        tau = 4 * i + s
                        jt = tau // 8
                        r = (tau % 8 - jt) % 8
                        rows.append(r * (NT * 128) + jt * 128)
                    for s in range(4):
                        for hh in range(2):
                            def ldk(e, row0=rows[s], par=par, s=s, hh=hh):
                                return e.dma_start(
                                    out=kbuf[:, par * 4 * D + s * D + hh * 1024:
                                             par * 4 * D + s * D + (hh + 1) * 1024],
                                    in_=d_ktag[hh * 8192 + row0:hh * 8192 + row0 + 128, :])
                            lastk = sch.add("Y", ldk,
                                            waits=wt if (s == 0 and hh == 0) else [],
                                            inc=f"dK{i % 2}", amt=16,
                                            tag=f"ldkt_{i}_{s}_{hh}")
                    wtv = [pr_agv] if i < 2 else [PR_CHUNK_DONE[i - 2]]
                    for s in range(4):
                        for hh in range(2):
                            def ldv(e, row0=rows[s], par=par, s=s, hh=hh):
                                return e.dma_start(
                                    out=vbuf[:, par * 4 * D + s * D + hh * 1024:
                                             par * 4 * D + s * D + (hh + 1) * 1024],
                                    in_=d_vag[hh * 8192 + row0:hh * 8192 + row0 + 128, :])
                            lastv = sch.add("G", ldv,
                                            waits=wtv if (s == 0 and hh == 0) else [],
                                            inc=f"dV{i % 2}", amt=16,
                                            tag=f"ldv_{i}_{s}_{hh}")
                    pr_ktload[i] = lastk
                    return lastv

                def emit_scores(n):
                    j, i = jobs[n]
                    par = n % 2
                    psl = pSC[:, par * 512:(par + 1) * 512]
                    masked = (i >= 2 * j)
                    wt = [pr_ktload[i]]
                    if n >= 2:
                        wt += [pr_nb[n - 2], pr_probs[n - 2]]
                    else:
                        wt += [pr_minit, pr_sinit]
                    if n == 0:
                        wt.append(pr_spill[NT - 1])

                    def mms(e, j=j, i=i, psl=psl, masked=masked):
                        kv = kbuf[:, (i % 2) * 4 * D:((i % 2) + 1) * 4 * D].rearrange(
                            "p (s c) -> p s c", s=4)
                        last = None
                        for a in range(A):
                            last = e.matmul(
                                psl, qT[:, a * 1024 + j * 128:a * 1024 + (j + 1) * 128],
                                kv[:, :, a * 128:(a + 1) * 128],
                                start=(a == 0), stop=(a == A - 1 and not masked))
                        if masked:
                            ii = i - 2 * j
                            last = e.matmul(
                                psl, identb[:, :],
                                maskb[:, (j * 2 + ii) * 512:(j * 2 + ii + 1) * 512],
                                start=False, stop=True)
                        return last
                    pr_scores[n] = sch.add("T", mms, waits=wt, inc="eT", tag=f"sc_{n}")

                def emit_transp(n):
                    par = n % 2
                    wt = [pr_probs[n]]
                    if n >= 2:
                        wt.append(pr_evac[n - 2])

                    def tr(e, par=par):
                        last = None
                        for s in range(4):
                            last = e.transpose(
                                pT[:, par * 1024 + s * 128:par * 1024 + (s + 1) * 128],
                                pbuf[:, par * 512 + s * 128:par * 512 + (s + 1) * 128],
                                identb[:, :])
                        return last
                    pr_transp[n] = sch.add("T", tr, waits=wt, inc="eT", tag=f"tr_{n}")

                def emit_attnv(n):
                    j, i = jobs[n]
                    par = n % 2
                    proms = []
                    for dv in range(4):
                        wt = [pr_evac[n], pr_kv[i]] if dv == 0 else []
                        if n >= 1:
                            wt.append(pr_ao_dv[n - 1][dv])

                        def av(e, n=n, i=i, dv=dv, par=par):
                            last = None
                            for s in range(4):
                                last = e.matmul(
                                    pAV[:, dv * 512:(dv + 1) * 512],
                                    ptile[:, par * 512 + s * 128:
                                          par * 512 + (s + 1) * 128],
                                    vbuf[:, (i % 2) * 4 * D + s * D + dv * 512:
                                         (i % 2) * 4 * D + s * D + (dv + 1) * 512],
                                    start=(s == 0), stop=(s == 3))
                            return last
                        proms.append(sch.add("T", av, waits=wt, inc="eT",
                                             tag=f"av_{n}_{dv}"))
                    pr_av_dv[n] = proms
                    pr_attnv[n] = proms[3]

                def emit_vopsA(n):
                    j, i = jobs[n]
                    par = n % 2
                    k_in_slot = slot_jobidx[j].index(n)
                    mo = c_mA + j if (k_in_slot % 2 == 0) else c_mB + j
                    mn = c_mB + j if (k_in_slot % 2 == 0) else c_mA + j
                    psl = pSC[:, par * 512:(par + 1) * 512]
                    sch.add("V", lambda e, psl=psl, par=par: e.reduce_max(
                        sm[:, c_cm + par:c_cm + par + 1], psl, axis=AX.X),
                        waits=[pr_scores[n]], tag=f"cmax_{n}")
                    sch.add("V", lambda e, mo=mo, mn=mn, par=par: e.tensor_tensor(
                        sm[:, mn:mn + 1], sm[:, mo:mo + 1],
                        sm[:, c_cm + par:c_cm + par + 1], ALU.max), tag=f"mnew_{n}")
                    sch.add("V", lambda e, mo=mo, mn=mn, par=par: e.tensor_tensor(
                        sm[:, c_d + par:c_d + par + 1], sm[:, mo:mo + 1],
                        sm[:, mn:mn + 1], ALU.subtract), tag=f"d_{n}")
                    pr_nb[n] = sch.add("V", lambda e, mn=mn, par=par: e.tensor_single_scalar(
                        sm[:, c_nb + par:c_nb + par + 1], sm[:, mn:mn + 1], -SCL,
                        ALU.mult), inc="eV", tag=f"nb_{n}")

                def emit_vopsB(n):
                    j, i = jobs[n]
                    par = n % 2
                    pr_srunstt[n] = sch.add("V", lambda e, j=j, par=par:
                        e.scalar_tensor_tensor(
                            sm[:, c_srun + j:c_srun + j + 1],
                            sm[:, c_srun + j:c_srun + j + 1],
                            sm[:, c_fac + par:c_fac + par + 1],
                            sm[:, c_cs + par:c_cs + par + 1], ALU.mult, ALU.add),
                        waits=[pr_probs[n], pr_factor[n]], inc="eV", tag=f"srun_{n}")
                    pr_evac[n] = sch.add("V", lambda e, par=par: e.tensor_copy(
                        ptile[:, par * 512:(par + 1) * 512],
                        pT[:, par * 1024:par * 1024 + 512]),
                        waits=[pr_transp[n]], inc="eV", tag=f"pev_{n}")

                def emit_vopsC(n):
                    j, i = jobs[n]
                    par = n % 2
                    proms = []
                    for dv in range(4):
                        proms.append(sch.add("V", lambda e, j=j, par=par, dv=dv:
                            e.scalar_tensor_tensor(
                                xbuf[:, j * D + dv * 512:j * D + (dv + 1) * 512],
                                xbuf[:, j * D + dv * 512:j * D + (dv + 1) * 512],
                                sm[:, c_fac + par:c_fac + par + 1],
                                pAV[:, dv * 512:(dv + 1) * 512],
                                ALU.mult, ALU.add),
                            waits=[pr_av_dv[n][dv]], inc="eV", tag=f"aostt_{n}_{dv}"))
                    pr_ao_dv[n] = proms
                    pr_aostt[n] = proms[3]

                def emit_sops(n):
                    par = n % 2
                    wt = [pr_nb[n]]
                    if n >= 2:
                        wt += [pr_aostt[n - 2], pr_srunstt[n - 2]]
                    pr_factor[n] = sch.add("S", lambda e, par=par: e.activation(
                        sm[:, c_fac + par:c_fac + par + 1],
                        sm[:, c_d + par:c_d + par + 1], AF.Exp, scale=SCL),
                        waits=wt, inc="eS", tag=f"fac_{n}")
                    wtp = [pr_scores[n]]
                    if n >= 2:
                        wtp.append(pr_transp[n - 2])
                    psl = pSC[:, par * 512:(par + 1) * 512]
                    pr_probs[n] = sch.add("S", lambda e, psl=psl, par=par: e.activation(
                        pbuf[:, par * 512:(par + 1) * 512], psl, AF.Exp,
                        bias=sm[:, c_nb + par:c_nb + par + 1], scale=SCL,
                        accum_out=sm[:, c_cs + par:c_cs + par + 1]),
                        waits=wtp, inc="eS", tag=f"probs_{n}")

                cur_chunk = -1
                for n in range(njobs):
                    j, i = jobs[n]
                    if i != cur_chunk:
                        cur_chunk = i
                        pr_kv[i] = emit_kv_load(i)
                    if n == 0:
                        emit_scores(0)
                    emit_vopsA(n)
                    emit_sops(n)
                    if n + 1 < njobs:
                        jj, ii = jobs[n + 1]
                        if ii != cur_chunk:
                            pr_kv[ii] = emit_kv_load(ii)
                            cur_chunk = ii
                        emit_scores(n + 1)
                    emit_transp(n)
                    emit_vopsB(n)
                    emit_attnv(n)
                    emit_vopsC(n)
                    PR_CHUNK_DONE[i] = pr_attnv[n]
                pr_attn_done = pr_aostt[njobs - 1]

            with (nc.sbuf_tensor("aon", [128, 2 * D], BF16) as aon,
                  nc.sbuf_tensor("aot", [128, NT * D], BF16) as aot,
                  nc.sbuf_tensor("wpool2", [128, 2 * A * 512], BF16) as wpool2,
                  nc.sbuf_tensor("xhb", [128, 4 * 512], F32) as xhb):
                pr_aotev = [None] * NT
                pr_tb2 = [None, None]
                pr_transp_aot = [None] * NT
                for j in range(NT):
                    last_of_slot = pr_aostt[slot_jobidx[j][-1]]
                    sch.add("V", lambda e, j=j: e.reciprocal(
                        sm[:, c_rec + j:c_rec + j + 1],
                        sm[:, c_srun + j:c_srun + j + 1]),
                        waits=[last_of_slot], tag=f"rec_{j}")
                    wt = [last_of_slot]
                    if j >= 2:
                        wt.append(pr_transp_aot[j - 2])
                    pr_aonj = sch.add("V", lambda e, j=j: e.tensor_single_scalar(
                        aon[:, (j % 2) * D:((j % 2) + 1) * D],
                        xbuf[:, j * D:(j + 1) * D],
                        sm[:, c_rec + j:c_rec + j + 1], ALU.mult),
                        waits=wt, inc="eV", tag=f"aon_{j}")
                    prt = None
                    for half in range(2):
                        wt2 = [pr_aonj]
                        if pr_tb2[half] is not None:
                            wt2.append(pr_tb2[half])

                        def tb2(e, j=j, half=half):
                            last = None
                            for a8 in range(8):
                                a = half * 8 + a8
                                last = e.transpose(
                                    pT[:, half * 1024 + a8 * 128:
                                       half * 1024 + (a8 + 1) * 128],
                                    aon[:, (j % 2) * D + a * 128:
                                        (j % 2) * D + (a + 1) * 128],
                                    identb[:, :])
                            return last
                        prt = sch.add("T", tb2, waits=wt2, inc="eT",
                                      tag=f"aotT_{j}_{half}")
                        pr_tb2[half] = sch.add("V", lambda e, j=j, half=half: e.tensor_copy(
                            aot[:, j * D + half * 1024:j * D + (half + 1) * 1024],
                            pT[:, half * 1024:(half + 1) * 1024]),
                            waits=[prt], inc="eV", tag=f"aotEv_{j}_{half}")
                    pr_transp_aot[j] = prt
                    pr_aotev[j] = pr_tb2[1]

                pr_wo = [None] * 4
                last_mm_of_dq = [None] * 4

                def emit_wo_load(qq, extra_wait):
                    slot = qq % 2
                    pr_wo[qq] = wag_quarter_load(3, qq, wpool2, slot * 8192,
                                                 f"dw{2 + slot}", extra_wait,
                                                 f"ldwo{qq}")

                emit_wo_load(0, [pr_attn_done])
                emit_wo_load(1, [])
                pr_epi5 = {}
                pc5 = 0
                for dq in range(4):
                    for j in range(NT):
                        k = pc5
                        pc5 += 1
                        bank = k % 4
                        xsl = k % 4
                        wtx = []
                        if k >= 4:
                            wtx.append(pr_epi5[k - 4])
                        if k == 0:
                            wtx.append(pr_attn_done)
                        pr_xh_ld = sch.add("Y", lambda e, j=j, dq=dq, xsl=xsl: e.dma_start(
                            out=xhb[:, xsl * 512:(xsl + 1) * 512],
                            in_=d_xh[j * 128:(j + 1) * 128, dq * 512:(dq + 1) * 512]),
                            waits=wtx, inc=f"dX{k % 4}", amt=16, tag=f"xhld_{k}")
                        wtm = [pr_wo[dq], pr_aotev[j]]
                        if k >= 4:
                            wtm.append(pr_epi5[k - 4])

                        def mmo(e, j=j, dq=dq, bank=bank):
                            slot = dq % 2
                            last = None
                            for a in range(A):
                                last = e.matmul(
                                    pAV[:, bank * 512:(bank + 1) * 512],
                                    aot[:, j * D + a * 128:j * D + (a + 1) * 128],
                                    wpool2[:, slot * 8192 + a * 512:
                                           slot * 8192 + (a + 1) * 512],
                                    start=(a == 0), stop=(a == A - 1))
                            return last
                        pr_mm = sch.add("T", mmo, waits=wtm, inc="eT", tag=f"opro_{k}")
                        last_mm_of_dq[dq] = pr_mm
                        pr_epi5[k] = sch.add("V", lambda e, j=j, dq=dq, bank=bank,
                                             xsl=xsl: e.tensor_tensor(
                            xbuf[:, j * D + dq * 512:j * D + (dq + 1) * 512],
                            xhb[:, xsl * 512:(xsl + 1) * 512],
                            pAV[:, bank * 512:(bank + 1) * 512], ALU.add),
                            waits=[pr_mm, pr_xh_ld], inc="eV", tag=f"epi5_{k}")
                    if dq + 2 <= 3:
                        emit_wo_load(dq + 2, [last_mm_of_dq[dq]])
                pr_x1_all = pr_epi5[pc5 - 1]

        with (nc.sbuf_tensor("g2T", [128, A * 1024], BF16) as g2T,
              nc.sbuf_tensor("g2sc", [128, 2 * D], BF16) as g2sc,
              nc.sbuf_tensor("Sbis", [128, 1024], F32) as Sbis,
              nc.sbuf_tensor("bmask", [128, 1024], BF16) as bmask,
              nc.sbuf_tensor("wpool3", [128, 2 * A * 512], BF16) as wpool3,
              nc.sbuf_tensor("bebc_s", [128, D], F32) as bebc,
              nc.sbuf_tensor("ebuf", [128, 20 * 512], F32) as ebuf,
              nc.sbuf_tensor("outb", [128, 4 * 512], F32) as outb,
              nc.sbuf_tensor("sstg", [16, 256], F32) as sstg,
              nc.sbuf_tensor("thbc", [128, 16], F32) as thbc,
              nc.sbuf_tensor("throw", [1, 16], F32) as throw,
              nc.sbuf_tensor("tmp16", [128, 16], F32) as tmp16):
            c_rmax = smc.col(2); c_nrm = smc.col(2); c_ssum = smc.col(2); c_sr = smc.col(2)
            c_lo = smc.col(); c_hi = smc.col(); c_mid = smc.col()
            c_cnt = smc.col(); c_cond = smc.col(); c_d1 = smc.col(); c_d2 = smc.col()
            c_red = smc.col()

            pr_bebc = sch.add("Y", lambda e: e.dma_start(out=bebc[:, :], in_=d_bebc),
                              waits=[pr_x1_all], inc="dbe", amt=16, tag="ld_bebc")

            pr_g2ev = [None] * NT
            pr_Sfin = [None] * NT
            pr_sT_dma = [None] * NT
            pr_tb3 = [None, None]
            pr_scread = [None, None]
            pr_sstg_free = [None, None]
            pr_we = [None] * 4

            def emit_we_load(qq, extra_wait):
                slot = qq % 2
                pr_we[qq] = wag_quarter_load(4, qq, wpool3, slot * 8192,
                                             f"dw{2 + slot}", extra_wait, f"ldwe{qq}")

            emit_we_load(0, [pr_x1_all])
            emit_we_load(1, [])
            pr_out = []
            pr_mmk = {}
            pr_e1k = {}
            pr_ock = {}
            pr_dmak = {}
            last_mm_of_dq7 = [None] * 4
            piece_list = [(dq, j) for dq in range(4) for j in range(NT)]

            def emit_piece_mm(k):
                dq, j = piece_list[k]
                bank = k % 4
                wtm = [pr_we[dq], pr_g2ev[j]]
                if k >= 4:
                    wtm.append(pr_e1k[k - 4])

                def mme(e, j=j, dq=dq, bank=bank):
                    slot = dq % 2
                    last = None
                    for a in range(A):
                        last = e.matmul(
                            pAV[:, bank * 512:(bank + 1) * 512],
                            g2T[:, a * 1024 + j * 128:a * 1024 + (j + 1) * 128],
                            wpool3[:, slot * 8192 + a * 512:
                                   slot * 8192 + (a + 1) * 512],
                            start=(a == 0), stop=(a == A - 1))
                    return last
                pr_mmk[k] = sch.add("T", mme, waits=wtm, inc="eT", tag=f"exp_{k}")
                last_mm_of_dq7[dq] = pr_mmk[k]

            def emit_piece_e1(k):
                dq, j = piece_list[k]
                bank = k % 4
                esl = k % 20
                wte = [pr_mmk[k], pr_bebc]
                if k >= 20:
                    wte.append(pr_ock[k - 20])

                def e1(e, dq=dq, bank=bank, esl=esl):
                    return e.tensor_tensor(
                        ebuf[:, esl * 512:(esl + 1) * 512],
                        pAV[:, bank * 512:(bank + 1) * 512],
                        bebc[:, dq * 512:(dq + 1) * 512], ALU.add)
                pr_e1k[k] = sch.add("V", e1, waits=wte, inc="eV", tag=f"e1_{k}")

            def emit_piece_oc(k):
                dq, j = piece_list[k]
                esl = k % 20
                osl = k % 4
                wt = [pr_e1k[k], gate_done[0]]
                if k >= 4:
                    wt.append(pr_dmak[k - 4])
                pr_ock[k] = sch.add("V", lambda e, j=j, dq=dq, esl=esl, osl=osl:
                    e.scalar_tensor_tensor(
                        outb[:, osl * 512:(osl + 1) * 512],
                        ebuf[:, esl * 512:(esl + 1) * 512],
                        sm[:, c_gate + j:c_gate + j + 1],
                        xbuf[:, j * D + dq * 512:j * D + (dq + 1) * 512],
                        ALU.mult, ALU.add), waits=wt, inc="eV", tag=f"oc_{k}")
                osem = f"dX{osl}" if osl % 2 == 0 else f"dss{osl // 2}"
                pr_dmak[k] = sch.add("Y" if osl % 2 == 0 else "S",
                    lambda e, j=j, dq=dq, osl=osl: e.dma_start(
                    out=d_out[j * 128:(j + 1) * 128, dq * 512:(dq + 1) * 512],
                    in_=outb[:, osl * 512:(osl + 1) * 512]),
                    waits=[pr_ock[k]], inc=osem, amt=16, tag=f"odma_{k}")
                pr_out.append(pr_dmak[k])

            pr_g2done = [None] * NT
            for j in range(NT):
                x1t = xbuf[:, j * D:(j + 1) * D]
                m3, r3, pr_r3, pr_sq3 = emit_ln(x1t, [pr_x1_all], f"ln3_{j}", outb[:, :])
                wt = [pr_r3]
                if pr_tb3[1] is not None:
                    wt.append(pr_tb3[1])
                pr_g2 = sch.add("V", lambda e, j=j, m3=m3, r3=r3: e.tensor_scalar(
                    g2sc[:, (j % 2) * D:((j % 2) + 1) * D], xbuf[:, j * D:(j + 1) * D],
                    sm[:, m3:m3 + 1], sm[:, r3:r3 + 1], ALU.subtract, op1=ALU.mult),
                    waits=wt, inc="eV", tag=f"g2_{j}")
                for half in range(2):
                    wt2 = [pr_g2]
                    if pr_tb3[half] is not None:
                        wt2.append(pr_tb3[half])

                    def tb3(e, j=j, half=half):
                        last = None
                        for a8 in range(8):
                            a = half * 8 + a8
                            last = e.transpose(
                                pT[:, half * 1024 + a8 * 128:
                                   half * 1024 + (a8 + 1) * 128],
                                g2sc[:, (j % 2) * D + a * 128:
                                     (j % 2) * D + (a + 1) * 128],
                                identb[:, :])
                        return last
                    prt = sch.add("T", tb3, waits=wt2, inc="eT", tag=f"g2T_{j}_{half}")

                    def ev3(e, j=j, half=half):
                        v = g2T[:, :].rearrange("p (a jj t) -> p a jj t", a=A, jj=NT)
                        return e.tensor_copy(
                            v[:, half * 8:(half + 1) * 8, j, :],
                            pT[:, half * 1024:(half + 1) * 1024].rearrange(
                                "p (a8 t) -> p a8 t", a8=8))
                    pr_tb3[half] = sch.add("V", ev3, waits=[prt], inc="eV",
                                           tag=f"g2Tev_{j}_{half}")
                pr_g2ev[j] = pr_tb3[1]
                emit_piece_mm(j)
                emit_piece_e1(j)
                if j >= 4:
                    emit_piece_mm(8 + j - 4)
                if j == 7:
                    emit_we_load(2, [pr_mmk[7]])


            for j in range(NT):
                par = j % 2
                wtr = [pr_g2ev[j], pr_wr]
                if pr_scread[par] is not None:
                    wtr.append(pr_scread[par])

                def mml(e, j=j, par=par):
                    last = None
                    for a in range(A):
                        last = e.matmul(
                            pSC[:, par * 512:par * 512 + 16],
                            g2T[:, a * 1024 + j * 128:a * 1024 + (j + 1) * 128],
                            wr_sb[:, a * 16:(a + 1) * 16],
                            start=(a == 0), stop=(a == A - 1))
                    return last
                pr_lg = sch.add("T", mml, waits=wtr, inc="eT", tag=f"logits_{j}")
                sch.add("V", lambda e, j=j, par=par: e.tensor_tensor(
                    S_sb[:, j * 16:(j + 1) * 16], pSC[:, par * 512:par * 512 + 16],
                    rbbc[:, :], ALU.add), waits=[pr_lg, pr_init_all], tag=f"spre_{j}")
                sch.add("V", lambda e, j=j, par=par: e.reduce_max(
                    sm[:, c_rmax + par:c_rmax + par + 1],
                    S_sb[:, j * 16:(j + 1) * 16], axis=AX.X), tag=f"rmax_{j}")
                pr_nrm = sch.add("V", lambda e, par=par: e.tensor_single_scalar(
                    sm[:, c_nrm + par:c_nrm + par + 1],
                    sm[:, c_rmax + par:c_rmax + par + 1], -1.0, ALU.mult),
                    inc="eV", tag=f"nrm_{j}")
                pr_sexp = sch.add("S", lambda e, j=j, par=par: e.activation(
                    S_sb[:, j * 16:(j + 1) * 16], S_sb[:, j * 16:(j + 1) * 16], AF.Exp,
                    bias=sm[:, c_nrm + par:c_nrm + par + 1],
                    accum_out=sm[:, c_ssum + par:c_ssum + par + 1]),
                    waits=[pr_nrm], inc="eS", tag=f"sexp_{j}")
                sch.add("V", lambda e, par=par: e.reciprocal(
                    sm[:, c_sr + par:c_sr + par + 1],
                    sm[:, c_ssum + par:c_ssum + par + 1]),
                    waits=[pr_sexp], tag=f"srec_{j}")
                pr_Sfin[j] = sch.add("V", lambda e, j=j, par=par: e.tensor_single_scalar(
                    S_sb[:, j * 16:(j + 1) * 16], S_sb[:, j * 16:(j + 1) * 16],
                    sm[:, c_sr + par:c_sr + par + 1], ALU.mult),
                    inc="eV", tag=f"sfin_{j}")
                pr_st = sch.add("T", lambda e, j=j, par=par: e.matmul(
                    pSC[0:16, par * 512 + 128:par * 512 + 256],
                    S_sb[:, j * 16:(j + 1) * 16], i128f[:, :], start=True, stop=True),
                    waits=[pr_Sfin[j], pr_init_all], inc="eT", tag=f"sT_{j}")
                wts = [pr_st]
                if pr_sstg_free[par] is not None:
                    wts.append(pr_sstg_free[par])
                pr_stev = sch.add("V", lambda e, par=par: e.tensor_copy(
                    sstg[0:16, par * 128:(par + 1) * 128],
                    pSC[0:16, par * 512 + 128:par * 512 + 256]),
                    waits=wts, inc="eV", tag=f"sTev_{j}")
                pr_scread[par] = pr_stev
                pr_sT_dma[j] = sch.add("S", lambda e, j=j, par=par: e.dma_start(
                    out=d_ssT[j * 16:(j + 1) * 16, :],
                    in_=sstg[0:16, par * 128:(par + 1) * 128]),
                    waits=[pr_stev], inc=f"dss{par}", amt=16, tag=f"sTdma_{j}")
                pr_sstg_free[par] = pr_sT_dma[j]

            pr_ags = sch.add("G", lambda e: e.collective_compute(
                "AllGather", ALU.bypass, replica_groups=[list(range(NCORE))],
                ins=[d_ssT.opt()], outs=[d_sagT.opt()]),
                waits=[pr_sT_dma[j] for j in range(NT)], inc="cc", amt=1, tag="AG_s")
            pr_sbis = sch.add("Y", lambda e: e.dma_start(
                out=Sbis[:, :].rearrange("p (ri t) -> p ri t", ri=NCORE),
                in_=d_sagT.rearrange("(ri p) t -> p ri t", p=128)),
                waits=[pr_ags], inc="dYm", amt=16, tag="ld_sbis")

            pr_init_lo = sch.add("V", lambda e: e.memset(sm[:, c_lo:c_lo + 1], 0.0),
                                 inc="eV", tag="bis_lo")

            pr_red = [None] * BISECT_ITERS
            pr_cnt = [None] * BISECT_ITERS
            pr_ccV = [None] * BISECT_ITERS
            bis_k = [0]
            gate_done = [None]

            def emit_one_bisect():
                k = bis_k[0]
                if k >= BISECT_ITERS:
                    return
                bis_k[0] += 1
                delta = 0.5 ** (k + 1)
                wt = ([pr_sbis, pr_init_lo] if k == 0 else [pr_ccV[k - 1]])
                sch.add("V", lambda e, delta=delta: e.tensor_single_scalar(
                    sm[:, c_mid:c_mid + 1], sm[:, c_lo:c_lo + 1], delta, ALU.add),
                    waits=wt, tag=f"bth_{k}")
                pr_red[k] = sch.add("V", lambda e: e.tensor_scalar(
                    bmask[:, :], Sbis[:, :], sm[:, c_mid:c_mid + 1], 0.0,
                    ALU.is_ge, op1=ALU.add, accum_out=sm[:, c_red:c_red + 1]),
                    inc="eV", tag=f"bred_{k}")
                wtc = [pr_red[k], pr_init_all]
                if k >= 2:
                    wtc.append(pr_ccV[k - 2])

                def cnt(e, k=k):
                    par = k % 2
                    return e.matmul(pSC[:, par * 512 + 256:par * 512 + 257],
                                    e2f[:, :], sm[:, c_red:c_red + 1],
                                    start=True, stop=True)
                pr_cnt[k] = sch.add("T", cnt, waits=wtc, inc="eT", tag=f"bcnt_{k}")
                par = k % 2
                sch.add("V", lambda e, par=par: e.tensor_single_scalar(
                    sm[:, c_cond:c_cond + 1],
                    pSC[:, par * 512 + 256:par * 512 + 257], 512.0, ALU.is_ge),
                    waits=[pr_cnt[k]], tag=f"bc0_{k}")
                pr_ccV[k] = sch.add("V", lambda e, delta=delta: e.scalar_tensor_tensor(
                    sm[:, c_lo:c_lo + 1], sm[:, c_cond:c_cond + 1], delta,
                    sm[:, c_lo:c_lo + 1], ALU.mult, ALU.add),
                    inc="eV", tag=f"bc1_{k}")

            def emit_gate():
                pr_th1 = sch.add("T", lambda e: e.matmul(
                    pSC[0:1, 0:16], sm[0:16, c_lo:c_lo + 1], i128f[0:16, 0:16],
                    start=True, stop=True),
                    waits=[pr_ccV[BISECT_ITERS - 1], pr_init_all], inc="eT", tag="th_row")
                pr_thev = sch.add("V", lambda e: e.tensor_copy(
                    throw[0:1, :], pSC[0:1, 0:16]), waits=[pr_th1], inc="eV",
                    tag="th_ev")
                pr_th2 = sch.add("T", lambda e: e.matmul(
                    pSC[:, 16:32], onesf[0:1, :], throw[0:1, 0:16],
                    start=True, stop=True), waits=[pr_thev, pr_init_all], inc="eT",
                    tag="th_bc")
                pr_thbc = sch.add("V", lambda e: e.tensor_copy(
                    thbc[:, :], pSC[:, 16:32]), waits=[pr_th2], inc="eV",
                    tag="thbc_ev")
                last = None
                for j in range(NT):
                    sch.add("V", lambda e, j=j: e.tensor_tensor(
                        tmp16[:, :], S_sb[:, j * 16:(j + 1) * 16], thbc[:, :],
                        ALU.is_ge), waits=[pr_thbc], tag=f"gm_{j}")
                    sch.add("V", lambda e, j=j: e.tensor_tensor(
                        tmp16[:, :], S_sb[:, j * 16:(j + 1) * 16], tmp16[:, :],
                        ALU.mult), tag=f"gs_{j}")
                    last = sch.add("V", lambda e, j=j: e.reduce_sum(
                        sm[:, c_gate + j:c_gate + j + 1], tmp16[:, :], axis=AX.X),
                        inc="eV", tag=f"gate_{j}")
                gate_done[0] = last


            for k in range(8, 12):
                emit_piece_e1(k)
            for k in range(12, 16):
                emit_piece_mm(k)
                emit_piece_e1(k)
            emit_we_load(3, [pr_mmk[15]])
            inter = {4: 16, 10: 17, 16: 18, 22: 19}
            while bis_k[0] < BISECT_ITERS:
                emit_one_bisect()
                k2 = inter.get(bis_k[0])
                if k2 is not None:
                    emit_piece_mm(k2)
                    emit_piece_e1(k2)
            emit_gate()
            for k in range(20):
                emit_piece_oc(k)
            for k in range(20, 32):
                emit_piece_mm(k)
                emit_piece_e1(k)
                emit_piece_oc(k)

            sch.add("Y", lambda e: e.nop(), waits=[pr_dmak[28], pr_dmak[29],
                                                    pr_dmak[30], pr_dmak[31]],
                    tag="final_wait")

        sch.resolve()
        sch.check()
        if check_only:
            counts = {k: len(sch.q[k]) for k in sch.QUEUES}
            print("schedule ok:", counts)
            return None
        with nc.Block() as block:
            @block.tensor
            def _(t):
                sch.emit_queue("T", t, semh)

            @block.vector
            def _(v):
                sch.emit_queue("V", v, semh)

            @block.scalar
            def _(s):
                sch.emit_queue("S", s, semh)

            @block.sync
            def _(y):
                sch.emit_queue("Y", y, semh)

            @block.gpsimd
            def _(g):
                sch.emit_queue("G", g, semh)
    return nc


def kernel(L0_attn_x, L0_attn_qkv, L0_attn_o_proj, L0_attn_gamma, L0_attn_beta,
           L0_ffn_router_w, L0_ffn_router_b, L0_ffn_expert_w, L0_ffn_expert_b):
    x = np.ascontiguousarray(np.asarray(L0_attn_x, dtype=np.float32).reshape(S_LEN, D))
    qkv = np.asarray(L0_attn_qkv, dtype=np.float32)
    wq = np.ascontiguousarray(qkv[:, 0:D])
    wk = np.ascontiguousarray(qkv[:, D:2 * D])
    wv = np.ascontiguousarray(qkv[:, 2 * D:3 * D])
    wo = np.ascontiguousarray(np.asarray(L0_attn_o_proj, dtype=np.float32))
    we = np.ascontiguousarray(np.asarray(L0_ffn_expert_w, dtype=np.float32))
    wr = np.ascontiguousarray(np.asarray(L0_ffn_router_w, dtype=np.float32))
    gamma = np.asarray(L0_attn_gamma, dtype=np.float32)
    beta = np.asarray(L0_attn_beta, dtype=np.float32)
    rb = np.asarray(L0_ffn_router_b, dtype=np.float32)
    be = np.asarray(L0_ffn_expert_b, dtype=np.float32)

    gbc = np.ascontiguousarray(np.broadcast_to(gamma[None, :], (128, D)))
    bbc = np.ascontiguousarray(np.broadcast_to(beta[None, :], (128, D)))
    bebc = np.ascontiguousarray(np.broadcast_to(be[None, :], (128, D)))
    rbbc = np.ascontiguousarray(np.broadcast_to(rb[None, :], (128, 16)))
    identb = np.eye(128, dtype=np.float32).astype(bfloat16)
    i128f = np.eye(128, dtype=np.float32)
    e2f = (np.arange(128)[:, None] % 16 == np.arange(128)[None, :] % 16).astype(np.float32)
    onesf = np.ones((1, 128), dtype=np.float32)

    row = np.arange(128)
    col = np.arange(KC)
    in_maps = []
    for c in range(NCORE):
        tiles = [T_of(c, j) for j in range(NT)]
        idx = np.concatenate([np.arange(t * 128, (t + 1) * 128) for t in tiles])
        x_own = np.ascontiguousarray(x[idx])
        masks = np.zeros((NT, 2, 128, KC), dtype=np.float32)
        for j in range(NT):
            t = tiles[j]
            q_global = t * 128 + row
            for ii in range(2):
                chunk = 2 * j + ii
                k_global = chunk * KC + col
                masks[j, ii] = np.where(k_global[None, :] <= q_global[:, None], 0.0, NEG)
        in_maps.append({
            "x": x_own, "wk32": wk, "wshin": np.ascontiguousarray(np.concatenate(
                [wq[256 * c:256 * (c + 1)], wk[256 * c:256 * (c + 1)],
                 wv[256 * c:256 * (c + 1)], wo[256 * c:256 * (c + 1)],
                 we[256 * c:256 * (c + 1)]], axis=0)), "wr": wr,
            "gbc": gbc, "bbc": bbc, "bebc": bebc, "rbbc": rbbc,
            "masks": masks.astype(bfloat16),
            "identb": identb, "i128f": i128f, "e2f": e2f, "onesf": onesf,
        })

    nc = build_program()
    res = run_bass_kernel_spmd(nc, in_maps, core_ids=list(range(NCORE)))

    full = np.zeros((S_LEN, D), dtype=np.float32)
    for c in range(NCORE):
        o = res.results[c]["out"]
        for j in range(NT):
            t = T_of(c, j)
            full[t * 128:(t + 1) * 128, :] = o[j * 128:(j + 1) * 128, :]
    return full.reshape(1, S_LEN, D)


if __name__ == "__main__":
    import sys
    if "--check" in sys.argv:
        build_program(check_only=True)
    else:
        nc = build_program()
        n_inst = sum(1 for f in nc.m.functions for b in f.blocks
                     for i in b.instructions)
        print("built ok, instructions:", n_inst)


# version 16 (final)
# speedup vs baseline: 1.0440x; 1.0026x over previous
import numpy as np
import ml_dtypes
import concourse.bass as bass
import concourse.mybir as mybir
from concourse.bass_utils import run_bass_kernel_spmd
from contextlib import ExitStack

F32 = mybir.dt.float32
BF16 = mybir.dt.bfloat16
AF = mybir.ActivationFunctionType
ALU = mybir.AluOpType
AX = mybir.AxisListType
bfloat16 = ml_dtypes.bfloat16

NCORE = 8
D = 2048
S_LEN = 8192
A = 16
NT = 8
KC = 512
NKC = 16
SCL = 1.0 / float(np.sqrt(2048.0))
EPS = 1e-5
NEG = -1e30
BISECT_ITERS = 36


def T_of(c, j):
    return 8 * j + (c + j) % 8


def slots_for_chunk(i):
    return [j for j in range(NT) if 2 * j + 2 > i]


class Prom:
    __slots__ = ("sem", "val")

    def __init__(self, sem):
        self.sem = sem
        self.val = None


class Oper:
    __slots__ = ("fn", "waits", "inc_sem", "inc_amt", "prom", "tag")

    def __init__(self, fn, waits, inc_sem, inc_amt, prom, tag):
        self.fn = fn; self.waits = waits; self.inc_sem = inc_sem
        self.inc_amt = inc_amt; self.prom = prom; self.tag = tag


class Sched:
    QUEUES = ("T", "V", "S", "Y", "G")
    SEMS = ("eT", "eV", "eS", "eG", "cc", "dY0", "dYm", "dK0", "dK1", "dV0", "dV1",
            "dX0", "dX1", "dX2", "dX3", "dSP", "dstg0", "dstg1", "dstg2", "dstg3",
            "dss0", "dss1", "dwr", "dw0", "dw1", "dw2", "dw3", "dbe", "dws0", "dws1",
            "dxv")

    def __init__(self):
        self.q = {k: [] for k in self.QUEUES}

    ENGINE_SEM = {"T": "eT", "V": "eV", "S": "eS", "G": "eG"}

    def add(self, queue, fn, waits=(), inc=None, amt=1, tag=""):
        if inc is None and queue in ("V", "S", "G"):
            inc = self.ENGINE_SEM[queue]
            amt = 1
        prom = Prom(inc) if inc is not None else None
        waits = [w for w in waits if w is not None]
        self.q[queue].append(Oper(fn, waits, inc, amt, prom, tag))
        return prom

    def resolve(self):
        for qn in ("V", "S", "G"):
            es = self.ENGINE_SEM[qn]
            prev = None
            for o in self.q[qn]:
                if o.inc_sem == es:
                    if prev is not None:
                        o.waits.append(prev)
                    prev = o.prom
        ctr = dict.fromkeys(self.SEMS, 0)
        for qn in self.QUEUES:
            for o in self.q[qn]:
                if o.inc_sem is not None:
                    ctr[o.inc_sem] += o.inc_amt
                    o.prom.val = ctr[o.inc_sem]

    def check(self):
        heads = {k: 0 for k in self.QUEUES}
        cur = dict.fromkeys(self.SEMS, 0)
        progress = True
        while progress:
            progress = False
            for qn in self.QUEUES:
                while heads[qn] < len(self.q[qn]):
                    o = self.q[qn][heads[qn]]
                    if all(w.val is not None and cur[w.sem] >= w.val for w in o.waits):
                        if o.inc_sem is not None:
                            cur[o.inc_sem] += o.inc_amt
                        heads[qn] += 1
                        progress = True
                    else:
                        break
        stuck = {k: (heads[k], len(self.q[k])) for k in self.QUEUES
                 if heads[k] < len(self.q[k])}
        if stuck:
            lines = []
            for k, (h, n) in stuck.items():
                o = self.q[k][h]
                missing = [(w.sem, w.val, cur[w.sem]) for w in o.waits
                           if w.val is None or cur[w.sem] < w.val]
                lines.append(f"queue {k} stuck at {h}/{n} tag={o.tag} missing={missing}")
            raise RuntimeError("SCHEDULE DEADLOCK:\n" + "\n".join(lines))

    def emit_queue(self, qn, eng, semh):
        water = dict.fromkeys(self.SEMS, 0)
        for o in self.q[qn]:
            for w in o.waits:
                assert w.val is not None, f"unresolved promise in {qn} tag={o.tag}"
                if w.val > water[w.sem]:
                    eng.wait_ge(semh[w.sem], w.val)
                    water[w.sem] = w.val
            instr = o.fn(eng)
            if o.inc_sem is not None:
                assert instr is not None, f"op {o.tag} returned no instruction"
                if o.inc_sem == "cc":
                    instr.then_inc(semh["cc"])
                else:
                    instr.then_inc(semh[o.inc_sem], o.inc_amt)


def build_program(check_only=False):
    nc = bass.Bass("TRN2", target_bir_lowering=False)
    sch = Sched()
    EPI_TAIL = []

    d_x = nc.dram_tensor("x", [NT * 128, D], F32, kind="ExternalInput").ap()
    d_wshin = nc.dram_tensor("wshin", [5 * 256, D], F32, kind="ExternalInput").ap()
    d_wk32 = nc.dram_tensor("wk32", [D, D], F32, kind="ExternalInput").ap()
    d_wr = nc.dram_tensor("wr", [D, 16], F32, kind="ExternalInput").ap()
    d_gbc = nc.dram_tensor("gbc", [128, D], F32, kind="ExternalInput").ap()
    d_bbc = nc.dram_tensor("bbc", [128, D], F32, kind="ExternalInput").ap()
    d_bebc = nc.dram_tensor("bebc", [128, D], F32, kind="ExternalInput").ap()
    d_rbbc = nc.dram_tensor("rbbc", [128, 16], F32, kind="ExternalInput").ap()
    d_masks = nc.dram_tensor("masks", [NT, 2, 128, KC], BF16, kind="ExternalInput").ap()
    d_ident = nc.dram_tensor("identb", [128, 128], BF16, kind="ExternalInput").ap()
    d_i128f = nc.dram_tensor("i128f", [128, 128], F32, kind="ExternalInput").ap()
    d_e2f = nc.dram_tensor("e2f", [128, 128], F32, kind="ExternalInput").ap()
    d_onesf = nc.dram_tensor("onesf", [1, 128], F32, kind="ExternalInput").ap()
    d_out = nc.dram_tensor("out", [NT * 128, D], F32, kind="ExternalOutput").ap()

    d_xh = nc.dram_tensor("xh_d", [NT * 128, D], F32, kind="Internal").ap()
    d_ktsh = nc.dram_tensor("kt_sh", [2 * NT * 128, D // 2], BF16, kind="Internal").ap()
    d_vsh = nc.dram_tensor("v_sh", [2 * NT * 128, D // 2], BF16, kind="Internal").ap()
    d_ktag = nc.dram_tensor("kt_ag", [2 * NCORE * NT * 128, D // 2], BF16, kind="Internal", addr_space="Shared").ap()
    d_vag = nc.dram_tensor("v_ag", [2 * NCORE * NT * 128, D // 2], BF16, kind="Internal", addr_space="Shared").ap()
    d_ssT = nc.dram_tensor("s_sT", [128, 128], F32, kind="Internal").ap()
    d_wsh = nc.dram_tensor("w_sh", [5 * 256, D], BF16, kind="Internal").ap()
    d_wag = nc.dram_tensor("w_ag", [NCORE * 5 * 256, D], BF16, kind="Internal",
                           addr_space="Shared").ap()
    d_sagT = nc.dram_tensor("s_agT", [NCORE * 128, 128], F32, kind="Internal", addr_space="Shared").ap()

    stk = ExitStack()
    with stk:
        def sb(name, shape, dt):
            return stk.enter_context(nc.sbuf_tensor(name, shape, dt))
        xbuf = sb("xbuf", [128, NT * D], F32)
        identb = sb("identb_s", [128, 128], BF16)
        i128f = sb("i128f_s", [128, 128], F32)
        e2f = sb("e2f_s", [128, 128], F32)
        onesf = sb("onesf_s", [1, 128], F32)
        sm = sb("sm", [128, 256], F32)
        stg = sb("stg", [128, 4 * 512], BF16)
        wr_sb = sb("wr_sb", [128, A * 16], BF16)
        rbbc = sb("rbbc_s", [128, 16], F32)
        S_sb = sb("S_sb", [128, NT * 16], F32)
        smi = sb("smi", [128, 4], mybir.dt.int32)

        class SMAlloc:
            def __init__(self):
                self.n = 0

            def col(self, k=1):
                c = self.n
                self.n += k
                assert self.n <= 256, "sm overflow"
                return c
        smc = SMAlloc()

        pSC = stk.enter_context(nc.psum_tensor("pSC", [128, 1024], F32))
        pT = stk.enter_context(nc.psum_tensor("pT", [128, 2048], BF16))
        pAV = stk.enter_context(nc.psum_tensor("pAV", [128, 2048], F32))

        semh = {s: stk.enter_context(nc.semaphore(s)) for s in Sched.SEMS}

        def emit_ln(src_ap, waits, tag, dump_ap, sq_waits=()):
            c_s1 = smc.col(); c_s2 = smc.col(); c_m = smc.col()
            c_bm = smc.col(); c_std = smc.col(); c_r = smc.col()
            sch.add("V", lambda e: e.reduce_sum(sm[:, c_s1:c_s1 + 1], src_ap, axis=AX.X),
                    waits=waits, tag=f"{tag}_rs")
            pr_sq = sch.add("S", lambda e: e.activation(
                dump_ap, src_ap, AF.Square, accum_out=sm[:, c_s2:c_s2 + 1]),
                waits=list(waits) + list(sq_waits), inc="eS", tag=f"{tag}_sq")
            sch.add("V", lambda e: e.tensor_single_scalar(
                sm[:, c_m:c_m + 1], sm[:, c_s1:c_s1 + 1], 1.0 / D, ALU.mult),
                tag=f"{tag}_m")
            sch.add("V", lambda e: e.tensor_scalar(
                sm[:, c_bm:c_bm + 1], sm[:, c_m:c_m + 1], sm[:, c_m:c_m + 1], EPS,
                ALU.mult, op1=ALU.subtract), tag=f"{tag}_mm")
            pr_bm = sch.add("V", lambda e: e.tensor_single_scalar(
                sm[:, c_bm:c_bm + 1], sm[:, c_bm:c_bm + 1], -1.0, ALU.mult),
                inc="eV", tag=f"{tag}_bm")
            pr_std = sch.add("S", lambda e: e.activation(
                sm[:, c_std:c_std + 1], sm[:, c_s2:c_s2 + 1], AF.Sqrt,
                bias=sm[:, c_bm:c_bm + 1], scale=1.0 / D),
                waits=[pr_bm], inc="eS", tag=f"{tag}_sqrt")
            pr_r = sch.add("V", lambda e: e.reciprocal(
                sm[:, c_r:c_r + 1], sm[:, c_std:c_std + 1]),
                waits=[pr_std], inc="eV", tag=f"{tag}_recip")
            return c_m, c_r, pr_r, pr_sq

        XENG = {0: ("Y", "dX0"), 1: ("S", "dSP"), 2: ("G", "dws1")}
        pr_xt = []
        for n in range(NT):
            qn, sem = XENG[n % 3]
            pr_xt.append(sch.add(qn, lambda e, n=n: e.dma_start(
                out=xbuf[:, n * D:(n + 1) * D],
                in_=d_x[n * 128:(n + 1) * 128, :]),
                inc=sem, amt=16, tag=f"ld_x{n}"))
        pr_ident = sch.add("Y", lambda e: e.dma_start(out=identb[:, :], in_=d_ident),
                           inc="dY0", amt=16, tag="ld_ident")
        pr_i128 = sch.add("Y", lambda e: e.dma_start(out=i128f[:, :], in_=d_i128f),
                          inc="dY0", amt=16, tag="ld_i128")
        pr_e2 = sch.add("Y", lambda e: e.dma_start(out=e2f[:, :], in_=d_e2f),
                        inc="dY0", amt=16, tag="ld_e2")
        pr_ones = sch.add("Y", lambda e: e.dma_start(out=onesf[:, :], in_=d_onesf),
                          inc="dY0", amt=16, tag="ld_ones")
        pr_rb = sch.add("Y", lambda e: e.dma_start(out=rbbc[:, :], in_=d_rbbc),
                        inc="dY0", amt=16, tag="ld_rb")
        pr_wr = sch.add("G", lambda e: e.dma_start(
            out=wr_sb[:, :], in_=d_wr.rearrange("(a p) e -> p a e", p=128)),
            inc="dwr", amt=16, tag="ld_wr")
        c_eps = smc.col()
        pr_epsc = sch.add("V", lambda e: e.memset(sm[:, c_eps:c_eps + 1], EPS),
                          inc="eV", tag="eps_const")

        WPH = {}
        WKPRE = {}

        def emit_wk_direct(qq, slot, extra):
            last = None
            for part in range(4):
                def ldwk(e, qq=qq, part=part, slot=slot):
                    srcw = d_wk32[part * 512:(part + 1) * 512,
                                  qq * 512:(qq + 1) * 512]
                    return e.dma_start(
                        out=WPH["wpool"][:, slot * 8192 + part * 2048:
                                         slot * 8192 + (part + 1) * 2048],
                        in_=srcw.rearrange("(a p) nn -> p a nn", p=128))
                last = sch.add("G", ldwk, waits=extra if part == 0 else [],
                               inc=f"dw{slot}", amt=16, tag=f"ldwk{qq}_{part}")
            return last

        pr_wcast = sch.add("G", lambda e: e.dma_start(
            out=d_wsh[:, :], in_=d_wshin[:, :]),
            inc="dws0", amt=16, tag="wcast")
        pr_agw = sch.add("G", lambda e: e.collective_compute(
            "AllGather", ALU.bypass, replica_groups=[list(range(NCORE))],
            ins=[d_wsh.opt()], outs=[d_wag.opt()]),
            waits=[pr_wcast], inc="cc", amt=1, tag="AG_w")
        WKPRE[0] = emit_wk_direct(0, 0, [])
        WKPRE[1] = emit_wk_direct(1, 1, [])

        def wag_quarter_load(w_idx, qq, dst_tile, dst_off, sem, extra_waits, tag):
            last = None
            for a1 in range(2):
                def ld(e, w_idx=w_idx, qq=qq, dst_tile=dst_tile, dst_off=dst_off,
                       a1=a1):
                    vv = d_wag.rearrange("(c w a1 p) d -> w a1 p c d", c=NCORE, w=5,
                                         a1=2)[w_idx, a1][:, :, qq * 512:(qq + 1) * 512]
                    ov = dst_tile[:, dst_off:dst_off + 8192].rearrange(
                        "p (c half nn) -> p c half nn", c=8, half=2)[:, :, a1, :]
                    return e.dma_start(out=ov, in_=vv)
                last = sch.add("Y", ld, waits=([pr_agw] + list(extra_waits))
                               if a1 == 0 else [], inc=sem, amt=16,
                               tag=f"{tag}_{a1}")
            return last

        with nc.sbuf_tensor("qT", [128, A * 1024], BF16) as qT:
            with (nc.sbuf_tensor("xnormT", [128, NT * D], BF16) as xnormT,
                  nc.sbuf_tensor("wpool", [128, 2 * A * 512], BF16) as wpool):
                WPH["wpool"] = wpool
                pr_spill = [None] * NT
                pr_xnT_evac = [None] * NT
                pr_xh = [None] * NT
                with (nc.sbuf_tensor("trash", [128, D], F32) as trash,
                      nc.sbuf_tensor("xnsc", [128, 2 * D], BF16) as xnsc,
                      nc.sbuf_tensor("gbc_s", [128, D], F32) as gbc,
                      nc.sbuf_tensor("bbc_s", [128, D], F32) as bbc):
                    pr_gbc = sch.add("Y", lambda e: e.dma_start(out=gbc[:, :], in_=d_gbc),
                                     inc="dY0", amt=16, tag="ld_gbc")
                    pr_bbc = sch.add("Y", lambda e: e.dma_start(out=bbc[:, :], in_=d_bbc),
                                     inc="dY0", amt=16, tag="ld_bbc")
                    pr_init_all = pr_bbc
                    pr_tb_last = [None, None]
                    pr_beta = [None] * NT
                    pr_tT = [None] * NT
                    for n in range(NT):
                        xt = xbuf[:, n * D:(n + 1) * D]
                        par = n % 2
                        trs = trash[:, :]
                        xns = xnsc[:, par * D:(par + 1) * D]
                        cs1 = smc.col(); cs2 = smc.col(4); cm1 = smc.col()
                        cv1 = smc.col(); cst1 = smc.col(); cr1 = smc.col()
                        cv2 = smc.col(); cst2 = smc.col(); cr2 = smc.col()
                        cmm = smc.col(); ca = smc.col(); cb = smc.col()
                        csf = smc.col()
                        sch.add("V", lambda e, xt=xt, cs1=cs1: e.reduce_sum(
                            sm[:, cs1:cs1 + 1], xt, axis=AX.X),
                            waits=[pr_xt[n]], tag=f"ln1_{n}_rs")

                        def sq(e, xt=xt, cs2=cs2):
                            last = None
                            for q in range(4):
                                last = e.activation(
                                    pAV[:, q * 512:(q + 1) * 512],
                                    xt[:, q * 512:(q + 1) * 512], AF.Square,
                                    accum_out=sm[:, cs2 + q:cs2 + q + 1])
                            return last
                        pr_sq = sch.add("S", sq, waits=[pr_xt[n]], inc="eS",
                                        tag=f"ln1_{n}_sq")
                        sch.add("V", lambda e, cm1=cm1, cs1=cs1: e.tensor_single_scalar(
                            sm[:, cm1:cm1 + 1], sm[:, cs1:cs1 + 1], 1.0 / D, ALU.mult),
                            tag=f"ln1_{n}_m")
                        sch.add("V", lambda e, cv1=cv1, cs2=cs2: e.reduce_sum(
                            sm[:, cv1:cv1 + 1], sm[:, cs2:cs2 + 4], axis=AX.X),
                            waits=[pr_sq], tag=f"ln1_{n}_v00")
                        sch.add("V", lambda e, cv1=cv1: e.tensor_single_scalar(
                            sm[:, cv1:cv1 + 1], sm[:, cv1:cv1 + 1], 1.0 / D, ALU.mult),
                            tag=f"ln1_{n}_v0")
                        sch.add("V", lambda e, cmm=cmm, cm1=cm1: e.tensor_tensor(
                            sm[:, cmm:cmm + 1], sm[:, cm1:cm1 + 1], sm[:, cm1:cm1 + 1],
                            ALU.mult), tag=f"ln1_{n}_mm")
                        pr_v1 = sch.add("V", lambda e, cv1=cv1, cmm=cmm: e.tensor_tensor(
                            sm[:, cv1:cv1 + 1], sm[:, cv1:cv1 + 1], sm[:, cmm:cmm + 1],
                            ALU.subtract), inc="eV", tag=f"ln1_{n}_v1")
                        pr_st1 = sch.add("S", lambda e, cst1=cst1, cv1=cv1: e.activation(
                            sm[:, cst1:cst1 + 1], sm[:, cv1:cv1 + 1], AF.Sqrt,
                            bias=sm[:, c_eps:c_eps + 1]),
                            waits=[pr_v1, pr_epsc], inc="eS", tag=f"ln1_{n}_std")
                        sch.add("V", lambda e, cr1=cr1, cst1=cst1: e.reciprocal(
                            sm[:, cr1:cr1 + 1], sm[:, cst1:cst1 + 1]),
                            waits=[pr_st1], tag=f"ln1_{n}_r")
                        sch.add("V", lambda e, cv2=cv2, cr1=cr1: e.tensor_tensor(
                            sm[:, cv2:cv2 + 1], sm[:, cr1:cr1 + 1], sm[:, cr1:cr1 + 1],
                            ALU.mult), tag=f"ln2_{n}_q")
                        pr_v2 = sch.add("V", lambda e, cv2=cv2, cv1=cv1: e.tensor_tensor(
                            sm[:, cv2:cv2 + 1], sm[:, cv2:cv2 + 1], sm[:, cv1:cv1 + 1],
                            ALU.mult), inc="eV", tag=f"ln2_{n}_v")
                        pr_st2 = sch.add("S", lambda e, cst2=cst2, cv2=cv2: e.activation(
                            sm[:, cst2:cst2 + 1], sm[:, cv2:cv2 + 1], AF.Sqrt,
                            bias=sm[:, c_eps:c_eps + 1]),
                            waits=[pr_v2], inc="eS", tag=f"ln2_{n}_std")
                        sch.add("V", lambda e, cr2=cr2, cst2=cst2: e.reciprocal(
                            sm[:, cr2:cr2 + 1], sm[:, cst2:cst2 + 1]),
                            waits=[pr_st2], tag=f"ln2_{n}_r")
                        sch.add("V", lambda e, csf=csf, cr1=cr1, cr2=cr2: e.tensor_tensor(
                            sm[:, csf:csf + 1], sm[:, cr1:cr1 + 1], sm[:, cr2:cr2 + 1],
                            ALU.mult), tag=f"ln2_{n}_sf")
                        sch.add("V", lambda e, ca=ca, cr1=cr1: e.tensor_single_scalar(
                            sm[:, ca:ca + 1], sm[:, cr1:cr1 + 1], 1.0, ALU.add),
                            tag=f"ln1_{n}_a")
                        sch.add("V", lambda e, cb=cb, cm1=cm1, cr1=cr1: e.tensor_tensor(
                            sm[:, cb:cb + 1], sm[:, cm1:cm1 + 1], sm[:, cr1:cr1 + 1],
                            ALU.mult), tag=f"ln1_{n}_b")
                        pr_t1 = sch.add("V", lambda e, xt=xt, trs=trs, cm1=cm1:
                            e.scalar_tensor_tensor(trs, xt, sm[:, cm1:cm1 + 1],
                                                   gbc[:, :], ALU.subtract, ALU.mult),
                            waits=[pr_init_all], inc="eV", tag=f"t1_{n}")
                        pr_xh[n] = sch.add("V", lambda e, xt=xt, ca=ca, cb=cb:
                            e.tensor_scalar(xt, xt, sm[:, ca:ca + 1],
                                            sm[:, cb:cb + 1], ALU.mult,
                                            op1=ALU.subtract),
                            waits=[pr_sq], inc="eV", tag=f"xh_{n}")
                        pr_spill[n] = sch.add("S", lambda e, n=n, xt=xt: e.dma_start(
                            out=d_xh[n * 128:(n + 1) * 128, :], in_=xt),
                            waits=[pr_xh[n]], inc="dSP", amt=16, tag=f"spill_{n}")
                        bw = [pr_init_all]
                        if n >= 2:
                            bw.append(pr_tT[n - 2])
                        pr_beta[n] = sch.add("V", lambda e, trs=trs, xns=xns, csf=csf:
                            e.scalar_tensor_tensor(
                            xns, trs, sm[:, csf:csf + 1],
                            bbc[:, :], ALU.mult, ALU.add),
                            waits=bw, inc="eV", tag=f"xnorm_{n}")
                        for half in range(2):
                            def tb(e, half=half, xns=xns):
                                last = None
                                for a8 in range(8):
                                    a = half * 8 + a8
                                    last = e.transpose(
                                        pT[:, half * 1024 + a8 * 128:
                                           half * 1024 + (a8 + 1) * 128],
                                        xns[:, a * 128:(a + 1) * 128], identb[:, :])
                                return last
                            wt = [pr_beta[n], pr_init_all]
                            if pr_tb_last[half] is not None:
                                wt.append(pr_tb_last[half])
                            pr_t = sch.add("T", tb, waits=wt, inc="eT",
                                           tag=f"xnT_{n}_{half}")
                            pr_tb_last[half] = sch.add(
                                "S", lambda e, n=n, half=half: e.activation(
                                    xnormT[:, n * D + half * 1024:
                                           n * D + (half + 1) * 1024],
                                    pT[:, half * 1024:(half + 1) * 1024], AF.Copy),
                                waits=[pr_t], inc="eS", tag=f"xnTev_{n}_{half}")
                        pr_tT[n] = pr_t
                        pr_xnT_evac[n] = pr_tb_last[1]
                    p1_done_V = pr_tb_last[1]
                    p1_done_G = pr_beta[NT - 1]

                with nc.sbuf_tensor("wpoolB", [128, 2 * A * 512], BF16) as wpoolB:
                    def wsl(slot):
                        return wpoolB if slot >= 2 else wpool

                    def wsl_off(slot):
                        return (slot - 2) * 8192 if slot >= 2 else slot * 8192
                    pr_kwrite = []
                    pr_vwrite = []
                    pr_sc_freeS = [None, None]
                    pr_stg_free = [None] * 4
                    stg_i = [0]
                    par_i = [0]
                    first_stg = [True]

                    def produce_piece(tag, lhsT_fn, rhs_fn, evac_dst_fn, dma_fn, waits_mm):
                        par = par_i[0] % 2
                        par_i[0] += 1
                        psl = pSC[:, par * 512:(par + 1) * 512]
                        wt = list(waits_mm)
                        if pr_sc_freeS[par] is not None:
                            wt.append(pr_sc_freeS[par])

                        def mms(e, lhsT_fn=lhsT_fn, rhs_fn=rhs_fn, psl=psl):
                            last = None
                            for a in range(A):
                                last = e.matmul(psl, lhsT_fn(a), rhs_fn(a),
                                                start=(a == 0), stop=(a == A - 1))
                            return last
                        pr_mm = sch.add("T", mms, waits=wt, inc="eT", tag=f"{tag}_mm")
                        if dma_fn is not None:
                            sl = stg_i[0] % 4
                            stg_i[0] += 1
                            dst = stg[:, sl * 512:(sl + 1) * 512]
                            ew = [pr_mm]
                            if pr_stg_free[sl] is not None:
                                ew.append(pr_stg_free[sl])
                            pr_ev = sch.add("S", lambda e, dst=dst, psl=psl: e.activation(
                                dst, psl, AF.Copy), waits=ew, inc="eS", tag=f"{tag}_ev")
                            pr_sc_freeS[par] = pr_ev
                            pr_d = sch.add("S", lambda e, dst=dst, dma_fn=dma_fn:
                                           dma_fn(e, dst),
                                           waits=[pr_ev], inc=f"dstg{sl}", amt=16,
                                           tag=f"{tag}_dma")
                            pr_stg_free[sl] = pr_d
                            return pr_mm, pr_d
                        pr_ev = sch.add("S", lambda e, psl=psl, f=evac_dst_fn: e.activation(
                            f(), psl, AF.Copy), waits=[pr_mm], inc="eS", tag=f"{tag}_ev")
                        pr_sc_freeS[par] = pr_ev
                        return pr_mm, pr_ev

                    pr_slot_freeT = [None, None, None, None]
                    gseq = [(w, qq) for w in ("k", "v", "q") for qq in range(4)]
                    wmap = {"q": 0, "k": 1, "v": 2}
                    pr_agkt = None
                    first_wagB = [True, True]
                    wag_i = [0]
                    for wname, qq in gseq:
                        if wname == "k":
                            slot = qq % 2
                            if qq < 2:
                                pr_w = WKPRE[qq]
                            else:
                                wtld = []
                                if pr_slot_freeT[slot] is not None:
                                    wtld.append(pr_slot_freeT[slot])
                                pr_w = emit_wk_direct(qq, slot, wtld)
                        else:
                            slot = 2 + wag_i[0] % 2
                            wag_i[0] += 1
                            wtld = []
                            if pr_slot_freeT[slot] is not None:
                                wtld.append(pr_slot_freeT[slot])
                            if first_wagB[slot - 2]:
                                wtld += [p1_done_V, p1_done_G]
                                first_wagB[slot - 2] = False
                            pr_w = wag_quarter_load(wmap[wname], qq, wsl(slot),
                                                    wsl_off(slot), f"dw{slot}", wtld,
                                                    f"ldw_{wname}{qq}")
                        last_mm = None
                        if wname == "k":
                            for g in range(2):
                                for ao4 in range(4):
                                    a_out = qq * 4 + ao4
                                    def lhsT_fn(a, slot=slot, ao4=ao4):
                                        base = wsl_off(slot)
                                        return wsl(slot)[:, base + a * 512 + ao4 * 128:
                                                         base + a * 512 + (ao4 + 1) * 128]
                                    def rhs_fn(a, g=g):
                                        v = xnormT[:, :].rearrange("p (n c) -> p n c", n=NT)
                                        return v[:, 4 * g:4 * g + 4, a * 128:(a + 1) * 128]
                                    def dma_fn(e, src, a_out=a_out, g=g):
                                        hh = a_out // 8
                                        dstv = d_ktsh[hh * 1024:(hh + 1) * 1024, :]
                                        dstv = dstv.rearrange("(j p) c -> j p c", p=128)[
                                            4 * g:4 * g + 4, :,
                                            (a_out % 8) * 128:((a_out % 8) + 1) * 128]
                                        return e.dma_start(
                                            out=dstv.rearrange("j p t -> p j t"),
                                            in_=src.rearrange("p (j t) -> p j t", j=4))
                                    pr_mm, pr_d = produce_piece(
                                        f"K{a_out}_{g}", lhsT_fn, rhs_fn, None, dma_fn,
                                        [pr_w, pr_xnT_evac[4 * g + 3]])
                                    pr_kwrite.append(pr_d)
                                    last_mm = pr_mm
                        elif wname == "v":
                            for j in range(NT):
                                def lhsT_fn(a, j=j):
                                    return xnormT[:, j * D + a * 128:
                                                  j * D + (a + 1) * 128]
                                def rhs_fn(a, slot=slot):
                                    base = wsl_off(slot)
                                    return wsl(slot)[:, base + a * 512:
                                                     base + (a + 1) * 512]
                                def dma_fn(e, src, j=j, qq=qq):
                                    hh = qq // 2
                                    return e.dma_start(
                                        out=d_vsh[hh * 1024 + j * 128:
                                                  hh * 1024 + (j + 1) * 128,
                                                  (qq % 2) * 512:((qq % 2) + 1) * 512],
                                        in_=src)
                                pr_mm, pr_d = produce_piece(
                                    f"V{qq}_{j}", lhsT_fn, rhs_fn, None, dma_fn,
                                    [pr_w, pr_xnT_evac[j]])
                                pr_vwrite.append(pr_d)
                                last_mm = pr_mm
                        else:
                            for g in range(2):
                                for ao4 in range(4):
                                    a_out = qq * 4 + ao4
                                    def lhsT_fn(a, slot=slot, ao4=ao4):
                                        base = wsl_off(slot)
                                        return wsl(slot)[:, base + a * 512 + ao4 * 128:
                                                         base + a * 512 + (ao4 + 1) * 128]
                                    def rhs_fn(a, g=g):
                                        v = xnormT[:, :].rearrange("p (n c) -> p n c", n=NT)
                                        return v[:, 4 * g:4 * g + 4, a * 128:(a + 1) * 128]
                                    def evac_dst_fn(a_out=a_out, g=g):
                                        return qT[:, a_out * 1024 + g * 512:
                                                  a_out * 1024 + (g + 1) * 512]
                                    pr_mm, pr_ev = produce_piece(
                                        f"Q{a_out}_{g}", lhsT_fn, rhs_fn, evac_dst_fn,
                                        None, [pr_w, pr_xnT_evac[4 * g + 3]])
                                    last_mm = pr_mm
                        pr_slot_freeT[slot] = last_mm
                        if wname == "k" and qq in (1, 3):
                            hh = 0 if qq == 1 else 1
                            pr_agkt = sch.add("G", lambda e, hh=hh: e.collective_compute(
                                "AllGather", ALU.bypass,
                                replica_groups=[list(range(NCORE))],
                                ins=[d_ktsh[hh * 1024:(hh + 1) * 1024, :]],
                                outs=[d_ktag[hh * 8192:(hh + 1) * 8192, :]]),
                                waits=list(pr_kwrite[hh * 16:(hh + 1) * 16]),
                                inc="cc", amt=1, tag=f"AG_kt{hh}")
                        if wname == "v" and qq in (1, 3):
                            hh = 0 if qq == 1 else 1
                            pr_agv = sch.add("G", lambda e, hh=hh: e.collective_compute(
                                "AllGather", ALU.bypass,
                                replica_groups=[list(range(NCORE))],
                                ins=[d_vsh[hh * 1024:(hh + 1) * 1024, :]],
                                outs=[d_vag[hh * 8192:(hh + 1) * 8192, :]]),
                                waits=list(pr_vwrite[hh * 16:(hh + 1) * 16]),
                                inc="cc", amt=1, tag=f"AG_v{hh}")
                    pr_p3_T = [pr_slot_freeT[0], pr_slot_freeT[1]]
                    pr_p3_S = [pr_sc_freeS[0], pr_sc_freeS[1]]
                    pr_p3_stg = list(pr_stg_free)

            with (nc.sbuf_tensor("kbuf", [128, 2 * 4 * D], BF16) as kbuf,
                  nc.sbuf_tensor("vbuf", [128, 2 * 4 * D], BF16) as vbuf,
                  nc.sbuf_tensor("maskb", [128, NT * 2 * KC], BF16) as maskb,
                  nc.sbuf_tensor("pbuf", [128, 2 * KC], BF16) as pbuf,
                  nc.sbuf_tensor("ptile", [128, 2 * KC], BF16) as ptile):
                jobs = []
                for i in range(NKC):
                    for j in slots_for_chunk(i):
                        jobs.append((j, i))
                njobs = len(jobs)
                slot_jobidx = {j: [] for j in range(NT)}
                for idx, (j, i) in enumerate(jobs):
                    slot_jobidx[j].append(idx)

                c_mA = smc.col(NT); c_mB = smc.col(NT)
                c_srun = smc.col(NT)
                c_cm = smc.col(2); c_d = smc.col(2); c_nb = smc.col(2)
                c_cs = smc.col(2); c_fac = smc.col(2)
                c_rec = smc.col(NT); c_gate = smc.col(NT)

                region_guard = pr_p3_T + pr_p3_S + pr_p3_stg
                pr_mask = sch.add("Y", lambda e: e.dma_start(
                    out=maskb[:, :].rearrange("p (j i t) -> p j i t", j=NT, i=2),
                    in_=d_masks.rearrange("j i p t -> p j i t")),
                    waits=region_guard, inc="dYm", amt=16, tag="ld_masks")
                pr_minit = sch.add("V", lambda e: e.memset(
                    sm[:, c_mA:c_mA + 2 * NT], NEG), inc="eV", tag="m_init")
                pr_sinit = sch.add("V", lambda e: e.memset(
                    sm[:, c_srun:c_srun + NT], 0.0), inc="eV", tag="s_init")

                pr_kv = [None] * NKC
                pr_ktload = [None] * NKC
                pr_scores = [None] * njobs
                pr_transp = [None] * njobs
                pr_attnv = [None] * njobs
                pr_nb = [None] * njobs
                pr_evac = [None] * njobs
                pr_aostt = [None] * njobs
                pr_av_dv = [None] * njobs
                pr_ao_dv = [None] * njobs
                pr_srunstt = [None] * njobs
                pr_factor = [None] * njobs
                pr_probs = [None] * njobs
                PR_CHUNK_DONE = {}

                def emit_kv_load(i):
                    par = i % 2
                    wt = [pr_agkt]
                    if i == 0:
                        wt.append(pr_mask)
                    if i >= 2:
                        wt.append(PR_CHUNK_DONE[i - 2])
                    lastk = None
                    lastv = None
                    rows = []
                    for s in range(4):
                        tau = 4 * i + s
                        jt = tau // 8
                        r = (tau % 8 - jt) % 8
                        rows.append(r * (NT * 128) + jt * 128)
                    for s in range(4):
                        for hh in range(2):
                            def ldk(e, row0=rows[s], par=par, s=s, hh=hh):
                                return e.dma_start(
                                    out=kbuf[:, par * 4 * D + s * D + hh * 1024:
                                             par * 4 * D + s * D + (hh + 1) * 1024],
                                    in_=d_ktag[hh * 8192 + row0:hh * 8192 + row0 + 128, :])
                            lastk = sch.add("Y", ldk,
                                            waits=wt if (s == 0 and hh == 0) else [],
                                            inc=f"dK{i % 2}", amt=16,
                                            tag=f"ldkt_{i}_{s}_{hh}")
                    wtv = [pr_agv] if i < 2 else [PR_CHUNK_DONE[i - 2]]
                    for s in range(4):
                        for hh in range(2):
                            def ldv(e, row0=rows[s], par=par, s=s, hh=hh):
                                return e.dma_start(
                                    out=vbuf[:, par * 4 * D + s * D + hh * 1024:
                                             par * 4 * D + s * D + (hh + 1) * 1024],
                                    in_=d_vag[hh * 8192 + row0:hh * 8192 + row0 + 128, :])
                            lastv = sch.add("G", ldv,
                                            waits=wtv if (s == 0 and hh == 0) else [],
                                            inc=f"dV{i % 2}", amt=16,
                                            tag=f"ldv_{i}_{s}_{hh}")
                    pr_ktload[i] = lastk
                    return lastv

                def emit_scores(n):
                    j, i = jobs[n]
                    par = n % 2
                    psl = pSC[:, par * 512:(par + 1) * 512]
                    masked = (i >= 2 * j)
                    wt = [pr_ktload[i]]
                    if n >= 2:
                        wt += [pr_nb[n - 2], pr_probs[n - 2]]
                    else:
                        wt += [pr_minit, pr_sinit]
                    if n == 0:
                        wt.append(pr_spill[NT - 1])

                    def mms(e, j=j, i=i, psl=psl, masked=masked):
                        kv = kbuf[:, (i % 2) * 4 * D:((i % 2) + 1) * 4 * D].rearrange(
                            "p (s c) -> p s c", s=4)
                        last = None
                        for a in range(A):
                            last = e.matmul(
                                psl, qT[:, a * 1024 + j * 128:a * 1024 + (j + 1) * 128],
                                kv[:, :, a * 128:(a + 1) * 128],
                                start=(a == 0), stop=(a == A - 1 and not masked))
                        if masked:
                            ii = i - 2 * j
                            last = e.matmul(
                                psl, identb[:, :],
                                maskb[:, (j * 2 + ii) * 512:(j * 2 + ii + 1) * 512],
                                start=False, stop=True)
                        return last
                    pr_scores[n] = sch.add("T", mms, waits=wt, inc="eT", tag=f"sc_{n}")

                def emit_transp(n):
                    par = n % 2
                    wt = [pr_probs[n]]
                    if n >= 2:
                        wt.append(pr_evac[n - 2])

                    def tr(e, par=par):
                        last = None
                        for s in range(4):
                            last = e.transpose(
                                pT[:, par * 1024 + s * 128:par * 1024 + (s + 1) * 128],
                                pbuf[:, par * 512 + s * 128:par * 512 + (s + 1) * 128],
                                identb[:, :])
                        return last
                    pr_transp[n] = sch.add("T", tr, waits=wt, inc="eT", tag=f"tr_{n}")

                def emit_attnv(n):
                    j, i = jobs[n]
                    par = n % 2
                    proms = []
                    for dv in range(4):
                        wt = [pr_evac[n], pr_kv[i]] if dv == 0 else []
                        if n >= 1:
                            wt.append(pr_ao_dv[n - 1][dv])

                        def av(e, n=n, i=i, dv=dv, par=par):
                            last = None
                            for s in range(4):
                                last = e.matmul(
                                    pAV[:, dv * 512:(dv + 1) * 512],
                                    ptile[:, par * 512 + s * 128:
                                          par * 512 + (s + 1) * 128],
                                    vbuf[:, (i % 2) * 4 * D + s * D + dv * 512:
                                         (i % 2) * 4 * D + s * D + (dv + 1) * 512],
                                    start=(s == 0), stop=(s == 3))
                            return last
                        proms.append(sch.add("T", av, waits=wt, inc="eT",
                                             tag=f"av_{n}_{dv}"))
                    pr_av_dv[n] = proms
                    pr_attnv[n] = proms[3]

                def emit_vopsA(n):
                    j, i = jobs[n]
                    par = n % 2
                    k_in_slot = slot_jobidx[j].index(n)
                    mo = c_mA + j if (k_in_slot % 2 == 0) else c_mB + j
                    mn = c_mB + j if (k_in_slot % 2 == 0) else c_mA + j
                    psl = pSC[:, par * 512:(par + 1) * 512]
                    sch.add("V", lambda e, psl=psl, par=par: e.reduce_max(
                        sm[:, c_cm + par:c_cm + par + 1], psl, axis=AX.X),
                        waits=[pr_scores[n]], tag=f"cmax_{n}")
                    sch.add("V", lambda e, mo=mo, mn=mn, par=par: e.tensor_tensor(
                        sm[:, mn:mn + 1], sm[:, mo:mo + 1],
                        sm[:, c_cm + par:c_cm + par + 1], ALU.max), tag=f"mnew_{n}")
                    sch.add("V", lambda e, mo=mo, mn=mn, par=par: e.tensor_tensor(
                        sm[:, c_d + par:c_d + par + 1], sm[:, mo:mo + 1],
                        sm[:, mn:mn + 1], ALU.subtract), tag=f"d_{n}")
                    pr_nb[n] = sch.add("V", lambda e, mn=mn, par=par: e.tensor_single_scalar(
                        sm[:, c_nb + par:c_nb + par + 1], sm[:, mn:mn + 1], -SCL,
                        ALU.mult), inc="eV", tag=f"nb_{n}")

                def emit_vopsB(n):
                    j, i = jobs[n]
                    par = n % 2
                    pr_srunstt[n] = sch.add("V", lambda e, j=j, par=par:
                        e.scalar_tensor_tensor(
                            sm[:, c_srun + j:c_srun + j + 1],
                            sm[:, c_srun + j:c_srun + j + 1],
                            sm[:, c_fac + par:c_fac + par + 1],
                            sm[:, c_cs + par:c_cs + par + 1], ALU.mult, ALU.add),
                        waits=[pr_probs[n], pr_factor[n]], inc="eV", tag=f"srun_{n}")
                    pr_evac[n] = sch.add("V", lambda e, par=par: e.tensor_copy(
                        ptile[:, par * 512:(par + 1) * 512],
                        pT[:, par * 1024:par * 1024 + 512]),
                        waits=[pr_transp[n]], inc="eV", tag=f"pev_{n}")

                def emit_vopsC(n):
                    j, i = jobs[n]
                    par = n % 2
                    proms = []
                    for dv in range(4):
                        proms.append(sch.add("V", lambda e, j=j, par=par, dv=dv:
                            e.scalar_tensor_tensor(
                                xbuf[:, j * D + dv * 512:j * D + (dv + 1) * 512],
                                xbuf[:, j * D + dv * 512:j * D + (dv + 1) * 512],
                                sm[:, c_fac + par:c_fac + par + 1],
                                pAV[:, dv * 512:(dv + 1) * 512],
                                ALU.mult, ALU.add),
                            waits=[pr_av_dv[n][dv]], inc="eV", tag=f"aostt_{n}_{dv}"))
                    pr_ao_dv[n] = proms
                    pr_aostt[n] = proms[3]

                def emit_sops(n):
                    par = n % 2
                    wt = [pr_nb[n]]
                    if n >= 2:
                        wt += [pr_aostt[n - 2], pr_srunstt[n - 2]]
                    pr_factor[n] = sch.add("S", lambda e, par=par: e.activation(
                        sm[:, c_fac + par:c_fac + par + 1],
                        sm[:, c_d + par:c_d + par + 1], AF.Exp, scale=SCL),
                        waits=wt, inc="eS", tag=f"fac_{n}")
                    wtp = [pr_scores[n]]
                    if n >= 2:
                        wtp.append(pr_transp[n - 2])
                    psl = pSC[:, par * 512:(par + 1) * 512]
                    pr_probs[n] = sch.add("S", lambda e, psl=psl, par=par: e.activation(
                        pbuf[:, par * 512:(par + 1) * 512], psl, AF.Exp,
                        bias=sm[:, c_nb + par:c_nb + par + 1], scale=SCL,
                        accum_out=sm[:, c_cs + par:c_cs + par + 1]),
                        waits=wtp, inc="eS", tag=f"probs_{n}")

                cur_chunk = -1
                for n in range(njobs):
                    j, i = jobs[n]
                    if i != cur_chunk:
                        cur_chunk = i
                        pr_kv[i] = emit_kv_load(i)
                    if n == 0:
                        emit_scores(0)
                    emit_vopsA(n)
                    emit_sops(n)
                    if n + 1 < njobs:
                        jj, ii = jobs[n + 1]
                        if ii != cur_chunk:
                            pr_kv[ii] = emit_kv_load(ii)
                            cur_chunk = ii
                        emit_scores(n + 1)
                    emit_transp(n)
                    emit_vopsB(n)
                    emit_attnv(n)
                    emit_vopsC(n)
                    PR_CHUNK_DONE[i] = pr_attnv[n]
                pr_attn_done = pr_aostt[njobs - 1]

            with (nc.sbuf_tensor("aon", [128, 2 * D], BF16) as aon,
                  nc.sbuf_tensor("aot", [128, NT * D], BF16) as aot,
                  nc.sbuf_tensor("wpool2", [128, 2 * A * 512], BF16) as wpool2,
                  nc.sbuf_tensor("xhb", [128, 4 * 512], F32) as xhb):
                pr_aotev = [None] * NT
                pr_tb2 = [None, None]
                pr_transp_aot = [None] * NT
                for j in range(NT):
                    last_of_slot = pr_aostt[slot_jobidx[j][-1]]
                    sch.add("V", lambda e, j=j: e.reciprocal(
                        sm[:, c_rec + j:c_rec + j + 1],
                        sm[:, c_srun + j:c_srun + j + 1]),
                        waits=[last_of_slot], tag=f"rec_{j}")
                    wt = [last_of_slot]
                    if j >= 2:
                        wt.append(pr_transp_aot[j - 2])
                    pr_aonj = sch.add("V", lambda e, j=j: e.tensor_single_scalar(
                        aon[:, (j % 2) * D:((j % 2) + 1) * D],
                        xbuf[:, j * D:(j + 1) * D],
                        sm[:, c_rec + j:c_rec + j + 1], ALU.mult),
                        waits=wt, inc="eV", tag=f"aon_{j}")
                    prt = None
                    for half in range(2):
                        wt2 = [pr_aonj]
                        if pr_tb2[half] is not None:
                            wt2.append(pr_tb2[half])

                        def tb2(e, j=j, half=half):
                            last = None
                            for a8 in range(8):
                                a = half * 8 + a8
                                last = e.transpose(
                                    pT[:, half * 1024 + a8 * 128:
                                       half * 1024 + (a8 + 1) * 128],
                                    aon[:, (j % 2) * D + a * 128:
                                        (j % 2) * D + (a + 1) * 128],
                                    identb[:, :])
                            return last
                        prt = sch.add("T", tb2, waits=wt2, inc="eT",
                                      tag=f"aotT_{j}_{half}")
                        pr_tb2[half] = sch.add("V", lambda e, j=j, half=half: e.tensor_copy(
                            aot[:, j * D + half * 1024:j * D + (half + 1) * 1024],
                            pT[:, half * 1024:(half + 1) * 1024]),
                            waits=[prt], inc="eV", tag=f"aotEv_{j}_{half}")
                    pr_transp_aot[j] = prt
                    pr_aotev[j] = pr_tb2[1]

                pr_wo = [None] * 4
                last_mm_of_dq = [None] * 4

                def emit_wo_load(qq, extra_wait):
                    slot = qq % 2
                    pr_wo[qq] = wag_quarter_load(3, qq, wpool2, slot * 8192,
                                                 f"dw{2 + slot}", extra_wait,
                                                 f"ldwo{qq}")

                emit_wo_load(0, [pr_attn_done])
                emit_wo_load(1, [])
                pr_epi5 = {}
                pc5 = 0
                for dq in range(4):
                    for j in range(NT):
                        k = pc5
                        pc5 += 1
                        bank = k % 4
                        xsl = k % 4
                        wtx = []
                        if k >= 4:
                            wtx.append(pr_epi5[k - 4])
                        if k == 0:
                            wtx.append(pr_attn_done)
                        pr_xh_ld = sch.add("Y", lambda e, j=j, dq=dq, xsl=xsl: e.dma_start(
                            out=xhb[:, xsl * 512:(xsl + 1) * 512],
                            in_=d_xh[j * 128:(j + 1) * 128, dq * 512:(dq + 1) * 512]),
                            waits=wtx, inc=f"dX{k % 4}", amt=16, tag=f"xhld_{k}")
                        wtm = [pr_wo[dq], pr_aotev[j]]
                        if k >= 4:
                            wtm.append(pr_epi5[k - 4])

                        def mmo(e, j=j, dq=dq, bank=bank):
                            slot = dq % 2
                            last = None
                            for a in range(A):
                                last = e.matmul(
                                    pAV[:, bank * 512:(bank + 1) * 512],
                                    aot[:, j * D + a * 128:j * D + (a + 1) * 128],
                                    wpool2[:, slot * 8192 + a * 512:
                                           slot * 8192 + (a + 1) * 512],
                                    start=(a == 0), stop=(a == A - 1))
                            return last
                        pr_mm = sch.add("T", mmo, waits=wtm, inc="eT", tag=f"opro_{k}")
                        last_mm_of_dq[dq] = pr_mm
                        pr_epi5[k] = sch.add("V", lambda e, j=j, dq=dq, bank=bank,
                                             xsl=xsl: e.tensor_tensor(
                            xbuf[:, j * D + dq * 512:j * D + (dq + 1) * 512],
                            xhb[:, xsl * 512:(xsl + 1) * 512],
                            pAV[:, bank * 512:(bank + 1) * 512], ALU.add),
                            waits=[pr_mm, pr_xh_ld], inc="eV", tag=f"epi5_{k}")
                    if dq + 2 <= 3:
                        emit_wo_load(dq + 2, [last_mm_of_dq[dq]])
                pr_x1_all = pr_epi5[pc5 - 1]

        with (nc.sbuf_tensor("g2T", [128, A * 1024], BF16) as g2T,
              nc.sbuf_tensor("g2sc", [128, 2 * D], BF16) as g2sc,
              nc.sbuf_tensor("Sbis", [128, 1024], F32) as Sbis,
              nc.sbuf_tensor("bmask", [128, 1024], BF16) as bmask,
              nc.sbuf_tensor("wpool3", [128, 2 * A * 512], BF16) as wpool3,
              nc.sbuf_tensor("bebc_s", [128, D], F32) as bebc,
              nc.sbuf_tensor("ebuf", [128, 20 * 512], F32) as ebuf,
              nc.sbuf_tensor("outb", [128, 4 * 512], F32) as outb,
              nc.sbuf_tensor("sstg", [16, 256], F32) as sstg,
              nc.sbuf_tensor("thbc", [128, 16], F32) as thbc,
              nc.sbuf_tensor("throw", [1, 16], F32) as throw,
              nc.sbuf_tensor("tmp16", [128, 16], F32) as tmp16):
            c_rmax = smc.col(2); c_nrm = smc.col(2); c_ssum = smc.col(2); c_sr = smc.col(2)
            c_lo = smc.col(); c_hi = smc.col(); c_mid = smc.col()
            c_cnt = smc.col(); c_cond = smc.col(); c_d1 = smc.col(); c_d2 = smc.col()
            c_red = smc.col()

            pr_bebc = sch.add("Y", lambda e: e.dma_start(out=bebc[:, :], in_=d_bebc),
                              waits=[pr_x1_all], inc="dbe", amt=16, tag="ld_bebc")

            pr_g2ev = [None] * NT
            pr_Sfin = [None] * NT
            pr_sT_dma = [None] * NT
            pr_tb3 = [None, None]
            pr_scread = [None, None]
            pr_sstg_free = [None, None]
            pr_we = [None] * 4

            def emit_we_load(qq, extra_wait):
                slot = qq % 2
                pr_we[qq] = wag_quarter_load(4, qq, wpool3, slot * 8192,
                                             f"dw{2 + slot}", extra_wait, f"ldwe{qq}")

            emit_we_load(0, [pr_x1_all])
            emit_we_load(1, [])
            pr_out = []
            pr_mmk = {}
            pr_e1k = {}
            pr_ock = {}
            pr_dmak = {}
            last_mm_of_dq7 = [None] * 4
            piece_list = [(dq, j) for dq in range(4) for j in range(NT)]

            def emit_piece_mm(k):
                dq, j = piece_list[k]
                bank = k % 4
                wtm = [pr_we[dq], pr_g2ev[j]]
                if k >= 4:
                    wtm.append(pr_e1k[k - 4])

                def mme(e, j=j, dq=dq, bank=bank):
                    slot = dq % 2
                    last = None
                    for a in range(A):
                        last = e.matmul(
                            pAV[:, bank * 512:(bank + 1) * 512],
                            g2T[:, a * 1024 + j * 128:a * 1024 + (j + 1) * 128],
                            wpool3[:, slot * 8192 + a * 512:
                                   slot * 8192 + (a + 1) * 512],
                            start=(a == 0), stop=(a == A - 1))
                    return last
                pr_mmk[k] = sch.add("T", mme, waits=wtm, inc="eT", tag=f"exp_{k}")
                last_mm_of_dq7[dq] = pr_mmk[k]

            def emit_piece_e1(k):
                dq, j = piece_list[k]
                bank = k % 4
                esl = k % 20
                wte = [pr_mmk[k], pr_bebc]
                if k >= 20:
                    wte.append(pr_ock[k - 20])

                def e1(e, dq=dq, bank=bank, esl=esl):
                    return e.tensor_tensor(
                        ebuf[:, esl * 512:(esl + 1) * 512],
                        pAV[:, bank * 512:(bank + 1) * 512],
                        bebc[:, dq * 512:(dq + 1) * 512], ALU.add)
                pr_e1k[k] = sch.add("V", e1, waits=wte, inc="eV", tag=f"e1_{k}")

            def emit_piece_oc(k):
                dq, j = piece_list[k]
                esl = k % 20
                osl = k % 4
                wt = [pr_e1k[k], gate_done[0]]
                if k >= 4:
                    wt.append(pr_dmak[k - 4])
                pr_ock[k] = sch.add("V", lambda e, j=j, dq=dq, esl=esl, osl=osl:
                    e.scalar_tensor_tensor(
                        outb[:, osl * 512:(osl + 1) * 512],
                        ebuf[:, esl * 512:(esl + 1) * 512],
                        sm[:, c_gate + j:c_gate + j + 1],
                        xbuf[:, j * D + dq * 512:j * D + (dq + 1) * 512],
                        ALU.mult, ALU.add), waits=wt, inc="eV", tag=f"oc_{k}")
                osem = f"dX{osl}" if osl % 2 == 0 else f"dss{osl // 2}"
                pr_dmak[k] = sch.add("Y" if osl % 2 == 0 else "S",
                    lambda e, j=j, dq=dq, osl=osl: e.dma_start(
                    out=d_out[j * 128:(j + 1) * 128, dq * 512:(dq + 1) * 512],
                    in_=outb[:, osl * 512:(osl + 1) * 512]),
                    waits=[pr_ock[k]], inc=osem, amt=16, tag=f"odma_{k}")
                pr_out.append(pr_dmak[k])

            pr_g2done = [None] * NT
            for j in range(NT):
                x1t = xbuf[:, j * D:(j + 1) * D]
                m3, r3, pr_r3, pr_sq3 = emit_ln(x1t, [pr_x1_all], f"ln3_{j}", outb[:, :])
                wt = [pr_r3]
                if pr_tb3[1] is not None:
                    wt.append(pr_tb3[1])
                pr_g2 = sch.add("V", lambda e, j=j, m3=m3, r3=r3: e.tensor_scalar(
                    g2sc[:, (j % 2) * D:((j % 2) + 1) * D], xbuf[:, j * D:(j + 1) * D],
                    sm[:, m3:m3 + 1], sm[:, r3:r3 + 1], ALU.subtract, op1=ALU.mult),
                    waits=wt, inc="eV", tag=f"g2_{j}")
                for half in range(2):
                    wt2 = [pr_g2]
                    if pr_tb3[half] is not None:
                        wt2.append(pr_tb3[half])

                    def tb3(e, j=j, half=half):
                        last = None
                        for a8 in range(8):
                            a = half * 8 + a8
                            last = e.transpose(
                                pT[:, half * 1024 + a8 * 128:
                                   half * 1024 + (a8 + 1) * 128],
                                g2sc[:, (j % 2) * D + a * 128:
                                     (j % 2) * D + (a + 1) * 128],
                                identb[:, :])
                        return last
                    prt = sch.add("T", tb3, waits=wt2, inc="eT", tag=f"g2T_{j}_{half}")

                    def ev3(e, j=j, half=half):
                        v = g2T[:, :].rearrange("p (a jj t) -> p a jj t", a=A, jj=NT)
                        return e.tensor_copy(
                            v[:, half * 8:(half + 1) * 8, j, :],
                            pT[:, half * 1024:(half + 1) * 1024].rearrange(
                                "p (a8 t) -> p a8 t", a8=8))
                    pr_tb3[half] = sch.add("V", ev3, waits=[prt], inc="eV",
                                           tag=f"g2Tev_{j}_{half}")
                pr_g2ev[j] = pr_tb3[1]
                emit_piece_mm(j)
                emit_piece_e1(j)
                if j >= 4:
                    emit_piece_mm(8 + j - 4)
                if j == 7:
                    emit_we_load(2, [pr_mmk[7]])


            for j in range(NT):
                par = j % 2
                wtr = [pr_g2ev[j], pr_wr]
                if pr_scread[par] is not None:
                    wtr.append(pr_scread[par])

                def mml(e, j=j, par=par):
                    last = None
                    for a in range(A):
                        last = e.matmul(
                            pSC[:, par * 512:par * 512 + 16],
                            g2T[:, a * 1024 + j * 128:a * 1024 + (j + 1) * 128],
                            wr_sb[:, a * 16:(a + 1) * 16],
                            start=(a == 0), stop=(a == A - 1))
                    return last
                pr_lg = sch.add("T", mml, waits=wtr, inc="eT", tag=f"logits_{j}")
                sch.add("V", lambda e, j=j, par=par: e.tensor_tensor(
                    S_sb[:, j * 16:(j + 1) * 16], pSC[:, par * 512:par * 512 + 16],
                    rbbc[:, :], ALU.add), waits=[pr_lg, pr_init_all], tag=f"spre_{j}")
                sch.add("V", lambda e, j=j, par=par: e.reduce_max(
                    sm[:, c_rmax + par:c_rmax + par + 1],
                    S_sb[:, j * 16:(j + 1) * 16], axis=AX.X), tag=f"rmax_{j}")
                pr_nrm = sch.add("V", lambda e, par=par: e.tensor_single_scalar(
                    sm[:, c_nrm + par:c_nrm + par + 1],
                    sm[:, c_rmax + par:c_rmax + par + 1], -1.0, ALU.mult),
                    inc="eV", tag=f"nrm_{j}")
                pr_sexp = sch.add("S", lambda e, j=j, par=par: e.activation(
                    S_sb[:, j * 16:(j + 1) * 16], S_sb[:, j * 16:(j + 1) * 16], AF.Exp,
                    bias=sm[:, c_nrm + par:c_nrm + par + 1],
                    accum_out=sm[:, c_ssum + par:c_ssum + par + 1]),
                    waits=[pr_nrm], inc="eS", tag=f"sexp_{j}")
                sch.add("V", lambda e, par=par: e.reciprocal(
                    sm[:, c_sr + par:c_sr + par + 1],
                    sm[:, c_ssum + par:c_ssum + par + 1]),
                    waits=[pr_sexp], tag=f"srec_{j}")
                pr_Sfin[j] = sch.add("V", lambda e, j=j, par=par: e.tensor_single_scalar(
                    S_sb[:, j * 16:(j + 1) * 16], S_sb[:, j * 16:(j + 1) * 16],
                    sm[:, c_sr + par:c_sr + par + 1], ALU.mult),
                    inc="eV", tag=f"sfin_{j}")
                pr_st = sch.add("T", lambda e, j=j, par=par: e.matmul(
                    pSC[0:16, par * 512 + 128:par * 512 + 256],
                    S_sb[:, j * 16:(j + 1) * 16], i128f[:, :], start=True, stop=True),
                    waits=[pr_Sfin[j], pr_init_all], inc="eT", tag=f"sT_{j}")
                wts = [pr_st]
                if pr_sstg_free[par] is not None:
                    wts.append(pr_sstg_free[par])
                pr_stev = sch.add("V", lambda e, par=par: e.tensor_copy(
                    sstg[0:16, par * 128:(par + 1) * 128],
                    pSC[0:16, par * 512 + 128:par * 512 + 256]),
                    waits=wts, inc="eV", tag=f"sTev_{j}")
                pr_scread[par] = pr_stev
                pr_sT_dma[j] = sch.add("S", lambda e, j=j, par=par: e.dma_start(
                    out=d_ssT[j * 16:(j + 1) * 16, :],
                    in_=sstg[0:16, par * 128:(par + 1) * 128]),
                    waits=[pr_stev], inc=f"dss{par}", amt=16, tag=f"sTdma_{j}")
                pr_sstg_free[par] = pr_sT_dma[j]

            pr_ags = sch.add("G", lambda e: e.collective_compute(
                "AllGather", ALU.bypass, replica_groups=[list(range(NCORE))],
                ins=[d_ssT.opt()], outs=[d_sagT.opt()]),
                waits=[pr_sT_dma[j] for j in range(NT)], inc="cc", amt=1, tag="AG_s")
            pr_sbis = sch.add("Y", lambda e: e.dma_start(
                out=Sbis[:, :].rearrange("p (ri t) -> p ri t", ri=NCORE),
                in_=d_sagT.rearrange("(ri p) t -> p ri t", p=128)),
                waits=[pr_ags], inc="dYm", amt=16, tag="ld_sbis")

            pr_init_lo = sch.add("V", lambda e: e.memset(sm[:, c_lo:c_lo + 1], 0.0),
                                 inc="eV", tag="bis_lo")

            pr_red = [None] * BISECT_ITERS
            pr_cnt = [None] * BISECT_ITERS
            pr_ccV = [None] * BISECT_ITERS
            bis_k = [0]
            gate_done = [None]

            def emit_one_bisect():
                k = bis_k[0]
                if k >= BISECT_ITERS:
                    return
                bis_k[0] += 1
                delta = 0.5 ** (k + 1)
                wt = ([pr_sbis, pr_init_lo] if k == 0 else [pr_ccV[k - 1]])
                sch.add("V", lambda e, delta=delta: e.tensor_single_scalar(
                    sm[:, c_mid:c_mid + 1], sm[:, c_lo:c_lo + 1], delta, ALU.add),
                    waits=wt, tag=f"bth_{k}")
                pr_red[k] = sch.add("V", lambda e: e.tensor_scalar(
                    bmask[:, :], Sbis[:, :], sm[:, c_mid:c_mid + 1], 0.0,
                    ALU.is_ge, op1=ALU.add, accum_out=sm[:, c_red:c_red + 1]),
                    inc="eV", tag=f"bred_{k}")
                wtc = [pr_red[k], pr_init_all]
                if k >= 2:
                    wtc.append(pr_ccV[k - 2])

                def cnt(e, k=k):
                    par = k % 2
                    return e.matmul(pSC[:, par * 512 + 256:par * 512 + 257],
                                    e2f[:, :], sm[:, c_red:c_red + 1],
                                    start=True, stop=True)
                pr_cnt[k] = sch.add("T", cnt, waits=wtc, inc="eT", tag=f"bcnt_{k}")
                par = k % 2
                sch.add("V", lambda e, par=par: e.tensor_single_scalar(
                    sm[:, c_cond:c_cond + 1],
                    pSC[:, par * 512 + 256:par * 512 + 257], 512.0, ALU.is_ge),
                    waits=[pr_cnt[k]], tag=f"bc0_{k}")
                pr_ccV[k] = sch.add("V", lambda e, delta=delta: e.scalar_tensor_tensor(
                    sm[:, c_lo:c_lo + 1], sm[:, c_cond:c_cond + 1], delta,
                    sm[:, c_lo:c_lo + 1], ALU.mult, ALU.add),
                    inc="eV", tag=f"bc1_{k}")

            def emit_gate():
                pr_th1 = sch.add("T", lambda e: e.matmul(
                    pSC[0:1, 0:16], sm[0:16, c_lo:c_lo + 1], i128f[0:16, 0:16],
                    start=True, stop=True),
                    waits=[pr_ccV[BISECT_ITERS - 1], pr_init_all], inc="eT", tag="th_row")
                pr_thev = sch.add("V", lambda e: e.tensor_copy(
                    throw[0:1, :], pSC[0:1, 0:16]), waits=[pr_th1], inc="eV",
                    tag="th_ev")
                pr_th2 = sch.add("T", lambda e: e.matmul(
                    pSC[:, 16:32], onesf[0:1, :], throw[0:1, 0:16],
                    start=True, stop=True), waits=[pr_thev, pr_init_all], inc="eT",
                    tag="th_bc")
                pr_thbc = sch.add("V", lambda e: e.tensor_copy(
                    thbc[:, :], pSC[:, 16:32]), waits=[pr_th2], inc="eV",
                    tag="thbc_ev")
                last = None
                for j in range(NT):
                    sch.add("V", lambda e, j=j: e.tensor_tensor(
                        tmp16[:, :], S_sb[:, j * 16:(j + 1) * 16], thbc[:, :],
                        ALU.is_ge), waits=[pr_thbc], tag=f"gm_{j}")
                    sch.add("V", lambda e, j=j: e.tensor_tensor(
                        tmp16[:, :], S_sb[:, j * 16:(j + 1) * 16], tmp16[:, :],
                        ALU.mult), tag=f"gs_{j}")
                    last = sch.add("V", lambda e, j=j: e.reduce_sum(
                        sm[:, c_gate + j:c_gate + j + 1], tmp16[:, :], axis=AX.X),
                        inc="eV", tag=f"gate_{j}")
                gate_done[0] = last


            for k in range(8, 12):
                emit_piece_e1(k)
            for k in range(12, 16):
                emit_piece_mm(k)
                emit_piece_e1(k)
            emit_we_load(3, [pr_mmk[15]])
            inter = {4: 16, 10: 17, 16: 18, 22: 19}
            while bis_k[0] < BISECT_ITERS:
                emit_one_bisect()
                k2 = inter.get(bis_k[0])
                if k2 is not None:
                    emit_piece_mm(k2)
                    emit_piece_e1(k2)
            emit_gate()
            for k in range(20):
                emit_piece_oc(k)
            for k in range(20, 32):
                emit_piece_mm(k)
                emit_piece_e1(k)
                emit_piece_oc(k)

            sch.add("Y", lambda e: e.nop(), waits=[pr_dmak[28], pr_dmak[29],
                                                    pr_dmak[30], pr_dmak[31]],
                    tag="final_wait")

        sch.resolve()
        sch.check()
        if check_only:
            counts = {k: len(sch.q[k]) for k in sch.QUEUES}
            print("schedule ok:", counts)
            return None
        with nc.Block() as block:
            @block.tensor
            def _(t):
                sch.emit_queue("T", t, semh)

            @block.vector
            def _(v):
                sch.emit_queue("V", v, semh)

            @block.scalar
            def _(s):
                sch.emit_queue("S", s, semh)

            @block.sync
            def _(y):
                sch.emit_queue("Y", y, semh)

            @block.gpsimd
            def _(g):
                sch.emit_queue("G", g, semh)
    return nc


def kernel(L0_attn_x, L0_attn_qkv, L0_attn_o_proj, L0_attn_gamma, L0_attn_beta,
           L0_ffn_router_w, L0_ffn_router_b, L0_ffn_expert_w, L0_ffn_expert_b):
    x = np.ascontiguousarray(np.asarray(L0_attn_x, dtype=np.float32).reshape(S_LEN, D))
    qkv = np.asarray(L0_attn_qkv, dtype=np.float32)
    wq = np.ascontiguousarray(qkv[:, 0:D])
    wk = np.ascontiguousarray(qkv[:, D:2 * D])
    wv = np.ascontiguousarray(qkv[:, 2 * D:3 * D])
    wo = np.ascontiguousarray(np.asarray(L0_attn_o_proj, dtype=np.float32))
    we = np.ascontiguousarray(np.asarray(L0_ffn_expert_w, dtype=np.float32))
    wr = np.ascontiguousarray(np.asarray(L0_ffn_router_w, dtype=np.float32))
    gamma = np.asarray(L0_attn_gamma, dtype=np.float32)
    beta = np.asarray(L0_attn_beta, dtype=np.float32)
    rb = np.asarray(L0_ffn_router_b, dtype=np.float32)
    be = np.asarray(L0_ffn_expert_b, dtype=np.float32)

    gbc = np.ascontiguousarray(np.broadcast_to(gamma[None, :], (128, D)))
    bbc = np.ascontiguousarray(np.broadcast_to(beta[None, :], (128, D)))
    bebc = np.ascontiguousarray(np.broadcast_to(be[None, :], (128, D)))
    rbbc = np.ascontiguousarray(np.broadcast_to(rb[None, :], (128, 16)))
    identb = np.eye(128, dtype=np.float32).astype(bfloat16)
    i128f = np.eye(128, dtype=np.float32)
    e2f = (np.arange(128)[:, None] % 16 == np.arange(128)[None, :] % 16).astype(np.float32)
    onesf = np.ones((1, 128), dtype=np.float32)

    row = np.arange(128)
    col = np.arange(KC)
    in_maps = []
    for c in range(NCORE):
        tiles = [T_of(c, j) for j in range(NT)]
        idx = np.concatenate([np.arange(t * 128, (t + 1) * 128) for t in tiles])
        x_own = np.ascontiguousarray(x[idx])
        masks = np.zeros((NT, 2, 128, KC), dtype=np.float32)
        for j in range(NT):
            t = tiles[j]
            q_global = t * 128 + row
            for ii in range(2):
                chunk = 2 * j + ii
                k_global = chunk * KC + col
                masks[j, ii] = np.where(k_global[None, :] <= q_global[:, None], 0.0, NEG)
        in_maps.append({
            "x": x_own, "wk32": wk, "wshin": np.ascontiguousarray(np.concatenate(
                [wq[256 * c:256 * (c + 1)], wk[256 * c:256 * (c + 1)],
                 wv[256 * c:256 * (c + 1)], wo[256 * c:256 * (c + 1)],
                 we[256 * c:256 * (c + 1)]], axis=0)), "wr": wr,
            "gbc": gbc, "bbc": bbc, "bebc": bebc, "rbbc": rbbc,
            "masks": masks.astype(bfloat16),
            "identb": identb, "i128f": i128f, "e2f": e2f, "onesf": onesf,
        })

    nc = build_program()
    res = run_bass_kernel_spmd(nc, in_maps, core_ids=list(range(NCORE)))

    full = np.zeros((S_LEN, D), dtype=np.float32)
    for c in range(NCORE):
        o = res.results[c]["out"]
        for j in range(NT):
            t = T_of(c, j)
            full[t * 128:(t + 1) * 128, :] = o[j * 128:(j + 1) * 128, :]
    return full.reshape(1, S_LEN, D)


if __name__ == "__main__":
    import sys
    if "--check" in sys.argv:
        build_program(check_only=True)
    else:
        nc = build_program()
        n_inst = sum(1 for f in nc.m.functions for b in f.blocks
                     for i in b.instructions)
        print("built ok, instructions:", n_inst)


# version 17
# speedup vs baseline: 1.0535x; 1.0091x over previous
import numpy as np
import ml_dtypes
import concourse.bass as bass
import concourse.mybir as mybir
from concourse.bass_utils import run_bass_kernel_spmd
from contextlib import ExitStack

F32 = mybir.dt.float32
BF16 = mybir.dt.bfloat16
AF = mybir.ActivationFunctionType
ALU = mybir.AluOpType
AX = mybir.AxisListType
bfloat16 = ml_dtypes.bfloat16

NCORE = 8
D = 2048
S_LEN = 8192
A = 16
NT = 8
KC = 512
NKC = 16
SCL = 1.0 / float(np.sqrt(2048.0))
EPS = 1e-5
NEG = -1e30
BISECT_ITERS = 36


def T_of(c, j):
    return 8 * j + (c + j) % 8


def slots_for_chunk(i):
    return [j for j in range(NT) if 2 * j + 2 > i]


class Prom:
    __slots__ = ("sem", "val")

    def __init__(self, sem):
        self.sem = sem
        self.val = None


class Oper:
    __slots__ = ("fn", "waits", "inc_sem", "inc_amt", "prom", "tag")

    def __init__(self, fn, waits, inc_sem, inc_amt, prom, tag):
        self.fn = fn; self.waits = waits; self.inc_sem = inc_sem
        self.inc_amt = inc_amt; self.prom = prom; self.tag = tag


class Sched:
    QUEUES = ("T", "V", "S", "Y", "G")
    SEMS = ("eT", "eV", "eS", "eG", "cc", "dY0", "dYm", "dK0", "dK1", "dV0", "dV1",
            "dX0", "dX1", "dX2", "dX3", "dSP", "dstg0", "dstg1", "dstg2", "dstg3",
            "dss0", "dss1", "dwr", "dw0", "dw1", "dw2", "dw3", "dbe", "dws0", "dws1",
            "dxv")

    def __init__(self):
        self.q = {k: [] for k in self.QUEUES}

    ENGINE_SEM = {"T": "eT", "V": "eV", "S": "eS", "G": "eG"}

    def add(self, queue, fn, waits=(), inc=None, amt=1, tag=""):
        if inc is None and queue in ("V", "S", "G"):
            inc = self.ENGINE_SEM[queue]
            amt = 1
        prom = Prom(inc) if inc is not None else None
        waits = [w for w in waits if w is not None]
        self.q[queue].append(Oper(fn, waits, inc, amt, prom, tag))
        return prom

    def resolve(self):
        for qn in ("V", "S", "G"):
            es = self.ENGINE_SEM[qn]
            prev = None
            for o in self.q[qn]:
                if o.inc_sem == es:
                    if prev is not None:
                        o.waits.append(prev)
                    prev = o.prom
        ctr = dict.fromkeys(self.SEMS, 0)
        for qn in self.QUEUES:
            for o in self.q[qn]:
                if o.inc_sem is not None:
                    ctr[o.inc_sem] += o.inc_amt
                    o.prom.val = ctr[o.inc_sem]

    def check(self):
        heads = {k: 0 for k in self.QUEUES}
        cur = dict.fromkeys(self.SEMS, 0)
        progress = True
        while progress:
            progress = False
            for qn in self.QUEUES:
                while heads[qn] < len(self.q[qn]):
                    o = self.q[qn][heads[qn]]
                    if all(w.val is not None and cur[w.sem] >= w.val for w in o.waits):
                        if o.inc_sem is not None:
                            cur[o.inc_sem] += o.inc_amt
                        heads[qn] += 1
                        progress = True
                    else:
                        break
        stuck = {k: (heads[k], len(self.q[k])) for k in self.QUEUES
                 if heads[k] < len(self.q[k])}
        if stuck:
            lines = []
            for k, (h, n) in stuck.items():
                o = self.q[k][h]
                missing = [(w.sem, w.val, cur[w.sem]) for w in o.waits
                           if w.val is None or cur[w.sem] < w.val]
                lines.append(f"queue {k} stuck at {h}/{n} tag={o.tag} missing={missing}")
            raise RuntimeError("SCHEDULE DEADLOCK:\n" + "\n".join(lines))

    def emit_queue(self, qn, eng, semh):
        water = dict.fromkeys(self.SEMS, 0)
        for o in self.q[qn]:
            for w in o.waits:
                assert w.val is not None, f"unresolved promise in {qn} tag={o.tag}"
                if w.val > water[w.sem]:
                    eng.wait_ge(semh[w.sem], w.val)
                    water[w.sem] = w.val
            instr = o.fn(eng)
            if o.inc_sem is not None:
                assert instr is not None, f"op {o.tag} returned no instruction"
                if o.inc_sem == "cc":
                    instr.then_inc(semh["cc"])
                else:
                    instr.then_inc(semh[o.inc_sem], o.inc_amt)


def build_program(check_only=False):
    nc = bass.Bass("TRN2", target_bir_lowering=False)
    sch = Sched()
    EPI_TAIL = []

    d_x = nc.dram_tensor("x", [NT * 128, D], F32, kind="ExternalInput").ap()
    d_wshin = nc.dram_tensor("wshin", [4 * 256, D], F32, kind="ExternalInput").ap()
    d_wk32 = nc.dram_tensor("wk32", [D, D], F32, kind="ExternalInput").ap()
    d_wr = nc.dram_tensor("wr", [D, 16], F32, kind="ExternalInput").ap()
    d_gbc = nc.dram_tensor("gbc", [128, D], F32, kind="ExternalInput").ap()
    d_bbc = nc.dram_tensor("bbc", [128, D], F32, kind="ExternalInput").ap()
    d_bebc = nc.dram_tensor("bebc", [128, D], F32, kind="ExternalInput").ap()
    d_rbbc = nc.dram_tensor("rbbc", [128, 16], F32, kind="ExternalInput").ap()
    d_masks = nc.dram_tensor("masks", [NT, 2, 128, KC], BF16, kind="ExternalInput").ap()
    d_ident = nc.dram_tensor("identb", [128, 128], BF16, kind="ExternalInput").ap()
    d_i128f = nc.dram_tensor("i128f", [128, 128], F32, kind="ExternalInput").ap()
    d_e2f = nc.dram_tensor("e2f", [128, 128], F32, kind="ExternalInput").ap()
    d_onesf = nc.dram_tensor("onesf", [1, 128], F32, kind="ExternalInput").ap()
    d_out = nc.dram_tensor("out", [NT * 128, D], F32, kind="ExternalOutput").ap()

    d_xh = nc.dram_tensor("xh_d", [NT * 128, D], F32, kind="Internal").ap()
    d_ktsh = nc.dram_tensor("kt_sh", [2 * NT * 128, D // 2], BF16, kind="Internal").ap()
    d_vsh = nc.dram_tensor("v_sh", [2 * NT * 128, D // 2], BF16, kind="Internal").ap()
    d_ktag = nc.dram_tensor("kt_ag", [2 * NCORE * NT * 128, D // 2], BF16, kind="Internal", addr_space="Shared").ap()
    d_vag = nc.dram_tensor("v_ag", [2 * NCORE * NT * 128, D // 2], BF16, kind="Internal", addr_space="Shared").ap()
    d_ssT = nc.dram_tensor("s_sT", [128, 128], F32, kind="Internal").ap()
    d_wsh = nc.dram_tensor("w_sh", [4 * 256, D], BF16, kind="Internal").ap()
    d_wag = nc.dram_tensor("w_ag", [NCORE * 4 * 256, D], BF16, kind="Internal",
                           addr_space="Shared").ap()
    d_sagT = nc.dram_tensor("s_agT", [NCORE * 128, 128], F32, kind="Internal", addr_space="Shared").ap()

    stk = ExitStack()
    with stk:
        def sb(name, shape, dt):
            return stk.enter_context(nc.sbuf_tensor(name, shape, dt))
        xbuf = sb("xbuf", [128, NT * D], F32)
        identb = sb("identb_s", [128, 128], BF16)
        i128f = sb("i128f_s", [128, 128], F32)
        e2f = sb("e2f_s", [128, 128], F32)
        onesf = sb("onesf_s", [1, 128], F32)
        sm = sb("sm", [128, 256], F32)
        stg = sb("stg", [128, 4 * 512], BF16)
        wr_sb = sb("wr_sb", [128, A * 16], BF16)
        rbbc = sb("rbbc_s", [128, 16], F32)
        S_sb = sb("S_sb", [128, NT * 16], F32)
        smi = sb("smi", [128, 4], mybir.dt.int32)

        class SMAlloc:
            def __init__(self):
                self.n = 0

            def col(self, k=1):
                c = self.n
                self.n += k
                assert self.n <= 256, "sm overflow"
                return c
        smc = SMAlloc()

        pSC = stk.enter_context(nc.psum_tensor("pSC", [128, 1024], F32))
        pT = stk.enter_context(nc.psum_tensor("pT", [128, 2048], BF16))
        pAV = stk.enter_context(nc.psum_tensor("pAV", [128, 2048], F32))

        semh = {s: stk.enter_context(nc.semaphore(s)) for s in Sched.SEMS}

        def emit_ln(src_ap, waits, tag, dump_ap, sq_waits=()):
            c_s1 = smc.col(); c_s2 = smc.col(); c_m = smc.col()
            c_bm = smc.col(); c_std = smc.col(); c_r = smc.col()
            sch.add("V", lambda e: e.reduce_sum(sm[:, c_s1:c_s1 + 1], src_ap, axis=AX.X),
                    waits=waits, tag=f"{tag}_rs")
            pr_sq = sch.add("S", lambda e: e.activation(
                dump_ap, src_ap, AF.Square, accum_out=sm[:, c_s2:c_s2 + 1]),
                waits=list(waits) + list(sq_waits), inc="eS", tag=f"{tag}_sq")
            sch.add("V", lambda e: e.tensor_single_scalar(
                sm[:, c_m:c_m + 1], sm[:, c_s1:c_s1 + 1], 1.0 / D, ALU.mult),
                tag=f"{tag}_m")
            sch.add("V", lambda e: e.tensor_scalar(
                sm[:, c_bm:c_bm + 1], sm[:, c_m:c_m + 1], sm[:, c_m:c_m + 1], EPS,
                ALU.mult, op1=ALU.subtract), tag=f"{tag}_mm")
            pr_bm = sch.add("V", lambda e: e.tensor_single_scalar(
                sm[:, c_bm:c_bm + 1], sm[:, c_bm:c_bm + 1], -1.0, ALU.mult),
                inc="eV", tag=f"{tag}_bm")
            pr_std = sch.add("S", lambda e: e.activation(
                sm[:, c_std:c_std + 1], sm[:, c_s2:c_s2 + 1], AF.Sqrt,
                bias=sm[:, c_bm:c_bm + 1], scale=1.0 / D),
                waits=[pr_bm], inc="eS", tag=f"{tag}_sqrt")
            pr_r = sch.add("V", lambda e: e.reciprocal(
                sm[:, c_r:c_r + 1], sm[:, c_std:c_std + 1]),
                waits=[pr_std], inc="eV", tag=f"{tag}_recip")
            return c_m, c_r, pr_r, pr_sq

        XENG = {0: ("Y", "dX0"), 1: ("S", "dSP")}
        pr_xt = []
        for n in range(NT):
            qn, sem = XENG[n % 2]
            pr_xt.append(sch.add(qn, lambda e, n=n: e.dma_start(
                out=xbuf[:, n * D:(n + 1) * D],
                in_=d_x[n * 128:(n + 1) * 128, :]),
                inc=sem, amt=16, tag=f"ld_x{n}"))
        pr_ident = sch.add("Y", lambda e: e.dma_start(out=identb[:, :], in_=d_ident),
                           inc="dY0", amt=16, tag="ld_ident")
        pr_i128 = sch.add("Y", lambda e: e.dma_start(out=i128f[:, :], in_=d_i128f),
                          inc="dY0", amt=16, tag="ld_i128")
        pr_e2 = sch.add("Y", lambda e: e.dma_start(out=e2f[:, :], in_=d_e2f),
                        inc="dY0", amt=16, tag="ld_e2")
        pr_ones = sch.add("Y", lambda e: e.dma_start(out=onesf[:, :], in_=d_onesf),
                          inc="dY0", amt=16, tag="ld_ones")
        pr_rb = sch.add("Y", lambda e: e.dma_start(out=rbbc[:, :], in_=d_rbbc),
                        inc="dY0", amt=16, tag="ld_rb")
        pr_wr = sch.add("G", lambda e: e.dma_start(
            out=wr_sb[:, :], in_=d_wr.rearrange("(a p) e -> p a e", p=128)),
            inc="dwr", amt=16, tag="ld_wr")
        c_eps = smc.col()
        pr_epsc = sch.add("V", lambda e: e.memset(sm[:, c_eps:c_eps + 1], EPS),
                          inc="eV", tag="eps_const")

        WPH = {}
        WKPRE = {}

        def emit_wk_direct(qq, slot, extra):
            last = None
            for part in range(4):
                def ldwk(e, qq=qq, part=part, slot=slot):
                    srcw = d_wk32[part * 512:(part + 1) * 512,
                                  qq * 512:(qq + 1) * 512]
                    return e.dma_start(
                        out=WPH["wpool"][:, slot * 8192 + part * 2048:
                                         slot * 8192 + (part + 1) * 2048],
                        in_=srcw.rearrange("(a p) nn -> p a nn", p=128))
                last = sch.add("G", ldwk, waits=extra if part == 0 else [],
                               inc=f"dw{slot}", amt=16, tag=f"ldwk{qq}_{part}")
            return last

        pr_wcast = sch.add("G", lambda e: e.dma_start(
            out=d_wsh[:, :], in_=d_wshin[:, :]),
            inc="dws0", amt=16, tag="wcast")
        pr_agw = sch.add("G", lambda e: e.collective_compute(
            "AllGather", ALU.bypass, replica_groups=[list(range(NCORE))],
            ins=[d_wsh.opt()], outs=[d_wag.opt()]),
            waits=[pr_wcast], inc="cc", amt=1, tag="AG_w")
        WKPRE[0] = emit_wk_direct(0, 0, [])
        WKPRE[1] = emit_wk_direct(1, 1, [])

        def wag_quarter_load(w_idx, qq, dst_tile, dst_off, sem, extra_waits, tag):
            last = None
            for a1 in range(2):
                def ld(e, w_idx=w_idx, qq=qq, dst_tile=dst_tile, dst_off=dst_off,
                       a1=a1):
                    vv = d_wag.rearrange("(c w a1 p) d -> w a1 p c d", c=NCORE, w=4,
                                         a1=2)[w_idx, a1][:, :, qq * 512:(qq + 1) * 512]
                    ov = dst_tile[:, dst_off:dst_off + 8192].rearrange(
                        "p (c half nn) -> p c half nn", c=8, half=2)[:, :, a1, :]
                    return e.dma_start(out=ov, in_=vv)
                last = sch.add("Y", ld, waits=([pr_agw] + list(extra_waits))
                               if a1 == 0 else [], inc=sem, amt=16,
                               tag=f"{tag}_{a1}")
            return last

        with nc.sbuf_tensor("qT", [128, A * 1024], BF16) as qT:
            with (nc.sbuf_tensor("xnormT", [128, NT * D], BF16) as xnormT,
                  nc.sbuf_tensor("wpool", [128, 2 * A * 512], BF16) as wpool):
                WPH["wpool"] = wpool
                pr_spill = [None] * NT
                pr_xnT_evac = [None] * NT
                pr_xh = [None] * NT
                with (nc.sbuf_tensor("trash", [128, D], F32) as trash,
                      nc.sbuf_tensor("xnsc", [128, 2 * D], BF16) as xnsc,
                      nc.sbuf_tensor("gbc_s", [128, D], F32) as gbc,
                      nc.sbuf_tensor("bbc_s", [128, D], F32) as bbc):
                    pr_gbc = sch.add("Y", lambda e: e.dma_start(out=gbc[:, :], in_=d_gbc),
                                     inc="dY0", amt=16, tag="ld_gbc")
                    pr_bbc = sch.add("Y", lambda e: e.dma_start(out=bbc[:, :], in_=d_bbc),
                                     inc="dY0", amt=16, tag="ld_bbc")
                    pr_init_all = pr_bbc
                    pr_tb_last = [None, None]
                    pr_beta = [None] * NT
                    pr_tT = [None] * NT
                    pr_sqn = [None] * NT
                    b_s1 = smc.col(8); b_s2 = smc.col(32); b_v = smc.col(8)
                    b_m = smc.col(8); b_mm = smc.col(8); b_st = smc.col(8)
                    b_r1 = smc.col(8); b_v2 = smc.col(8); b_st2 = smc.col(8)
                    b_r2 = smc.col(8); b_sf = smc.col(8); b_a = smc.col(8)
                    b_b = smc.col(8)
                    for g in range(2):
                        c0 = 4 * g
                        for n in range(c0, c0 + 4):
                            xt = xbuf[:, n * D:(n + 1) * D]
                            sch.add("V", lambda e, xt=xt, n=n: e.reduce_sum(
                                sm[:, b_s1 + n:b_s1 + n + 1], xt, axis=AX.X),
                                waits=[pr_xt[n]], tag=f"ln1_{n}_rs")

                            def sq(e, xt=xt, n=n):
                                last = None
                                for q in range(4):
                                    last = e.activation(
                                        pAV[:, q * 512:(q + 1) * 512],
                                        xt[:, q * 512:(q + 1) * 512], AF.Square,
                                        accum_out=sm[:, b_s2 + 4 * n + q:
                                                     b_s2 + 4 * n + q + 1])
                                return last
                            pr_sqn[n] = sch.add("S", sq, waits=[pr_xt[n]], inc="eS",
                                                tag=f"ln1_{n}_sq")
                            sch.add("V", lambda e, n=n: e.reduce_sum(
                                sm[:, b_v + n:b_v + n + 1],
                                sm[:, b_s2 + 4 * n:b_s2 + 4 * n + 4], axis=AX.X),
                                waits=[pr_sqn[n]], tag=f"ln1_{n}_v00")
                        def s4(b, c0=c0):
                            return sm[:, b + c0:b + c0 + 4]
                        sch.add("V", lambda e, s4=s4: e.tensor_single_scalar(
                            s4(b_m), s4(b_s1), 1.0 / D, ALU.mult), tag=f"lnB{g}_m")
                        sch.add("V", lambda e, s4=s4: e.tensor_single_scalar(
                            s4(b_v), s4(b_v), 1.0 / D, ALU.mult), tag=f"lnB{g}_v0")
                        sch.add("V", lambda e, s4=s4: e.tensor_tensor(
                            s4(b_mm), s4(b_m), s4(b_m), ALU.mult), tag=f"lnB{g}_mm")
                        pr_v1g = sch.add("V", lambda e, s4=s4: e.tensor_tensor(
                            s4(b_v), s4(b_v), s4(b_mm), ALU.subtract),
                            inc="eV", tag=f"lnB{g}_v1")
                        pr_st1g = sch.add("S", lambda e, s4=s4: e.activation(
                            s4(b_st), s4(b_v), AF.Sqrt, bias=sm[:, c_eps:c_eps + 1]),
                            waits=[pr_v1g, pr_epsc], inc="eS", tag=f"lnB{g}_std")
                        sch.add("V", lambda e, s4=s4: e.reciprocal(s4(b_r1), s4(b_st)),
                                waits=[pr_st1g], tag=f"lnB{g}_r1")
                        sch.add("V", lambda e, s4=s4: e.tensor_tensor(
                            s4(b_v2), s4(b_r1), s4(b_r1), ALU.mult), tag=f"lnB{g}_q")
                        pr_v2g = sch.add("V", lambda e, s4=s4: e.tensor_tensor(
                            s4(b_v2), s4(b_v2), s4(b_v), ALU.mult),
                            inc="eV", tag=f"lnB{g}_v2")
                        pr_st2g = sch.add("S", lambda e, s4=s4: e.activation(
                            s4(b_st2), s4(b_v2), AF.Sqrt, bias=sm[:, c_eps:c_eps + 1]),
                            waits=[pr_v2g], inc="eS", tag=f"lnB{g}_std2")
                        sch.add("V", lambda e, s4=s4: e.reciprocal(s4(b_r2), s4(b_st2)),
                                waits=[pr_st2g], tag=f"lnB{g}_r2")
                        sch.add("V", lambda e, s4=s4: e.tensor_tensor(
                            s4(b_sf), s4(b_r1), s4(b_r2), ALU.mult), tag=f"lnB{g}_sf")
                        sch.add("V", lambda e, s4=s4: e.tensor_single_scalar(
                            s4(b_a), s4(b_r1), 1.0, ALU.add), tag=f"lnB{g}_a")
                        sch.add("V", lambda e, s4=s4: e.tensor_tensor(
                            s4(b_b), s4(b_m), s4(b_r1), ALU.mult), tag=f"lnB{g}_b")
                        for n in range(c0, c0 + 4):
                            xt = xbuf[:, n * D:(n + 1) * D]
                            par = n % 2
                            trs = trash[:, :]
                            xns = xnsc[:, par * D:(par + 1) * D]
                            pr_t1 = sch.add("V", lambda e, xt=xt, trs=trs, n=n:
                                e.scalar_tensor_tensor(
                                trs, xt, sm[:, b_m + n:b_m + n + 1],
                                gbc[:, :], ALU.subtract, ALU.mult),
                                waits=[pr_init_all], inc="eV", tag=f"t1_{n}")
                            pr_xh[n] = sch.add("V", lambda e, xt=xt, n=n:
                                e.tensor_scalar(xt, xt, sm[:, b_a + n:b_a + n + 1],
                                                sm[:, b_b + n:b_b + n + 1], ALU.mult,
                                                op1=ALU.subtract),
                                waits=[pr_sqn[n]], inc="eV", tag=f"xh_{n}")
                            pr_spill[n] = sch.add("S", lambda e, n=n, xt=xt: e.dma_start(
                                out=d_xh[n * 128:(n + 1) * 128, :], in_=xt),
                                waits=[pr_xh[n]], inc="dSP", amt=16, tag=f"spill_{n}")
                            bw = [pr_init_all]
                            if n >= 2:
                                bw.append(pr_tT[n - 2])
                            pr_beta[n] = sch.add("V", lambda e, trs=trs, xns=xns, n=n:
                                e.scalar_tensor_tensor(
                                xns, trs, sm[:, b_sf + n:b_sf + n + 1],
                                bbc[:, :], ALU.mult, ALU.add),
                                waits=bw, inc="eV", tag=f"xnorm_{n}")
                            for half in range(2):
                                def tb(e, half=half, xns=xns):
                                    last = None
                                    for a8 in range(8):
                                        a = half * 8 + a8
                                        last = e.transpose(
                                            pT[:, half * 1024 + a8 * 128:
                                               half * 1024 + (a8 + 1) * 128],
                                            xns[:, a * 128:(a + 1) * 128], identb[:, :])
                                    return last
                                wt = [pr_beta[n], pr_init_all]
                                if pr_tb_last[half] is not None:
                                    wt.append(pr_tb_last[half])
                                pr_t = sch.add("T", tb, waits=wt, inc="eT",
                                               tag=f"xnT_{n}_{half}")
                                pr_tb_last[half] = sch.add(
                                    "S", lambda e, n=n, half=half: e.activation(
                                        xnormT[:, n * D + half * 1024:
                                               n * D + (half + 1) * 1024],
                                        pT[:, half * 1024:(half + 1) * 1024], AF.Copy),
                                    waits=[pr_t], inc="eS", tag=f"xnTev_{n}_{half}")
                            pr_tT[n] = pr_t
                            pr_xnT_evac[n] = pr_tb_last[1]
                    p1_done_V = pr_tb_last[1]
                    p1_done_G = pr_beta[NT - 1]

                with nc.sbuf_tensor("wpoolB", [128, 2 * A * 512], BF16) as wpoolB:
                    def wsl(slot):
                        return wpoolB if slot >= 2 else wpool

                    def wsl_off(slot):
                        return (slot - 2) * 8192 if slot >= 2 else slot * 8192
                    pr_kwrite = []
                    pr_vwrite = []
                    pr_sc_freeS = [None, None]
                    pr_stg_free = [None] * 4
                    stg_i = [0]
                    par_i = [0]
                    first_stg = [True]

                    def produce_piece(tag, lhsT_fn, rhs_fn, evac_dst_fn, dma_fn, waits_mm):
                        par = par_i[0] % 2
                        par_i[0] += 1
                        psl = pSC[:, par * 512:(par + 1) * 512]
                        wt = list(waits_mm)
                        if pr_sc_freeS[par] is not None:
                            wt.append(pr_sc_freeS[par])

                        def mms(e, lhsT_fn=lhsT_fn, rhs_fn=rhs_fn, psl=psl):
                            last = None
                            for a in range(A):
                                last = e.matmul(psl, lhsT_fn(a), rhs_fn(a),
                                                start=(a == 0), stop=(a == A - 1))
                            return last
                        pr_mm = sch.add("T", mms, waits=wt, inc="eT", tag=f"{tag}_mm")
                        if dma_fn is not None:
                            sl = stg_i[0] % 4
                            stg_i[0] += 1
                            dst = stg[:, sl * 512:(sl + 1) * 512]
                            ew = [pr_mm]
                            if pr_stg_free[sl] is not None:
                                ew.append(pr_stg_free[sl])
                            pr_ev = sch.add("S", lambda e, dst=dst, psl=psl: e.activation(
                                dst, psl, AF.Copy), waits=ew, inc="eS", tag=f"{tag}_ev")
                            pr_sc_freeS[par] = pr_ev
                            pr_d = sch.add("S", lambda e, dst=dst, dma_fn=dma_fn:
                                           dma_fn(e, dst),
                                           waits=[pr_ev], inc=f"dstg{sl}", amt=16,
                                           tag=f"{tag}_dma")
                            pr_stg_free[sl] = pr_d
                            return pr_mm, pr_d
                        pr_ev = sch.add("S", lambda e, psl=psl, f=evac_dst_fn: e.activation(
                            f(), psl, AF.Copy), waits=[pr_mm], inc="eS", tag=f"{tag}_ev")
                        pr_sc_freeS[par] = pr_ev
                        return pr_mm, pr_ev

                    pr_slot_freeT = [None, None, None, None]
                    gseq = [(w, qq) for w in ("k", "v", "q") for qq in range(4)]
                    wmap = {"q": 0, "v": 1}
                    pr_agkt = None
                    first_wagB = [True, True]
                    wag_i = [0]
                    for wname, qq in gseq:
                        if wname == "k":
                            slot = qq % 2
                            if qq < 2:
                                pr_w = WKPRE[qq]
                            else:
                                wtld = []
                                if pr_slot_freeT[slot] is not None:
                                    wtld.append(pr_slot_freeT[slot])
                                pr_w = emit_wk_direct(qq, slot, wtld)
                        else:
                            slot = 2 + wag_i[0] % 2
                            wag_i[0] += 1
                            wtld = []
                            if pr_slot_freeT[slot] is not None:
                                wtld.append(pr_slot_freeT[slot])
                            if first_wagB[slot - 2]:
                                wtld += [p1_done_V, p1_done_G]
                                first_wagB[slot - 2] = False
                            pr_w = wag_quarter_load(wmap[wname], qq, wsl(slot),
                                                    wsl_off(slot), f"dw{slot}", wtld,
                                                    f"ldw_{wname}{qq}")
                        last_mm = None
                        if wname == "k":
                            for g in range(2):
                                for ao4 in range(4):
                                    a_out = qq * 4 + ao4
                                    def lhsT_fn(a, slot=slot, ao4=ao4):
                                        base = wsl_off(slot)
                                        return wsl(slot)[:, base + a * 512 + ao4 * 128:
                                                         base + a * 512 + (ao4 + 1) * 128]
                                    def rhs_fn(a, g=g):
                                        v = xnormT[:, :].rearrange("p (n c) -> p n c", n=NT)
                                        return v[:, 4 * g:4 * g + 4, a * 128:(a + 1) * 128]
                                    def dma_fn(e, src, a_out=a_out, g=g):
                                        hh = a_out // 8
                                        dstv = d_ktsh[hh * 1024:(hh + 1) * 1024, :]
                                        dstv = dstv.rearrange("(j p) c -> j p c", p=128)[
                                            4 * g:4 * g + 4, :,
                                            (a_out % 8) * 128:((a_out % 8) + 1) * 128]
                                        return e.dma_start(
                                            out=dstv.rearrange("j p t -> p j t"),
                                            in_=src.rearrange("p (j t) -> p j t", j=4))
                                    pr_mm, pr_d = produce_piece(
                                        f"K{a_out}_{g}", lhsT_fn, rhs_fn, None, dma_fn,
                                        [pr_w, pr_xnT_evac[4 * g + 3]])
                                    pr_kwrite.append(pr_d)
                                    last_mm = pr_mm
                        elif wname == "v":
                            for j in range(NT):
                                def lhsT_fn(a, j=j):
                                    return xnormT[:, j * D + a * 128:
                                                  j * D + (a + 1) * 128]
                                def rhs_fn(a, slot=slot):
                                    base = wsl_off(slot)
                                    return wsl(slot)[:, base + a * 512:
                                                     base + (a + 1) * 512]
                                def dma_fn(e, src, j=j, qq=qq):
                                    hh = qq // 2
                                    return e.dma_start(
                                        out=d_vsh[hh * 1024 + j * 128:
                                                  hh * 1024 + (j + 1) * 128,
                                                  (qq % 2) * 512:((qq % 2) + 1) * 512],
                                        in_=src)
                                pr_mm, pr_d = produce_piece(
                                    f"V{qq}_{j}", lhsT_fn, rhs_fn, None, dma_fn,
                                    [pr_w, pr_xnT_evac[j]])
                                pr_vwrite.append(pr_d)
                                last_mm = pr_mm
                        else:
                            for g in range(2):
                                for ao4 in range(4):
                                    a_out = qq * 4 + ao4
                                    def lhsT_fn(a, slot=slot, ao4=ao4):
                                        base = wsl_off(slot)
                                        return wsl(slot)[:, base + a * 512 + ao4 * 128:
                                                         base + a * 512 + (ao4 + 1) * 128]
                                    def rhs_fn(a, g=g):
                                        v = xnormT[:, :].rearrange("p (n c) -> p n c", n=NT)
                                        return v[:, 4 * g:4 * g + 4, a * 128:(a + 1) * 128]
                                    def evac_dst_fn(a_out=a_out, g=g):
                                        return qT[:, a_out * 1024 + g * 512:
                                                  a_out * 1024 + (g + 1) * 512]
                                    pr_mm, pr_ev = produce_piece(
                                        f"Q{a_out}_{g}", lhsT_fn, rhs_fn, evac_dst_fn,
                                        None, [pr_w, pr_xnT_evac[4 * g + 3]])
                                    last_mm = pr_mm
                        pr_slot_freeT[slot] = last_mm
                        if wname == "k" and qq in (1, 3):
                            hh = 0 if qq == 1 else 1
                            pr_agkt = sch.add("G", lambda e, hh=hh: e.collective_compute(
                                "AllGather", ALU.bypass,
                                replica_groups=[list(range(NCORE))],
                                ins=[d_ktsh[hh * 1024:(hh + 1) * 1024, :]],
                                outs=[d_ktag[hh * 8192:(hh + 1) * 8192, :]]),
                                waits=list(pr_kwrite[hh * 16:(hh + 1) * 16]),
                                inc="cc", amt=1, tag=f"AG_kt{hh}")
                        if wname == "v" and qq in (1, 3):
                            hh = 0 if qq == 1 else 1
                            pr_agv = sch.add("G", lambda e, hh=hh: e.collective_compute(
                                "AllGather", ALU.bypass,
                                replica_groups=[list(range(NCORE))],
                                ins=[d_vsh[hh * 1024:(hh + 1) * 1024, :]],
                                outs=[d_vag[hh * 8192:(hh + 1) * 8192, :]]),
                                waits=list(pr_vwrite[hh * 16:(hh + 1) * 16]),
                                inc="cc", amt=1, tag=f"AG_v{hh}")
                    pr_p3_T = [pr_slot_freeT[0], pr_slot_freeT[1]]
                    pr_p3_S = [pr_sc_freeS[0], pr_sc_freeS[1]]
                    pr_p3_stg = list(pr_stg_free)

            with (nc.sbuf_tensor("kbuf", [128, 2 * 4 * D], BF16) as kbuf,
                  nc.sbuf_tensor("vbuf", [128, 2 * 4 * D], BF16) as vbuf,
                  nc.sbuf_tensor("maskb", [128, NT * 2 * KC], BF16) as maskb,
                  nc.sbuf_tensor("pbuf", [128, 2 * KC], BF16) as pbuf,
                  nc.sbuf_tensor("ptile", [128, 2 * KC], BF16) as ptile):
                jobs = []
                for i in range(NKC):
                    for j in slots_for_chunk(i):
                        jobs.append((j, i))
                njobs = len(jobs)
                slot_jobidx = {j: [] for j in range(NT)}
                for idx, (j, i) in enumerate(jobs):
                    slot_jobidx[j].append(idx)

                c_mA = smc.col(NT); c_mB = smc.col(NT)
                c_srun = smc.col(NT)
                c_cm = smc.col(2); c_d = smc.col(2); c_nb = smc.col(2)
                c_cs = smc.col(2); c_fac = smc.col(2)
                c_rec = smc.col(NT); c_gate = smc.col(NT)

                region_guard = pr_p3_T + pr_p3_S + pr_p3_stg
                pr_mask = sch.add("Y", lambda e: e.dma_start(
                    out=maskb[:, :].rearrange("p (j i t) -> p j i t", j=NT, i=2),
                    in_=d_masks.rearrange("j i p t -> p j i t")),
                    waits=region_guard, inc="dYm", amt=16, tag="ld_masks")
                pr_minit = sch.add("V", lambda e: e.memset(
                    sm[:, c_mA:c_mA + 2 * NT], NEG), inc="eV", tag="m_init")
                pr_sinit = sch.add("V", lambda e: e.memset(
                    sm[:, c_srun:c_srun + NT], 0.0), inc="eV", tag="s_init")

                pr_kv = [None] * NKC
                pr_ktload = [None] * NKC
                pr_scores = [None] * njobs
                pr_transp = [None] * njobs
                pr_attnv = [None] * njobs
                pr_nb = [None] * njobs
                pr_evac = [None] * njobs
                pr_aostt = [None] * njobs
                pr_av_dv = [None] * njobs
                pr_ao_dv = [None] * njobs
                pr_srunstt = [None] * njobs
                pr_factor = [None] * njobs
                pr_probs = [None] * njobs
                PR_CHUNK_DONE = {}

                def emit_kv_load(i):
                    par = i % 2
                    wt = [pr_agkt]
                    if i == 0:
                        wt.append(pr_mask)
                    if i >= 2:
                        wt.append(PR_CHUNK_DONE[i - 2])
                    lastk = None
                    lastv = None
                    rows = []
                    for s in range(4):
                        tau = 4 * i + s
                        jt = tau // 8
                        r = (tau % 8 - jt) % 8
                        rows.append(r * (NT * 128) + jt * 128)
                    for s in range(4):
                        for hh in range(2):
                            def ldk(e, row0=rows[s], par=par, s=s, hh=hh):
                                return e.dma_start(
                                    out=kbuf[:, par * 4 * D + s * D + hh * 1024:
                                             par * 4 * D + s * D + (hh + 1) * 1024],
                                    in_=d_ktag[hh * 8192 + row0:hh * 8192 + row0 + 128, :])
                            lastk = sch.add("Y", ldk,
                                            waits=wt if (s == 0 and hh == 0) else [],
                                            inc=f"dK{i % 2}", amt=16,
                                            tag=f"ldkt_{i}_{s}_{hh}")
                    wtv = [pr_agv] if i < 2 else [PR_CHUNK_DONE[i - 2]]
                    for s in range(4):
                        for hh in range(2):
                            def ldv(e, row0=rows[s], par=par, s=s, hh=hh):
                                return e.dma_start(
                                    out=vbuf[:, par * 4 * D + s * D + hh * 1024:
                                             par * 4 * D + s * D + (hh + 1) * 1024],
                                    in_=d_vag[hh * 8192 + row0:hh * 8192 + row0 + 128, :])
                            lastv = sch.add("S", ldv,
                                            waits=wtv if (s == 0 and hh == 0) else [],
                                            inc=f"dV{i % 2}", amt=16,
                                            tag=f"ldv_{i}_{s}_{hh}")
                    pr_ktload[i] = lastk
                    return lastv

                def emit_scores(n):
                    j, i = jobs[n]
                    par = n % 2
                    psl = pSC[:, par * 512:(par + 1) * 512]
                    masked = (i >= 2 * j)
                    wt = [pr_ktload[i]]
                    if n >= 2:
                        wt += [pr_nb[n - 2], pr_probs[n - 2]]
                    else:
                        wt += [pr_minit, pr_sinit]
                    if n == 0:
                        wt.append(pr_spill[NT - 1])

                    def mms(e, j=j, i=i, psl=psl, masked=masked):
                        kv = kbuf[:, (i % 2) * 4 * D:((i % 2) + 1) * 4 * D].rearrange(
                            "p (s c) -> p s c", s=4)
                        last = None
                        for a in range(A):
                            last = e.matmul(
                                psl, qT[:, a * 1024 + j * 128:a * 1024 + (j + 1) * 128],
                                kv[:, :, a * 128:(a + 1) * 128],
                                start=(a == 0), stop=(a == A - 1 and not masked))
                        if masked:
                            ii = i - 2 * j
                            last = e.matmul(
                                psl, identb[:, :],
                                maskb[:, (j * 2 + ii) * 512:(j * 2 + ii + 1) * 512],
                                start=False, stop=True)
                        return last
                    pr_scores[n] = sch.add("T", mms, waits=wt, inc="eT", tag=f"sc_{n}")

                def emit_transp(n):
                    par = n % 2
                    wt = [pr_probs[n]]
                    if n >= 2:
                        wt.append(pr_evac[n - 2])

                    def tr(e, par=par):
                        last = None
                        for s in range(4):
                            last = e.transpose(
                                pT[:, par * 1024 + s * 128:par * 1024 + (s + 1) * 128],
                                pbuf[:, par * 512 + s * 128:par * 512 + (s + 1) * 128],
                                identb[:, :])
                        return last
                    pr_transp[n] = sch.add("T", tr, waits=wt, inc="eT", tag=f"tr_{n}")

                def emit_attnv(n):
                    j, i = jobs[n]
                    par = n % 2
                    proms = []
                    for dv in range(4):
                        wt = [pr_evac[n], pr_kv[i]] if dv == 0 else []
                        if n >= 1:
                            wt.append(pr_ao_dv[n - 1][dv])

                        def av(e, n=n, i=i, dv=dv, par=par):
                            last = None
                            for s in range(4):
                                last = e.matmul(
                                    pAV[:, dv * 512:(dv + 1) * 512],
                                    ptile[:, par * 512 + s * 128:
                                          par * 512 + (s + 1) * 128],
                                    vbuf[:, (i % 2) * 4 * D + s * D + dv * 512:
                                         (i % 2) * 4 * D + s * D + (dv + 1) * 512],
                                    start=(s == 0), stop=(s == 3))
                            return last
                        proms.append(sch.add("T", av, waits=wt, inc="eT",
                                             tag=f"av_{n}_{dv}"))
                    pr_av_dv[n] = proms
                    pr_attnv[n] = proms[3]

                def emit_vopsA(n):
                    j, i = jobs[n]
                    par = n % 2
                    k_in_slot = slot_jobidx[j].index(n)
                    mo = c_mA + j if (k_in_slot % 2 == 0) else c_mB + j
                    mn = c_mB + j if (k_in_slot % 2 == 0) else c_mA + j
                    psl = pSC[:, par * 512:(par + 1) * 512]
                    sch.add("V", lambda e, psl=psl, par=par: e.reduce_max(
                        sm[:, c_cm + par:c_cm + par + 1], psl, axis=AX.X),
                        waits=[pr_scores[n]], tag=f"cmax_{n}")
                    sch.add("V", lambda e, mo=mo, mn=mn, par=par: e.tensor_tensor(
                        sm[:, mn:mn + 1], sm[:, mo:mo + 1],
                        sm[:, c_cm + par:c_cm + par + 1], ALU.max), tag=f"mnew_{n}")
                    sch.add("V", lambda e, mo=mo, mn=mn, par=par: e.tensor_tensor(
                        sm[:, c_d + par:c_d + par + 1], sm[:, mo:mo + 1],
                        sm[:, mn:mn + 1], ALU.subtract), tag=f"d_{n}")
                    pr_nb[n] = sch.add("V", lambda e, mn=mn, par=par: e.tensor_single_scalar(
                        sm[:, c_nb + par:c_nb + par + 1], sm[:, mn:mn + 1], -SCL,
                        ALU.mult), inc="eV", tag=f"nb_{n}")

                def emit_vopsB(n):
                    j, i = jobs[n]
                    par = n % 2
                    pr_srunstt[n] = sch.add("V", lambda e, j=j, par=par:
                        e.scalar_tensor_tensor(
                            sm[:, c_srun + j:c_srun + j + 1],
                            sm[:, c_srun + j:c_srun + j + 1],
                            sm[:, c_fac + par:c_fac + par + 1],
                            sm[:, c_cs + par:c_cs + par + 1], ALU.mult, ALU.add),
                        waits=[pr_probs[n], pr_factor[n]], inc="eV", tag=f"srun_{n}")
                    pr_evac[n] = sch.add("V", lambda e, par=par: e.tensor_copy(
                        ptile[:, par * 512:(par + 1) * 512],
                        pT[:, par * 1024:par * 1024 + 512]),
                        waits=[pr_transp[n]], inc="eV", tag=f"pev_{n}")

                def emit_vopsC(n):
                    j, i = jobs[n]
                    par = n % 2
                    proms = []
                    for dv in range(4):
                        proms.append(sch.add("V", lambda e, j=j, par=par, dv=dv:
                            e.scalar_tensor_tensor(
                                xbuf[:, j * D + dv * 512:j * D + (dv + 1) * 512],
                                xbuf[:, j * D + dv * 512:j * D + (dv + 1) * 512],
                                sm[:, c_fac + par:c_fac + par + 1],
                                pAV[:, dv * 512:(dv + 1) * 512],
                                ALU.mult, ALU.add),
                            waits=[pr_av_dv[n][dv]], inc="eV", tag=f"aostt_{n}_{dv}"))
                    pr_ao_dv[n] = proms
                    pr_aostt[n] = proms[3]

                def emit_sops(n):
                    par = n % 2
                    wt = [pr_nb[n]]
                    if n >= 2:
                        wt += [pr_aostt[n - 2], pr_srunstt[n - 2]]
                    pr_factor[n] = sch.add("S", lambda e, par=par: e.activation(
                        sm[:, c_fac + par:c_fac + par + 1],
                        sm[:, c_d + par:c_d + par + 1], AF.Exp, scale=SCL),
                        waits=wt, inc="eS", tag=f"fac_{n}")
                    wtp = [pr_scores[n]]
                    if n >= 2:
                        wtp.append(pr_transp[n - 2])
                    psl = pSC[:, par * 512:(par + 1) * 512]
                    pr_probs[n] = sch.add("S", lambda e, psl=psl, par=par: e.activation(
                        pbuf[:, par * 512:(par + 1) * 512], psl, AF.Exp,
                        bias=sm[:, c_nb + par:c_nb + par + 1], scale=SCL,
                        accum_out=sm[:, c_cs + par:c_cs + par + 1]),
                        waits=wtp, inc="eS", tag=f"probs_{n}")

                cur_chunk = -1
                for n in range(njobs):
                    j, i = jobs[n]
                    if i != cur_chunk:
                        cur_chunk = i
                        pr_kv[i] = emit_kv_load(i)
                    if n == 0:
                        emit_scores(0)
                    emit_vopsA(n)
                    emit_sops(n)
                    if n + 1 < njobs:
                        jj, ii = jobs[n + 1]
                        if ii != cur_chunk:
                            pr_kv[ii] = emit_kv_load(ii)
                            cur_chunk = ii
                        emit_scores(n + 1)
                    emit_transp(n)
                    emit_vopsB(n)
                    emit_attnv(n)
                    emit_vopsC(n)
                    PR_CHUNK_DONE[i] = pr_attnv[n]
                pr_attn_done = pr_aostt[njobs - 1]

            with (nc.sbuf_tensor("aon", [128, 2 * D], BF16) as aon,
                  nc.sbuf_tensor("aot", [128, NT * D], BF16) as aot,
                  nc.sbuf_tensor("wpool2", [128, 2 * A * 512], BF16) as wpool2,
                  nc.sbuf_tensor("xhb", [128, 4 * 512], F32) as xhb):
                pr_aotev = [None] * NT
                pr_tb2 = [None, None]
                pr_transp_aot = [None] * NT
                for j in range(NT):
                    last_of_slot = pr_aostt[slot_jobidx[j][-1]]
                    sch.add("V", lambda e, j=j: e.reciprocal(
                        sm[:, c_rec + j:c_rec + j + 1],
                        sm[:, c_srun + j:c_srun + j + 1]),
                        waits=[last_of_slot], tag=f"rec_{j}")
                    wt = [last_of_slot]
                    if j >= 2:
                        wt.append(pr_transp_aot[j - 2])
                    pr_aonj = sch.add("V", lambda e, j=j: e.tensor_single_scalar(
                        aon[:, (j % 2) * D:((j % 2) + 1) * D],
                        xbuf[:, j * D:(j + 1) * D],
                        sm[:, c_rec + j:c_rec + j + 1], ALU.mult),
                        waits=wt, inc="eV", tag=f"aon_{j}")
                    prt = None
                    for half in range(2):
                        wt2 = [pr_aonj]
                        if pr_tb2[half] is not None:
                            wt2.append(pr_tb2[half])

                        def tb2(e, j=j, half=half):
                            last = None
                            for a8 in range(8):
                                a = half * 8 + a8
                                last = e.transpose(
                                    pT[:, half * 1024 + a8 * 128:
                                       half * 1024 + (a8 + 1) * 128],
                                    aon[:, (j % 2) * D + a * 128:
                                        (j % 2) * D + (a + 1) * 128],
                                    identb[:, :])
                            return last
                        prt = sch.add("T", tb2, waits=wt2, inc="eT",
                                      tag=f"aotT_{j}_{half}")
                        pr_tb2[half] = sch.add("V", lambda e, j=j, half=half: e.tensor_copy(
                            aot[:, j * D + half * 1024:j * D + (half + 1) * 1024],
                            pT[:, half * 1024:(half + 1) * 1024]),
                            waits=[prt], inc="eV", tag=f"aotEv_{j}_{half}")
                    pr_transp_aot[j] = prt
                    pr_aotev[j] = pr_tb2[1]

                pr_wo = [None] * 4
                last_mm_of_dq = [None] * 4

                def emit_wo_load(qq, extra_wait):
                    slot = qq % 2
                    pr_wo[qq] = wag_quarter_load(2, qq, wpool2, slot * 8192,
                                                 f"dw{2 + slot}", extra_wait,
                                                 f"ldwo{qq}")

                emit_wo_load(0, [pr_attn_done])
                emit_wo_load(1, [])
                pr_epi5 = {}
                pc5 = 0
                for dq in range(4):
                    for j in range(NT):
                        k = pc5
                        pc5 += 1
                        bank = k % 4
                        xsl = k % 4
                        wtx = []
                        if k >= 4:
                            wtx.append(pr_epi5[k - 4])
                        if k == 0:
                            wtx.append(pr_attn_done)
                        pr_xh_ld = sch.add("Y", lambda e, j=j, dq=dq, xsl=xsl: e.dma_start(
                            out=xhb[:, xsl * 512:(xsl + 1) * 512],
                            in_=d_xh[j * 128:(j + 1) * 128, dq * 512:(dq + 1) * 512]),
                            waits=wtx, inc=f"dX{k % 4}", amt=16, tag=f"xhld_{k}")
                        wtm = [pr_wo[dq], pr_aotev[j]]
                        if k >= 4:
                            wtm.append(pr_epi5[k - 4])

                        def mmo(e, j=j, dq=dq, bank=bank):
                            slot = dq % 2
                            last = None
                            for a in range(A):
                                last = e.matmul(
                                    pAV[:, bank * 512:(bank + 1) * 512],
                                    aot[:, j * D + a * 128:j * D + (a + 1) * 128],
                                    wpool2[:, slot * 8192 + a * 512:
                                           slot * 8192 + (a + 1) * 512],
                                    start=(a == 0), stop=(a == A - 1))
                            return last
                        pr_mm = sch.add("T", mmo, waits=wtm, inc="eT", tag=f"opro_{k}")
                        last_mm_of_dq[dq] = pr_mm
                        pr_epi5[k] = sch.add("V", lambda e, j=j, dq=dq, bank=bank,
                                             xsl=xsl: e.tensor_tensor(
                            xbuf[:, j * D + dq * 512:j * D + (dq + 1) * 512],
                            xhb[:, xsl * 512:(xsl + 1) * 512],
                            pAV[:, bank * 512:(bank + 1) * 512], ALU.add),
                            waits=[pr_mm, pr_xh_ld], inc="eV", tag=f"epi5_{k}")
                    if dq + 2 <= 3:
                        emit_wo_load(dq + 2, [last_mm_of_dq[dq]])
                pr_x1_all = pr_epi5[pc5 - 1]

        with (nc.sbuf_tensor("g2T", [128, A * 1024], BF16) as g2T,
              nc.sbuf_tensor("g2sc", [128, 2 * D], BF16) as g2sc,
              nc.sbuf_tensor("Sbis", [128, 1024], F32) as Sbis,
              nc.sbuf_tensor("bmask", [128, 1024], BF16) as bmask,
              nc.sbuf_tensor("wpool3", [128, 2 * A * 512], BF16) as wpool3,
              nc.sbuf_tensor("bebc_s", [128, D], F32) as bebc,
              nc.sbuf_tensor("ebuf", [128, 20 * 512], F32) as ebuf,
              nc.sbuf_tensor("outb", [128, 4 * 512], F32) as outb,
              nc.sbuf_tensor("sstg", [16, 256], F32) as sstg,
              nc.sbuf_tensor("thbc", [128, 16], F32) as thbc,
              nc.sbuf_tensor("throw", [1, 16], F32) as throw,
              nc.sbuf_tensor("tmp16", [128, 16], F32) as tmp16):
            c_rmax = smc.col(2); c_nrm = smc.col(2); c_ssum = smc.col(2); c_sr = smc.col(2)
            c_lo = smc.col(); c_hi = smc.col(); c_mid = smc.col()
            c_cnt = smc.col(); c_cond = smc.col(); c_d1 = smc.col(); c_d2 = smc.col()
            c_red = smc.col()

            pr_bebc = sch.add("Y", lambda e: e.dma_start(out=bebc[:, :], in_=d_bebc),
                              waits=[pr_x1_all], inc="dbe", amt=16, tag="ld_bebc")

            pr_g2ev = [None] * NT
            pr_Sfin = [None] * NT
            pr_sT_dma = [None] * NT
            pr_tb3 = [None, None]
            pr_scread = [None, None]
            pr_sstg_free = [None, None]
            pr_we = [None] * 4

            def emit_we_load(qq, extra_wait):
                slot = qq % 2
                pr_we[qq] = wag_quarter_load(3, qq, wpool3, slot * 8192,
                                             f"dw{2 + slot}", extra_wait, f"ldwe{qq}")

            emit_we_load(0, [pr_x1_all])
            emit_we_load(1, [])
            pr_out = []
            pr_mmk = {}
            pr_e1k = {}
            pr_ock = {}
            pr_dmak = {}
            last_mm_of_dq7 = [None] * 4
            piece_list = [(dq, j) for dq in range(4) for j in range(NT)]

            def emit_piece_mm(k):
                dq, j = piece_list[k]
                bank = k % 4
                wtm = [pr_we[dq], pr_g2ev[j]]
                if k >= 4:
                    wtm.append(pr_e1k[k - 4])

                def mme(e, j=j, dq=dq, bank=bank):
                    slot = dq % 2
                    last = None
                    for a in range(A):
                        last = e.matmul(
                            pAV[:, bank * 512:(bank + 1) * 512],
                            g2T[:, a * 1024 + j * 128:a * 1024 + (j + 1) * 128],
                            wpool3[:, slot * 8192 + a * 512:
                                   slot * 8192 + (a + 1) * 512],
                            start=(a == 0), stop=(a == A - 1))
                    return last
                pr_mmk[k] = sch.add("T", mme, waits=wtm, inc="eT", tag=f"exp_{k}")
                last_mm_of_dq7[dq] = pr_mmk[k]

            def emit_piece_e1(k):
                dq, j = piece_list[k]
                bank = k % 4
                esl = k % 20
                wte = [pr_mmk[k], pr_bebc]
                if k >= 20:
                    wte.append(pr_ock[k - 20])

                def e1(e, dq=dq, bank=bank, esl=esl):
                    return e.tensor_tensor(
                        ebuf[:, esl * 512:(esl + 1) * 512],
                        pAV[:, bank * 512:(bank + 1) * 512],
                        bebc[:, dq * 512:(dq + 1) * 512], ALU.add)
                pr_e1k[k] = sch.add("V", e1, waits=wte, inc="eV", tag=f"e1_{k}")

            def emit_piece_oc(k):
                dq, j = piece_list[k]
                esl = k % 20
                osl = k % 4
                wt = [pr_e1k[k], gate_done[0]]
                if k >= 4:
                    wt.append(pr_dmak[k - 4])
                pr_ock[k] = sch.add("V", lambda e, j=j, dq=dq, esl=esl, osl=osl:
                    e.scalar_tensor_tensor(
                        outb[:, osl * 512:(osl + 1) * 512],
                        ebuf[:, esl * 512:(esl + 1) * 512],
                        sm[:, c_gate + j:c_gate + j + 1],
                        xbuf[:, j * D + dq * 512:j * D + (dq + 1) * 512],
                        ALU.mult, ALU.add), waits=wt, inc="eV", tag=f"oc_{k}")
                osem = f"dX{osl}" if osl % 2 == 0 else f"dss{osl // 2}"
                pr_dmak[k] = sch.add("Y" if osl % 2 == 0 else "S",
                    lambda e, j=j, dq=dq, osl=osl: e.dma_start(
                    out=d_out[j * 128:(j + 1) * 128, dq * 512:(dq + 1) * 512],
                    in_=outb[:, osl * 512:(osl + 1) * 512]),
                    waits=[pr_ock[k]], inc=osem, amt=16, tag=f"odma_{k}")
                pr_out.append(pr_dmak[k])

            pr_g2done = [None] * NT
            for j in range(NT):
                x1t = xbuf[:, j * D:(j + 1) * D]
                m3, r3, pr_r3, pr_sq3 = emit_ln(x1t, [pr_x1_all], f"ln3_{j}", outb[:, :])
                wt = [pr_r3]
                if pr_tb3[1] is not None:
                    wt.append(pr_tb3[1])
                pr_g2 = sch.add("V", lambda e, j=j, m3=m3, r3=r3: e.tensor_scalar(
                    g2sc[:, (j % 2) * D:((j % 2) + 1) * D], xbuf[:, j * D:(j + 1) * D],
                    sm[:, m3:m3 + 1], sm[:, r3:r3 + 1], ALU.subtract, op1=ALU.mult),
                    waits=wt, inc="eV", tag=f"g2_{j}")
                for half in range(2):
                    wt2 = [pr_g2]
                    if pr_tb3[half] is not None:
                        wt2.append(pr_tb3[half])

                    def tb3(e, j=j, half=half):
                        last = None
                        for a8 in range(8):
                            a = half * 8 + a8
                            last = e.transpose(
                                pT[:, half * 1024 + a8 * 128:
                                   half * 1024 + (a8 + 1) * 128],
                                g2sc[:, (j % 2) * D + a * 128:
                                     (j % 2) * D + (a + 1) * 128],
                                identb[:, :])
                        return last
                    prt = sch.add("T", tb3, waits=wt2, inc="eT", tag=f"g2T_{j}_{half}")

                    def ev3(e, j=j, half=half):
                        v = g2T[:, :].rearrange("p (a jj t) -> p a jj t", a=A, jj=NT)
                        return e.tensor_copy(
                            v[:, half * 8:(half + 1) * 8, j, :],
                            pT[:, half * 1024:(half + 1) * 1024].rearrange(
                                "p (a8 t) -> p a8 t", a8=8))
                    pr_tb3[half] = sch.add("V", ev3, waits=[prt], inc="eV",
                                           tag=f"g2Tev_{j}_{half}")
                pr_g2ev[j] = pr_tb3[1]
                emit_piece_mm(j)
                emit_piece_e1(j)
                if j >= 4:
                    emit_piece_mm(8 + j - 4)
                if j == 7:
                    emit_we_load(2, [pr_mmk[7]])


            for j in range(NT):
                par = j % 2
                wtr = [pr_g2ev[j], pr_wr]
                if pr_scread[par] is not None:
                    wtr.append(pr_scread[par])

                def mml(e, j=j, par=par):
                    last = None
                    for a in range(A):
                        last = e.matmul(
                            pSC[:, par * 512:par * 512 + 16],
                            g2T[:, a * 1024 + j * 128:a * 1024 + (j + 1) * 128],
                            wr_sb[:, a * 16:(a + 1) * 16],
                            start=(a == 0), stop=(a == A - 1))
                    return last
                pr_lg = sch.add("T", mml, waits=wtr, inc="eT", tag=f"logits_{j}")
                sch.add("V", lambda e, j=j, par=par: e.tensor_tensor(
                    S_sb[:, j * 16:(j + 1) * 16], pSC[:, par * 512:par * 512 + 16],
                    rbbc[:, :], ALU.add), waits=[pr_lg, pr_init_all], tag=f"spre_{j}")
                sch.add("V", lambda e, j=j, par=par: e.reduce_max(
                    sm[:, c_rmax + par:c_rmax + par + 1],
                    S_sb[:, j * 16:(j + 1) * 16], axis=AX.X), tag=f"rmax_{j}")
                pr_nrm = sch.add("V", lambda e, par=par: e.tensor_single_scalar(
                    sm[:, c_nrm + par:c_nrm + par + 1],
                    sm[:, c_rmax + par:c_rmax + par + 1], -1.0, ALU.mult),
                    inc="eV", tag=f"nrm_{j}")
                pr_sexp = sch.add("S", lambda e, j=j, par=par: e.activation(
                    S_sb[:, j * 16:(j + 1) * 16], S_sb[:, j * 16:(j + 1) * 16], AF.Exp,
                    bias=sm[:, c_nrm + par:c_nrm + par + 1],
                    accum_out=sm[:, c_ssum + par:c_ssum + par + 1]),
                    waits=[pr_nrm], inc="eS", tag=f"sexp_{j}")
                sch.add("V", lambda e, par=par: e.reciprocal(
                    sm[:, c_sr + par:c_sr + par + 1],
                    sm[:, c_ssum + par:c_ssum + par + 1]),
                    waits=[pr_sexp], tag=f"srec_{j}")
                pr_Sfin[j] = sch.add("V", lambda e, j=j, par=par: e.tensor_single_scalar(
                    S_sb[:, j * 16:(j + 1) * 16], S_sb[:, j * 16:(j + 1) * 16],
                    sm[:, c_sr + par:c_sr + par + 1], ALU.mult),
                    inc="eV", tag=f"sfin_{j}")
                pr_st = sch.add("T", lambda e, j=j, par=par: e.matmul(
                    pSC[0:16, par * 512 + 128:par * 512 + 256],
                    S_sb[:, j * 16:(j + 1) * 16], i128f[:, :], start=True, stop=True),
                    waits=[pr_Sfin[j], pr_init_all], inc="eT", tag=f"sT_{j}")
                wts = [pr_st]
                if pr_sstg_free[par] is not None:
                    wts.append(pr_sstg_free[par])
                pr_stev = sch.add("V", lambda e, par=par: e.tensor_copy(
                    sstg[0:16, par * 128:(par + 1) * 128],
                    pSC[0:16, par * 512 + 128:par * 512 + 256]),
                    waits=wts, inc="eV", tag=f"sTev_{j}")
                pr_scread[par] = pr_stev
                pr_sT_dma[j] = sch.add("S", lambda e, j=j, par=par: e.dma_start(
                    out=d_ssT[j * 16:(j + 1) * 16, :],
                    in_=sstg[0:16, par * 128:(par + 1) * 128]),
                    waits=[pr_stev], inc=f"dss{par}", amt=16, tag=f"sTdma_{j}")
                pr_sstg_free[par] = pr_sT_dma[j]

            pr_ags = sch.add("G", lambda e: e.collective_compute(
                "AllGather", ALU.bypass, replica_groups=[list(range(NCORE))],
                ins=[d_ssT.opt()], outs=[d_sagT.opt()]),
                waits=[pr_sT_dma[j] for j in range(NT)], inc="cc", amt=1, tag="AG_s")
            pr_sbis = sch.add("Y", lambda e: e.dma_start(
                out=Sbis[:, :].rearrange("p (ri t) -> p ri t", ri=NCORE),
                in_=d_sagT.rearrange("(ri p) t -> p ri t", p=128)),
                waits=[pr_ags], inc="dYm", amt=16, tag="ld_sbis")

            pr_init_lo = sch.add("V", lambda e: e.memset(sm[:, c_lo:c_lo + 1], 0.0),
                                 inc="eV", tag="bis_lo")

            pr_red = [None] * BISECT_ITERS
            pr_cnt = [None] * BISECT_ITERS
            pr_ccV = [None] * BISECT_ITERS
            bis_k = [0]
            gate_done = [None]

            def emit_one_bisect():
                k = bis_k[0]
                if k >= BISECT_ITERS:
                    return
                bis_k[0] += 1
                delta = 0.5 ** (k + 1)
                wt = ([pr_sbis, pr_init_lo] if k == 0 else [pr_ccV[k - 1]])
                sch.add("V", lambda e, delta=delta: e.tensor_single_scalar(
                    sm[:, c_mid:c_mid + 1], sm[:, c_lo:c_lo + 1], delta, ALU.add),
                    waits=wt, tag=f"bth_{k}")
                pr_red[k] = sch.add("V", lambda e: e.tensor_scalar(
                    bmask[:, :], Sbis[:, :], sm[:, c_mid:c_mid + 1], 0.0,
                    ALU.is_ge, op1=ALU.add, accum_out=sm[:, c_red:c_red + 1]),
                    inc="eV", tag=f"bred_{k}")
                wtc = [pr_red[k], pr_init_all]
                if k >= 2:
                    wtc.append(pr_ccV[k - 2])

                def cnt(e, k=k):
                    par = k % 2
                    return e.matmul(pSC[:, par * 512 + 256:par * 512 + 257],
                                    e2f[:, :], sm[:, c_red:c_red + 1],
                                    start=True, stop=True)
                pr_cnt[k] = sch.add("T", cnt, waits=wtc, inc="eT", tag=f"bcnt_{k}")
                par = k % 2
                sch.add("V", lambda e, par=par: e.tensor_single_scalar(
                    sm[:, c_cond:c_cond + 1],
                    pSC[:, par * 512 + 256:par * 512 + 257], 512.0, ALU.is_ge),
                    waits=[pr_cnt[k]], tag=f"bc0_{k}")
                pr_ccV[k] = sch.add("V", lambda e, delta=delta: e.scalar_tensor_tensor(
                    sm[:, c_lo:c_lo + 1], sm[:, c_cond:c_cond + 1], delta,
                    sm[:, c_lo:c_lo + 1], ALU.mult, ALU.add),
                    inc="eV", tag=f"bc1_{k}")

            def emit_gate():
                pr_th1 = sch.add("T", lambda e: e.matmul(
                    pSC[0:1, 0:16], sm[0:16, c_lo:c_lo + 1], i128f[0:16, 0:16],
                    start=True, stop=True),
                    waits=[pr_ccV[BISECT_ITERS - 1], pr_init_all], inc="eT", tag="th_row")
                pr_thev = sch.add("V", lambda e: e.tensor_copy(
                    throw[0:1, :], pSC[0:1, 0:16]), waits=[pr_th1], inc="eV",
                    tag="th_ev")
                pr_th2 = sch.add("T", lambda e: e.matmul(
                    pSC[:, 16:32], onesf[0:1, :], throw[0:1, 0:16],
                    start=True, stop=True), waits=[pr_thev, pr_init_all], inc="eT",
                    tag="th_bc")
                pr_thbc = sch.add("V", lambda e: e.tensor_copy(
                    thbc[:, :], pSC[:, 16:32]), waits=[pr_th2], inc="eV",
                    tag="thbc_ev")
                last = None
                for j in range(NT):
                    sch.add("V", lambda e, j=j: e.tensor_tensor(
                        tmp16[:, :], S_sb[:, j * 16:(j + 1) * 16], thbc[:, :],
                        ALU.is_ge), waits=[pr_thbc], tag=f"gm_{j}")
                    sch.add("V", lambda e, j=j: e.tensor_tensor(
                        tmp16[:, :], S_sb[:, j * 16:(j + 1) * 16], tmp16[:, :],
                        ALU.mult), tag=f"gs_{j}")
                    last = sch.add("V", lambda e, j=j: e.reduce_sum(
                        sm[:, c_gate + j:c_gate + j + 1], tmp16[:, :], axis=AX.X),
                        inc="eV", tag=f"gate_{j}")
                gate_done[0] = last


            for k in range(8, 12):
                emit_piece_e1(k)
            for k in range(12, 16):
                emit_piece_mm(k)
                emit_piece_e1(k)
            emit_we_load(3, [pr_mmk[15]])
            inter = {4: 16, 10: 17, 16: 18, 22: 19}
            while bis_k[0] < BISECT_ITERS:
                emit_one_bisect()
                k2 = inter.get(bis_k[0])
                if k2 is not None:
                    emit_piece_mm(k2)
                    emit_piece_e1(k2)
            emit_gate()
            for k in range(20):
                emit_piece_oc(k)
            for k in range(20, 32):
                emit_piece_mm(k)
                emit_piece_e1(k)
                emit_piece_oc(k)

            sch.add("Y", lambda e: e.nop(), waits=[pr_dmak[28], pr_dmak[29],
                                                    pr_dmak[30], pr_dmak[31]],
                    tag="final_wait")

        sch.resolve()
        sch.check()
        if check_only:
            counts = {k: len(sch.q[k]) for k in sch.QUEUES}
            print("schedule ok:", counts)
            return None
        with nc.Block() as block:
            @block.tensor
            def _(t):
                sch.emit_queue("T", t, semh)

            @block.vector
            def _(v):
                sch.emit_queue("V", v, semh)

            @block.scalar
            def _(s):
                sch.emit_queue("S", s, semh)

            @block.sync
            def _(y):
                sch.emit_queue("Y", y, semh)

            @block.gpsimd
            def _(g):
                sch.emit_queue("G", g, semh)
    return nc


def kernel(L0_attn_x, L0_attn_qkv, L0_attn_o_proj, L0_attn_gamma, L0_attn_beta,
           L0_ffn_router_w, L0_ffn_router_b, L0_ffn_expert_w, L0_ffn_expert_b):
    x = np.ascontiguousarray(np.asarray(L0_attn_x, dtype=np.float32).reshape(S_LEN, D))
    qkv = np.asarray(L0_attn_qkv, dtype=np.float32)
    wq = np.ascontiguousarray(qkv[:, 0:D])
    wk = np.ascontiguousarray(qkv[:, D:2 * D])
    wv = np.ascontiguousarray(qkv[:, 2 * D:3 * D])
    wo = np.ascontiguousarray(np.asarray(L0_attn_o_proj, dtype=np.float32))
    we = np.ascontiguousarray(np.asarray(L0_ffn_expert_w, dtype=np.float32))
    wr = np.ascontiguousarray(np.asarray(L0_ffn_router_w, dtype=np.float32))
    gamma = np.asarray(L0_attn_gamma, dtype=np.float32)
    beta = np.asarray(L0_attn_beta, dtype=np.float32)
    rb = np.asarray(L0_ffn_router_b, dtype=np.float32)
    be = np.asarray(L0_ffn_expert_b, dtype=np.float32)

    gbc = np.ascontiguousarray(np.broadcast_to(gamma[None, :], (128, D)))
    bbc = np.ascontiguousarray(np.broadcast_to(beta[None, :], (128, D)))
    bebc = np.ascontiguousarray(np.broadcast_to(be[None, :], (128, D)))
    rbbc = np.ascontiguousarray(np.broadcast_to(rb[None, :], (128, 16)))
    identb = np.eye(128, dtype=np.float32).astype(bfloat16)
    i128f = np.eye(128, dtype=np.float32)
    e2f = (np.arange(128)[:, None] % 16 == np.arange(128)[None, :] % 16).astype(np.float32)
    onesf = np.ones((1, 128), dtype=np.float32)

    row = np.arange(128)
    col = np.arange(KC)
    in_maps = []
    for c in range(NCORE):
        tiles = [T_of(c, j) for j in range(NT)]
        idx = np.concatenate([np.arange(t * 128, (t + 1) * 128) for t in tiles])
        x_own = np.ascontiguousarray(x[idx])
        masks = np.zeros((NT, 2, 128, KC), dtype=np.float32)
        for j in range(NT):
            t = tiles[j]
            q_global = t * 128 + row
            for ii in range(2):
                chunk = 2 * j + ii
                k_global = chunk * KC + col
                masks[j, ii] = np.where(k_global[None, :] <= q_global[:, None], 0.0, NEG)
        in_maps.append({
            "x": x_own, "wk32": wk, "wshin": np.ascontiguousarray(np.concatenate(
                [wq[256 * c:256 * (c + 1)],
                 wv[256 * c:256 * (c + 1)], wo[256 * c:256 * (c + 1)],
                 we[256 * c:256 * (c + 1)]], axis=0)), "wr": wr,
            "gbc": gbc, "bbc": bbc, "bebc": bebc, "rbbc": rbbc,
            "masks": masks.astype(bfloat16),
            "identb": identb, "i128f": i128f, "e2f": e2f, "onesf": onesf,
        })

    nc = build_program()
    res = run_bass_kernel_spmd(nc, in_maps, core_ids=list(range(NCORE)))

    full = np.zeros((S_LEN, D), dtype=np.float32)
    for c in range(NCORE):
        o = res.results[c]["out"]
        for j in range(NT):
            t = T_of(c, j)
            full[t * 128:(t + 1) * 128, :] = o[j * 128:(j + 1) * 128, :]
    return full.reshape(1, S_LEN, D)


if __name__ == "__main__":
    import sys
    if "--check" in sys.argv:
        build_program(check_only=True)
    else:
        nc = build_program()
        n_inst = sum(1 for f in nc.m.functions for b in f.blocks
                     for i in b.instructions)
        print("built ok, instructions:", n_inst)
